# version 14
# baseline (speedup 1.0000x reference)
import numpy as np
from contextlib import ExitStack

import concourse.bass as bass
import concourse.mybir as mybir
from concourse.bass_utils import run_bass_kernel_spmd

F32 = mybir.dt.float32
BF16 = mybir.dt.bfloat16
AL = mybir.AluOpType
AF = mybir.ActivationFunctionType
AX = mybir.AxisListType.X

MAGIC = 12582912.0

D = 1024
KD = 8
H = 16
I = 2816
KI = 22
NT = 512
TPB = 4
B = 4
T = B * TPB
SCL = 0.125
EPS = 1e-5

ENGS = ("sync", "gpsimd", "scalar", "vector", "tensor")
DMA_SEMS = {"ld", "gd", "cc", "sd", "vd"}


class Prog:

    def __init__(self):
        self.steps = {e: [] for e in ENGS}
        self.cnt = {}
        self.strict_v = False

    def tok(self, s):
        return (s, self.cnt.get(s, 0))

    def emit(self, eng, fn=None, waits=(), inc=None, n=1):
        ws = [w for w in waits if w is not None and w[1]]
        if eng == "vector" and self.strict_v and self.cnt.get("v", 0):
            ws.append(("v", self.cnt["v"]))
        post = None
        if inc is not None:
            self.cnt[inc] = self.cnt.get(inc, 0) + n
            post = self.cnt[inc]
        self.steps[eng].append([ws, fn, inc, n])
        return (inc, post) if inc is not None else None

    def barrier(self, owner, skip=("cc",)):
        snap = {s: v for s, v in self.cnt.items() if v > 0 and s not in skip}
        for e in ENGS:
            ws = [(s, v) for s, v in snap.items() if owner.get(s) != e]
            self.emit(e, None, waits=ws)

    def finalize(self):
        used = {}
        for e in ENGS:
            for ws, fn, inc, n in self.steps[e]:
                for s, v in ws:
                    used.setdefault(s, set()).add(v)
        remap = {}
        for s in list(self.cnt):
            if s in DMA_SEMS:
                continue
            kept = sorted(used.get(s, ()))
            remap[s] = {v: i + 1 for i, v in enumerate(kept)}
        for e in ENGS:
            seen = {}
            out = []
            for ws, fn, inc, n in self.steps[e]:
                nws = []
                for s, v in ws:
                    if s in remap:
                        v = remap[s].get(v)
                        if v is None:
                            continue
                    if seen.get(s, 0) >= v:
                        continue
                    seen[s] = v
                    nws.append((s, v))
                if inc is not None and inc in remap:
                    pass
                out.append([nws, fn, inc, n])
            self.steps[e] = out
        run = {}
        for e in ENGS:
            for st in self.steps[e]:
                ws, fn, inc, n = st
                if inc is not None and inc in remap:
                    run[inc] = run.get(inc, 0) + n
                    if run[inc] in remap[inc] and remap[inc][run[inc]]:
                        st[2] = inc
                    else:
                        st[2] = None


def build_nc():
    nc = bass.Bass("TRN2", target_bir_lowering=False,
                   detect_race_conditions=False)

    x_in = nc.dram_tensor("x", [T, 128, KD * NT], F32, kind="ExternalInput").ap()
    wq_in = nc.dram_tensor("wq", [128, KD, D], F32, kind="ExternalInput").ap()
    wk_in = nc.dram_tensor("wk", [128, KD, D], F32, kind="ExternalInput").ap()
    wt_in = nc.dram_tensor("wt", [128, KD, D], F32, kind="ExternalInput").ap()
    wqa_in = nc.dram_tensor("wqa", [128, KD, H], F32, kind="ExternalInput").ap()
    wka_in = nc.dram_tensor("wka", [128, KD, H], F32, kind="ExternalInput").ap()
    bq_in = nc.dram_tensor("bq", [128, KD], F32, kind="ExternalInput").ap()
    bk_in = nc.dram_tensor("bk", [128, KD], F32, kind="ExternalInput").ap()
    bt_in = nc.dram_tensor("bt", [128, KD], F32, kind="ExternalInput").ap()
    bqa_in = nc.dram_tensor("bqa", [H, 1], F32, kind="ExternalInput").ap()
    bka_in = nc.dram_tensor("bka", [H, 1], F32, kind="ExternalInput").ap()
    wg_in = nc.dram_tensor("wg", [KI, 128, D], F32, kind="ExternalInput").ap()
    wu_in = nc.dram_tensor("wu", [KI, 128, D], F32, kind="ExternalInput").ap()
    wd_in = nc.dram_tensor("wd", [KD, 128, I], F32, kind="ExternalInput").ap()
    c_in = nc.dram_tensor("cmod", [128, KD, B], F32, kind="ExternalInput").ap()
    wm_in = nc.dram_tensor("wmod", [128, KD, 6 * D], F32, kind="ExternalInput").ap()
    clipx_in = nc.dram_tensor("clipx", [1, 1], F32, kind="ExternalInput").ap()
    clipd_in = nc.dram_tensor("clipd", [1, 1], F32, kind="ExternalInput").ap()

    y_out = nc.dram_tensor("y", [T, 128, KD, NT], F32, kind="ExternalOutput").ap()

    xm_hbm = nc.dram_tensor("xm_s", [T, 128, KD, NT], F32).ap()
    mid_hbm = nc.dram_tensor("mid_s", [T, 128, KI, NT], F32).ap()
    rows2_hbm = nc.dram_tensor("rows2_s", [T, 3, NT], BF16).ap()
    cc1_i = nc.dram_tensor("cc1_i", [1], F32).ap()
    cc1_o = nc.dram_tensor("cc1_o", [1], F32).ap()
    cc2_i = nc.dram_tensor("cc2_i", [1], F32).ap()
    cc2_o = nc.dram_tensor("cc2_o", [1], F32).ap()

    p = Prog()
    owner = {"ld": "sync", "gd": "gpsimd", "g": "gpsimd", "cc": "gpsimd",
             "sd": "scalar", "vd": "vector",
             "t": "tensor", "v": "vector", "s": "scalar"}
    root = ExitStack()
    sem = {n: root.enter_context(nc.semaphore(n)) for n in owner}
    SB = lambda name, shape, dt: nc.sbuf_tensor(name, shape, dt)
    PS = lambda name, shape, dt: nc.psum_tensor(name, shape, dt)
    E = p.emit

    mods = root.enter_context(SB("mods", [128, 6 * KD * B], F32))
    gabt = root.enter_context(SB("gabt", [128, B, KD], F32))
    bqc = root.enter_context(SB("bqc", [128, KD], F32))
    bkc = root.enter_context(SB("bkc", [128, KD], F32))
    btc = root.enter_context(SB("btc", [128, KD], F32))
    bqa8 = root.enter_context(SB("bqa8", [H, 1], F32))
    bka8 = root.enter_context(SB("bka8", [H, 1], F32))
    id_bf = root.enter_context(SB("id_bf", [128, 128], BF16))
    id_f = root.enter_context(SB("id_f", [128, 128], F32))
    ones_col = root.enter_context(SB("ones_col", [128, 1], BF16))
    ones_row = root.enter_context(SB("ones_row", [1, 128], BF16))
    emaskT = root.enter_context(SB("emaskT", [H, KD * 128], BF16))
    absacc2 = root.enter_context(SB("absacc2", [128, 1], F32))
    clipx_sb = root.enter_context(SB("clipx_sb", [1, 1], F32))
    clipd_sb = root.enter_context(SB("clipd_sb", [1, 1], F32))
    s1col = root.enter_context(SB("s1col", [128, 1], F32))
    is1col = root.enter_context(SB("is1col", [128, 1], F32))
    s2col = root.enter_context(SB("s2col", [128, 1], F32))
    is2col = root.enter_context(SB("is2col", [128, 1], F32))
    sgc = root.enter_context(SB("sgc", [128, KI], F32))
    suc = root.enter_context(SB("suc", [128, KI], F32))
    sdc = root.enter_context(SB("sdc", [128, KD], F32))
    sg1 = root.enter_context(SB("sg1", [128, KI], F32))
    su1 = root.enter_context(SB("su1", [128, KI], F32))
    sd2 = root.enter_context(SB("sd2", [128, KD], F32))
    c2s1 = root.enter_context(SB("c2s1", [128, KD * B], F32))
    d2s1 = root.enter_context(SB("d2s1", [128, KD * B], F32))
    cst_eps = root.enter_context(SB("cst_eps", [128, 1], F32))
    cst_nh = root.enter_context(SB("cst_nh", [128, 1], F32))
    cst_mg = root.enter_context(SB("cst_mg", [128, 1], F32))
    cst_nmg = root.enter_context(SB("cst_nmg", [128, 1], F32))
    cst_scl = root.enter_context(SB("cst_scl", [128, 1], F32))

    MT_SA, MT_CA, MT_GA, MT_SB, MT_CB, MT_GB = range(6)

    def modc(t_, c, b):
        j = t_ * KD * B + c * B + b
        return mods[:, j:j + 1]

    def rng_(t_):
        return mods[:, t_ * KD * B:(t_ + 1) * KD * B]

    watt = ExitStack()
    wq_sb = watt.enter_context(SB("wq_sb", [128, KD, D], BF16))
    wk_sb = watt.enter_context(SB("wk_sb", [128, KD, D], BF16))
    wt_sb = watt.enter_context(SB("wt_sb", [128, KD, D], BF16))
    wqa_sb = watt.enter_context(SB("wqa_sb", [128, KD, H], BF16))
    wka_sb = watt.enter_context(SB("wka_sb", [128, KD, H], BF16))

    su_arena = ExitStack()
    c_f = su_arena.enter_context(SB("c_f", [128, KD * B], F32))
    c_bf = su_arena.enter_context(SB("c_bf", [128, KD, B], BF16))
    wm_sb = [su_arena.enter_context(SB(f"wm{i}", [128, 3 * D], F32)) for i in range(3)]
    wmb_sb = [su_arena.enter_context(SB(f"wmb{i}", [128, 3 * D], BF16)) for i in range(3)]
    wstg = [su_arena.enter_context(SB(f"wstg{i}", [128, D], F32)) for i in range(4)]
    modT = su_arena.enter_context(SB("modT", [B, 3 * D], F32))
    scr_a = su_arena.enter_context(SB("scr_a", [128, KD * B], F32))
    scr_b = su_arena.enter_context(SB("scr_b", [128, KD * B], F32))
    scr_c = su_arena.enter_context(SB("scr_c", [128, KD * B], F32))
    iota_i = su_arena.enter_context(SB("iota_i", [H, 128], mybir.dt.int32))
    jramp = su_arena.enter_context(SB("jramp", [H, 128], F32))
    hidx = su_arena.enter_context(SB("hidx", [H, 1], F32))
    tgt_sc = su_arena.enter_context(SB("tgt_sc", [H, 128], F32))
    mod_ps = su_arena.enter_context(PS("mod_ps", [B, 3 * D], F32))
    tp_ps = su_arena.enter_context(PS("tp_ps", [128, 96], F32))

    for dst, src in ((bqc, bq_in), (bkc, bk_in), (btc, bt_in)):
        E("sync", lambda e, d=dst, s=src: e.dma_start(out=d[:, :], in_=s[:, :]),
          inc="ld", n=16)
    E("sync", lambda e: e.dma_start(out=bqa8[:, :], in_=bqa_in[:, :]), inc="ld", n=16)
    E("sync", lambda e: e.dma_start(out=bka8[:, :], in_=bka_in[:, :]), inc="ld", n=16)
    E("sync", lambda e: e.dma_start(out=clipx_sb[:, :], in_=clipx_in[:, :]), inc="ld", n=16)
    E("sync", lambda e: e.dma_start(out=clipd_sb[:, :], in_=clipd_in[:, :]), inc="ld", n=16)
    E("sync", lambda e: e.dma_start(out=c_f[:, :], in_=c_in[:, :, :]), inc="ld", n=16)
    t_small = p.tok("ld")

    t_w_ld = {}
    for (key, dst, src) in (("wqa", wqa_sb, wqa_in), ("wka", wka_sb, wka_in)):
        t_w_ld[key] = E("gpsimd", lambda e, d=dst, s=src: e.dma_start(
            out=d[:, :, :], in_=s[:, :, :]), inc="gd", n=16)
    g_wstg = [None] * 4
    wi = 0
    for (key, dst, src, ceng, csem) in (
            ("wq", wq_sb, wq_in, "vector", "v"),
            ("wk", wk_sb, wk_in, "gpsimd", "g"),
            ("wt", wt_sb, wt_in, "vector", "v")):
        for c in range(KD):
            sbf = wi % 4
            eng0, sem0 = [("sync", "ld"), ("scalar", "sd")][wi % 2]
            tldw = E(eng0, lambda e, sbf=sbf, src=src, c=c: e.dma_start(
                out=wstg[sbf][:, :], in_=src[:, c, :]),
                waits=[g_wstg[sbf]], inc=sem0, n=16)
            g_wstg[sbf] = E(ceng, lambda e, sbf=sbf, dst=dst, c=c: e.tensor_copy(
                out=dst[:, c, :], in_=wstg[sbf][:, :]),
                waits=[tldw], inc=csem)
            wi += 1
        t_w_ld[key] = g_wstg[(wi - 1) % 4]

    nc.const_aps.register(F32, EPS, cst_eps[:, 0:1])
    nc.const_aps.register(F32, -0.5, cst_nh[:, 0:1])
    nc.const_aps.register(F32, MAGIC, cst_mg[:, 0:1])
    nc.const_aps.register(F32, -MAGIC, cst_nmg[:, 0:1])
    nc.const_aps.register(F32, SCL, cst_scl[:, 0:1])
    p.strict_v = True
    E("vector", lambda e: e.memset(cst_eps[:, :], EPS), inc="v")
    E("vector", lambda e: e.memset(cst_nh[:, :], -0.5), inc="v")
    E("vector", lambda e: e.memset(cst_mg[:, :], MAGIC), inc="v")
    E("vector", lambda e: e.memset(cst_nmg[:, :], -MAGIC), inc="v")
    E("vector", lambda e: e.memset(cst_scl[:, :], SCL), inc="v")
    E("vector", lambda e: e.memset(ones_col[:, :], 1.0 / D), inc="v")
    E("vector", lambda e: e.memset(ones_row[:, :], 1.0), inc="v")
    E("vector", lambda e: e.memset(absacc2[:, :], 0.0), inc="v")
    E("gpsimd", lambda e: e.memset(id_bf[:, :], 0.0))
    E("gpsimd", lambda e: e.affine_select(
        out=id_bf[:, :], in_=id_bf[:, :], compare_op=AL.not_equal, fill=1.0,
        base=0, pattern=[[-1, 128]], channel_multiplier=1), inc="g")
    E("gpsimd", lambda e: e.memset(id_f[:, :], 0.0))
    E("gpsimd", lambda e: e.affine_select(
        out=id_f[:, :], in_=id_f[:, :], compare_op=AL.not_equal, fill=1.0,
        base=0, pattern=[[-1, 128]], channel_multiplier=1), inc="g")
    E("gpsimd", lambda e: e.iota(iota_i[:, :], pattern=[[1, 128]], base=0,
                                 channel_multiplier=0), inc="g")
    t_io1 = p.tok("g")
    E("vector", lambda e: e.tensor_copy(out=jramp[:, :], in_=iota_i[:, :]),
      waits=[t_io1], inc="v")
    E("vector", lambda e: e.tensor_scalar(
        out=jramp[:, :], in0=jramp[:, :], scalar1=64.0, scalar2=None,
        op0=AL.is_ge, op1=AL.bypass), inc="v")
    t_jr = p.tok("v")
    E("gpsimd", lambda e: e.iota(iota_i[:, 0:1], pattern=[[0, 1]], base=0,
                                 channel_multiplier=1), waits=[t_jr], inc="g")
    t_io2 = p.tok("g")
    E("vector", lambda e: e.tensor_copy(out=hidx[:, :], in_=iota_i[:, 0:1]),
      waits=[t_io2], inc="v")
    for c in range(KD):
        E("vector", lambda e, c=c: e.tensor_scalar(
            out=tgt_sc[:, :], in0=jramp[:, :], scalar1=float(2 * c),
            scalar2=None, op0=AL.add, op1=AL.bypass), inc="v")
        E("vector", lambda e, c=c: e.tensor_scalar(
            out=emaskT[:, c * 128:(c + 1) * 128], in0=tgt_sc[:, :],
            scalar1=hidx[:, :], scalar2=None,
            op0=AL.is_equal, op1=AL.bypass), inc="v")
    t_masks = p.tok("v")

    tsil = E("scalar", lambda e: e.activation(
        out=c_f[:, :], in_=c_f[:, :], func=AF.Silu),
        waits=[t_small], inc="s")
    E("vector", lambda e: e.tensor_copy(out=c_bf[:, :, :], in_=c_f[:, :]),
      waits=[tsil], inc="v")
    t_cbf = p.tok("v")

    p.strict_v = False
    g_wm = [None, None, None]
    g_wmb = [None, None, None]
    g_modps_v = None
    WMQ = [("sync", "ld"), ("scalar", "sd"), ("gpsimd", "gd")]
    for half in range(2):
        t_mm_k = [None] * KD
        for kc in range(KD):
            kk = half * KD + kc
            engm, semm = WMQ[kk % 3]
            tld = E(engm, lambda e, kc=kc, half=half, kk=kk: e.dma_start(
                out=wm_sb[kk % 3][:, :],
                in_=wm_in[:, kc, half * 3 * D:(half + 1) * 3 * D]),
                waits=[g_wm[kk % 3]], inc=semm, n=16)
            ceng, csem = ("vector", "v") if kk % 2 == 0 else ("gpsimd", "g")
            tcast = E(ceng, lambda e, kk=kk: e.tensor_copy(
                out=wmb_sb[kk % 3][:, :], in_=wm_sb[kk % 3][:, :]),
                waits=[tld, g_wmb[kk % 3]], inc=csem)
            g_wm[kk % 3] = tcast

            def mm(e, kc=kc, kk=kk):
                last = None
                for j in range(6):
                    last = e.matmul(mod_ps[:, j * NT:(j + 1) * NT],
                                    c_bf[:, kc, :],
                                    wmb_sb[kk % 3][:, j * NT:(j + 1) * NT],
                                    start=(kc == 0), stop=(kc == KD - 1))
                return last
            t_mm_k[kc] = E("tensor", mm,
                           waits=[tcast, t_cbf, g_wmb[kk % 3], g_modps_v],
                           inc="t")
            g_wmb[kk % 3] = t_mm_k[kc]
        tv = E("vector", lambda e: e.tensor_copy(out=modT[:, :], in_=mod_ps[:, :]),
               waits=[t_mm_k[-1]], inc="v")
        g_modps_v = tv

        def tps(e):
            last = None
            for i in range(24):
                last = e.transpose(tp_ps[:, i * B:(i + 1) * B],
                                   modT[:, i * 128:(i + 1) * 128],
                                   id_f[0:B, 0:B])
            return last
        tt_ = E("tensor", tps, waits=[tv, p.tok("g")], inc="t")
        E("vector", lambda e, half=half: e.tensor_copy(
            out=mods[:, half * 96:(half + 1) * 96], in_=tp_ps[:, :]),
            waits=[tt_], inc="v")
        g_wm = [p.tok("t")] * 3
    t_mods_raw = p.tok("v")
    p.strict_v = True

    for tC in (MT_CA, MT_CB):
        E("vector", lambda e, tC=tC: e.tensor_scalar(
            out=rng_(tC), in0=rng_(tC), scalar1=1.0, scalar2=None,
            op0=AL.add, op1=AL.bypass), waits=[t_mods_raw], inc="v")
    for tG in (MT_GA, MT_GB):
        E("vector", lambda e, tG=tG: e.tensor_scalar(
            out=scr_a[:, :], in0=rng_(tG), scalar1=0.0, scalar2=None,
            op0=AL.is_ge, op1=AL.bypass), waits=[t_mods_raw], inc="v")
        E("vector", lambda e, tG=tG: e.tensor_scalar(
            out=scr_b[:, :], in0=rng_(tG), scalar1=0.25, scalar2=None,
            op0=AL.max, op1=AL.bypass), inc="v")
        E("vector", lambda e, tG=tG: e.tensor_scalar(
            out=scr_c[:, :], in0=rng_(tG), scalar1=-0.25, scalar2=None,
            op0=AL.min, op1=AL.bypass), inc="v")
        E("vector", lambda e: e.tensor_sub(scr_b[:, :], scr_b[:, :], scr_c[:, :]),
          inc="v")
        E("vector", lambda e: e.tensor_mul(scr_a[:, :], scr_a[:, :], scr_b[:, :]),
          inc="v")
        E("vector", lambda e, tG=tG: e.tensor_add(rng_(tG), scr_a[:, :], scr_c[:, :]),
          inc="v")
    for b in range(B):
        for mc in range(KD):
            E("vector", lambda e, b=b, mc=mc: e.tensor_mul(
                gabt[:, b, mc:mc + 1], modc(MT_GA, mc, b), btc[:, mc:mc + 1]),
                inc="v")
    E("vector", lambda e: e.tensor_scalar(
        out=bqa8[:, :], in0=bqa8[:, :], scalar1=SCL, scalar2=None,
        op0=AL.mult, op1=AL.bypass), waits=[t_small], inc="v")
    E("vector", lambda e: e.tensor_scalar(
        out=bka8[:, :], in0=bka8[:, :], scalar1=SCL, scalar2=None,
        op0=AL.mult, op1=AL.bypass), inc="v")

    p.strict_v = False
    su_arena.close()
    p.barrier(owner)

    ph1 = ExitStack()
    x_sb = [ph1.enter_context(SB(f"x{i}", [128, KD, NT], F32)) for i in range(2)]
    h_sb = [ph1.enter_context(SB(f"h_sb{i}", [128, KD, NT], BF16)) for i in range(2)]
    mq_sb = ph1.enter_context(SB("mq_sb", [128, KD, TPB * NT], BF16))
    mk_sb = ph1.enter_context(SB("mk_sb", [128, KD, TPB * NT], BF16))
    sc2 = [ph1.enter_context(SB(f"sc2_{i}", [H, NT], BF16)) for i in range(2)]
    pparts = ph1.enter_context(SB("pparts", [128, KD * TPB], F32))
    pqc = ph1.enter_context(SB("pqc", [128, KD], F32))
    pkc = ph1.enter_context(SB("pkc", [128, KD], F32))
    xb_sc = [ph1.enter_context(SB(f"xb{i}", [128, NT], BF16)) for i in range(4)]
    x2_sc = [ph1.enter_context(SB(f"x2{i}", [128, NT], BF16)) for i in range(4)]
    xcp = [ph1.enter_context(SB(f"xcp{i}", [128, NT], F32)) for i in range(2)]
    te_sc = [ph1.enter_context(SB(f"te{i}", [128, NT], F32)) for i in range(2)]
    redg_sc = [ph1.enter_context(SB(f"redg{i}", [128, 1], F32)) for i in range(2)]
    absg = ph1.enter_context(SB("absg", [128, 1], F32))
    xm_sb = ph1.enter_context(SB("xm_sb", [128, KD, NT], F32))
    rowf = [ph1.enter_context(SB(f"rowf{i}", [1, NT], F32)) for i in range(2)]
    rows3 = ph1.enter_context(SB("rows3", [1, 3 * NT], BF16))
    qsum = ph1.enter_context(SB("qsum", [H, 1], F32))
    qmax = ph1.enter_context(SB("qmax", [H, 1], F32))
    qrec = ph1.enter_context(SB("qrec", [H, 1], F32))
    qsums = ph1.enter_context(SB("qsums", [H, TPB], F32))
    negc = ph1.enter_context(SB("negc", [H, 1], F32))
    biasc = ph1.enter_context(SB("biasc", [H, 1], F32))
    qrh = ph1.enter_context(SB("qrh", [H, 1], BF16))
    qrl = ph1.enter_context(SB("qrl", [H, 1], BF16))

    ps1 = ExitStack()
    ps_a = ps1.enter_context(PS("ps_a", [128, NT], F32))
    ps_b = ps1.enter_context(PS("ps_b", [128, NT], F32))
    ps_c = ps1.enter_context(PS("ps_c", [128, NT], F32))
    ps_d = ps1.enter_context(PS("ps_d", [128, NT], F32))
    ps_e1 = ps1.enter_context(PS("ps_e1", [128, NT], F32))
    ps_e2 = ps1.enter_context(PS("ps_e2", [128, NT], F32))
    ps_f = ps1.enter_context(PS("ps_f", [128, NT], F32))
    ps_g2 = ps1.enter_context(PS("ps_g2", [128, NT], F32))

    E("vector", lambda e: e.memset(absg[:, :], 0.0), inc="v")

    g_h_t = [None, None]
    g_ab = [None, None, None, None]
    g_cd_t = None
    g_mq_t = None
    g_mk_t = None
    g_psf_v = None
    g_psg2_v = None
    g_psg2_s = None
    g_xm_g = None
    g_xm_cons = None
    g_te = [None, None]
    g_xb = [None, None, None, None]
    g_x2 = [None, None, None, None]
    g_scq = [None, None]
    g_xcg = [None, None]
    g_redg = [None, None]
    g_rows3 = [None, None]
    x_cons = [[], []]
    pool_st = {}
    mk_ep = [None] * TPB
    t_wka_fresh = t_w_ld["wka"]
    t_wt_fresh = [t_w_ld["wt"], t_w_ld["wt"]]

    def rows_math(wtok):
        E("vector", lambda e: e.tensor_copy(out=rowf[0][:, :], in_=ps_c[0:1, :]),
          waits=[wtok], inc="v")
        E("vector", lambda e: e.tensor_copy(out=rows3[:, 2 * NT:3 * NT],
                                            in_=rowf[0][:, :]),
          waits=[g_rows3[0], g_rows3[1]], inc="v")
        E("vector", lambda e: e.tensor_mul(rowf[0][:, :], rowf[0][:, :],
                                           rowf[0][:, :]), inc="v")
        tv = E("vector", lambda e: e.tensor_sub(rowf[1][:, :], ps_d[0:1, :],
                                                rowf[0][:, :]), inc="v")
        ts_ = E("scalar", lambda e: e.activation(
            out=rowf[1][:, :], in_=rowf[1][:, :], func=AF.Ln, bias=EPS),
            waits=[tv], inc="s")
        ts_ = E("scalar", lambda e: e.activation(
            out=rowf[1][:, :], in_=rowf[1][:, :], func=AF.Exp, scale=-0.5),
            waits=[ts_], inc="s")
        E("vector", lambda e: e.tensor_copy(out=rows3[:, 0:NT], in_=rowf[1][:, :]),
          waits=[ts_], inc="v")
        E("vector", lambda e: e.tensor_sub(rows3[:, NT:2 * NT], rowf[1][:, :],
                                           rows3[:, 0:NT]), inc="v")
        return p.tok("v")

    def bcasts(t_rows):
        def mmb(e):
            e.matmul(ps_d[:, :], ones_row[:, :], rows3[:, 2 * NT:3 * NT],
                     start=True, stop=True)
            e.matmul(ps_c[:, :], ones_row[:, :], rows3[:, 0:NT], start=True,
                     stop=False)
            return e.matmul(ps_c[:, :], ones_row[:, :], rows3[:, NT:2 * NT],
                            start=False, stop=True)
        tb = E("tensor", mmb, waits=[t_rows], inc="t")
        g_rows3[0] = tb
        return tb

    def mk_chain(t, kind):
        b = t // TPB
        xb = t % 2
        ST = {}

        def s_load():
            ST["tld"] = E("sync", lambda e, t=t, xb=xb: e.dma_start(
                out=x_sb[xb][:, :, :],
                in_=x_in[t].rearrange("p (c n) -> p c n", c=KD)),
                waits=list(x_cons[xb]), inc="ld", n=16)
            x_cons[xb] = []

        def s_stats():
            nonlocal g_cd_t
            src = x_sb[xb] if kind == "s1" else xm_sb
            wt0 = ST["tld"] if kind == "s1" else ST["t_src"]
            tm = None
            for c in range(KD):
                cb4 = c % 4
                tc_ = E("gpsimd", lambda e, c=c, cb4=cb4, src=src: e.tensor_copy(
                    out=xb_sc[cb4][:, :], in_=src[:, c, :]),
                    waits=[wt0, g_xb[cb4]], inc="g")
                tq_ = E("scalar", lambda e, c=c, cb4=cb4, src=src: e.activation(
                    out=x2_sc[cb4][:, :], in_=src[:, c, :], func=AF.Square),
                    waits=[wt0, g_x2[cb4]], inc="s")

                def mmst(e, c=c, cb4=cb4):
                    e.matmul(ps_c[0:1, :], ones_col[:, :], xb_sc[cb4][:, :],
                             start=(c == 0), stop=(c == KD - 1))
                    return e.matmul(ps_d[0:1, :], ones_col[:, :],
                                    x2_sc[cb4][:, :],
                                    start=(c == 0), stop=(c == KD - 1))
                tm = E("tensor", mmst, waits=[tc_, tq_, g_cd_t], inc="t")
                g_cd_t = None
                g_xb[cb4] = tm
                g_x2[cb4] = tm
            ST["t_cp"] = p.tok("g")
            ST["t_sq"] = p.tok("s")
            ST["tm"] = tm

        def s_rows():
            ST["t_rows"] = rows_math(ST["tm"])

        def s_bcast():
            ST["tb"] = bcasts(ST["t_rows"])
            if kind == "s3":
                ST["t_r2st"] = E("gpsimd", lambda e, t=t: e.dma_start(
                    out=rows2_hbm[t].rearrange("a n -> (a n)").rearrange(
                        "(a n) -> a n", a=1), in_=rows3[:, :]),
                    waits=[ST["t_rows"]], inc="gd", n=16)
                g_rows3[1] = ST["t_r2st"]

        def mk_ln(c):
            def s_ln():
                nonlocal g_cd_t
                cp = c % 2
                tC = MT_CA if kind == "s1" else MT_CB
                tS = MT_SA if kind == "s1" else MT_SB
                src = x_sb[xb] if kind == "s1" else xm_sb
                E("vector", lambda e, c=c, cp=cp, src=src: e.tensor_sub(
                    xcp[cp][:, :], src[:, c, :], ps_d[:, :]),
                    waits=[ST["tb"], g_xcg[cp]], inc="v")
                E("vector", lambda e, c=c, cp=cp, b=b, tC=tC:
                    e.scalar_tensor_tensor(
                        out=xcp[cp][:, :], in0=xcp[cp][:, :],
                        scalar=modc(tC, c, b), in1=ps_c[:, :],
                        op0=AL.mult, op1=AL.mult), inc="v")
                if kind == "s1":
                    E("vector", lambda e, c=c, cp=cp, b=b, xb=xb, tS=tS:
                        e.tensor_scalar(
                            out=h_sb[xb][:, c, :], in0=xcp[cp][:, :],
                            scalar1=modc(tS, c, b), scalar2=None,
                            op0=AL.add, op1=AL.bypass),
                        waits=[g_h_t[xb]], inc="v")
                    g_h_t[xb] = None
                else:
                    E("vector", lambda e, c=c, cp=cp, b=b, tS=tS:
                        e.tensor_scalar(
                            out=xcp[cp][:, :], in0=xcp[cp][:, :],
                            scalar1=modc(tS, c, b), scalar2=None,
                            op0=AL.add, op1=AL.bypass), inc="v")
                    E("vector", lambda e, cp=cp: e.tensor_reduce(
                        out=redg_sc[cp][:, :], in_=xcp[cp][:, :], axis=AX,
                        op=AL.max, apply_absolute_value=True), inc="v")
                    E("vector", lambda e, cp=cp: e.tensor_max(
                        absg[:, :], absg[:, :], redg_sc[cp][:, :]), inc="v")
                    g_xcg[cp] = p.tok("v")
                if c == KD - 1:
                    ST["t_ln"] = p.tok("v")
                    if kind == "s1":
                        x_cons[xb] = [p.tok("v"), ST["t_cp"], ST["t_sq"]]
                        ST["t_h"] = p.tok("v")
                    g_cd_t = p.tok("v")
            return s_ln

        steps = []
        if kind == "s1":
            steps.append(s_load)
        steps.append(s_stats)
        steps.append(s_rows)
        steps.append(s_bcast)
        for c in range(KD):
            steps.append(mk_ln(c))
        return ST, steps

    def pool_scores(q, src_mm, sums, bias8, tq_mm):
        nonlocal g_psg2_s, g_psf_v
        qp = q % 2
        qsl = slice(q * NT, (q + 1) * NT)
        if q == 0:
            E("vector", lambda e, bias8=bias8: e.tensor_scalar(
                out=sc2[0][:, :], in0=ps_g2[0:H, :], scalar1=SCL,
                scalar2=bias8[:, :], op0=AL.mult, op1=AL.add),
                waits=[tq_mm, g_scq[0]], inc="v")
            E("vector", lambda e: e.tensor_reduce(
                out=qmax[:, :], in_=sc2[0][:, :], axis=AX, op=AL.max),
                waits=[p.tok("v")], inc="v")
            E("vector", lambda e: e.tensor_scalar(
                out=negc[:, :], in0=qmax[:, :], scalar1=-1.0, scalar2=None,
                op0=AL.mult, op1=AL.bypass), waits=[p.tok("v")], inc="v")
            tbc = E("vector", lambda e, bias8=bias8: e.tensor_add(
                biasc[:, :], bias8[:, :], negc[:, :]),
                waits=[p.tok("v")], inc="v")
            pool_st["biasc"] = tbc
            tse = E("scalar", lambda e, sums=sums: e.activation(
                out=sc2[0][:, :], in_=sc2[0][:, :], func=AF.Exp,
                bias=negc[:, :], accum_out=sums[:, 0:1]), waits=[tbc], inc="s")
        else:
            tse = E("scalar", lambda e, q=q, qp=qp, sums=sums: e.activation(
                out=sc2[qp][:, :], in_=ps_g2[0:H, :], func=AF.Exp,
                scale=SCL, bias=biasc[:, :], accum_out=sums[:, q:q + 1]),
                waits=[tq_mm, pool_st["biasc"], g_scq[qp]], inc="s")
        g_psg2_s = tse
        tmx = None
        for c in range(KD):
            def mmx(e, c=c, qp=qp):
                return e.matmul(ps_f[:, :], emaskT[:, c * 128:(c + 1) * 128],
                                sc2[qp][:, :], start=True, stop=True)
            tmx = E("tensor", mmx, waits=[tse, g_psf_v], inc="t")
            g_psf_v = E("vector", lambda e, c=c, q=q, qsl=qsl, src_mm=src_mm:
                        e.scalar_tensor_tensor(
                            out=xcp[0][:, :], in0=src_mm[:, c, qsl], scalar=1.0,
                            in1=ps_f[:, :], op0=AL.mult, op1=AL.mult,
                            accum_out=pparts[:, c * TPB + q:c * TPB + q + 1]),
                        waits=[tmx, g_xcg[0]], inc="v")
        g_scq[qp] = tmx

    def pool_final(dst, sums):
        nonlocal g_psg2_v
        p.strict_v = True
        E("vector", lambda e, sums=sums: e.tensor_reduce(
            out=qsum[:, :], in_=sums[:, :], axis=AX, op=AL.add),
            waits=[p.tok("s"), p.tok("v")], inc="v")
        E("vector", lambda e: e.reciprocal(qrec[:, :], qsum[:, :]), inc="v")
        E("vector", lambda e: e.tensor_copy(out=qrh[:, :], in_=qrec[:, :]),
          inc="v")
        tvl = E("vector", lambda e: e.tensor_sub(qrl[:, :], qrec[:, :],
                                                 qrh[:, :]), inc="v")
        for c in range(KD):
            def mmr(e, c=c):
                e.matmul(ps_g2[:, 0:1], emaskT[:, c * 128:(c + 1) * 128],
                         qrh[:, :], start=True, stop=False)
                return e.matmul(ps_g2[:, 0:1], emaskT[:, c * 128:(c + 1) * 128],
                                qrl[:, :], start=False, stop=True)
            tmr = E("tensor", mmr, waits=[tvl, p.tok("s"), p.tok("v")], inc="t")
            E("vector", lambda e, c=c, dst=dst: e.tensor_reduce(
                out=dst[:, c:c + 1], in_=pparts[:, c * TPB:(c + 1) * TPB],
                axis=AX, op=AL.add), waits=[p.tok("v")], inc="v")
            E("vector", lambda e, c=c, dst=dst: e.tensor_mul(
                dst[:, c:c + 1], dst[:, c:c + 1], ps_g2[:, 0:1]),
                waits=[tmr, p.tok("v")], inc="v")
        E("vector", None, waits=[p.tok("v")])
        p.strict_v = False
        g_psg2_v = p.tok("v")
        return p.tok("v")

    def run_steps(steps, n):
        for _ in range(n):
            if steps:
                steps.pop(0)()

    ST0, steps0 = mk_chain(0, "s1")
    for s in steps0:
        s()
    cur = ST0

    for b in range(B):
        for q in range(TPB):
            t = b * TPB + q
            xb = t % 2
            nsl = slice(q * NT, (q + 1) * NT)
            t_h = cur["t_h"]
            if t + 1 < T and q < TPB - 1:
                nxt, nsteps = mk_chain(t + 1, "s1")
                run_steps(nsteps, 1)
            else:
                nxt, nsteps = None, []
            gi = 0
            for (wsb, bcol, dst, wkey) in ((wq_sb, bqc, mq_sb, "wq"),
                                           (wk_sb, bkc, mk_sb, "wk")):
                for mc in range(KD):
                    bank = [ps_a, ps_b, ps_e1, ps_e2][gi % 4]

                    def mmp(e, wsb=wsb, mc=mc, bank=bank, xb=xb):
                        last = None
                        for kc in range(KD):
                            last = e.matmul(
                                bank[:, :], wsb[:, kc, mc * 128:(mc + 1) * 128],
                                h_sb[xb][:, kc, :], start=(kc == 0),
                                stop=(kc == KD - 1))
                        return last
                    tm = E("tensor", mmp,
                           waits=[t_h, t_w_ld[wkey], g_ab[gi % 4]], inc="t")
                    wmqk = g_mq_t if dst is mq_sb else g_mk_t
                    if gi % 2 == 0:
                        tv = E("scalar",
                               lambda e, dst=dst, bcol=bcol, mc=mc, bank=bank,
                               nsl=nsl: e.activation(
                                   out=dst[:, mc, nsl], in_=bank[:, :],
                                   func=AF.Identity, bias=bcol[:, mc:mc + 1]),
                               waits=[tm, wmqk], inc="s")
                    else:
                        tv = E("vector",
                               lambda e, dst=dst, bcol=bcol, mc=mc, bank=bank,
                               nsl=nsl: e.tensor_scalar(
                                   out=dst[:, mc, nsl], in0=bank[:, :],
                                   scalar1=bcol[:, mc:mc + 1], scalar2=None,
                                   op0=AL.add, op1=AL.bypass),
                               waits=[tm, wmqk], inc="v")
                    g_ab[gi % 4] = tv
                    gi += 1
                    if gi >= 3:
                        run_steps(nsteps, 1)
            run_steps(nsteps, 99)
            t_mqmk = (p.tok("v"), p.tok("s"))
            mk_ep[q] = t_mqmk

            def mmq(e, nsl=nsl):
                last = None
                for kc in range(KD):
                    last = e.matmul(ps_g2[0:H, :], wqa_sb[:, kc, :],
                                    mq_sb[:, kc, nsl], start=(kc == 0),
                                    stop=(kc == KD - 1))
                return last
            tq = E("tensor", mmq, waits=[t_mqmk[0], t_mqmk[1], t_w_ld["wqa"],
                                         g_psg2_s, g_psg2_v], inc="t")
            pool_scores(q, mq_sb, qsums, bqa8, tq)
            g_h_t[xb] = p.tok("t")
            if nxt is not None:
                cur = nxt
        g_mq_t = None
        g_mk_t = None

        t_pq = pool_final(pqc, qsums)
        g_mq_t = p.tok("t")
        tka = None
        for c in range(KD):
            tka = E("vector", lambda e, c=c: e.tensor_scalar(
                out=wka_sb[:, c, :], in0=wka_sb[:, c, :],
                scalar1=pqc[:, c:c + 1], scalar2=None,
                op0=AL.mult, op1=AL.bypass),
                waits=[t_pq, t_wka_fresh] if c == 0 else [], inc="v")
        t_kafold = tka

        if b + 1 < B:
            cur, csteps = mk_chain((b + 1) * TPB, "s1")
            run_steps(csteps, 1)
        else:
            csteps = []
        tk_last = None
        for q in range(TPB):
            qsl = slice(q * NT, (q + 1) * NT)

            def mmk(e, qsl=qsl):
                last = None
                for kc in range(KD):
                    last = e.matmul(ps_g2[0:H, :], wka_sb[:, kc, :],
                                    mk_sb[:, kc, qsl], start=(kc == 0),
                                    stop=(kc == KD - 1))
                return last
            tk = E("tensor", mmk, waits=[t_kafold, mk_ep[q][0], mk_ep[q][1],
                                         g_psg2_s, g_psg2_v], inc="t")
            tk_last = tk
            pool_scores(q, mk_sb, qsums, bka8, tk)
            run_steps(csteps, 3)
        run_steps(csteps, 99)

        t_pk = pool_final(pkc, qsums)
        g_mk_t = p.tok("t")
        tpk2 = E("vector", lambda e: e.tensor_mul(
            pkc[:, :], pkc[:, :], pqc[:, :]), waits=[t_pk], inc="v")
        tfv = tfg = None
        for c in range(KD):
            if c < 4:
                tfv = E("vector", lambda e, c=c: e.tensor_scalar(
                    out=wt_sb[:, c, :], in0=wt_sb[:, c, :],
                    scalar1=pkc[:, c:c + 1], scalar2=None,
                    op0=AL.mult, op1=AL.bypass),
                    waits=[t_wt_fresh[0]] if c == 0 else [], inc="v")
            else:
                tfg = E("gpsimd", lambda e, c=c: e.tensor_scalar(
                    out=wt_sb[:, c, :], in0=wt_sb[:, c, :],
                    scalar1=pkc[:, c:c + 1], scalar2=None,
                    op0=AL.mult, op1=AL.bypass),
                    waits=[tpk2, t_wt_fresh[1]] if c == 4 else [], inc="g")
        t_tfold = (tfv, tfg)
        if b + 1 < B:
            t_wka_fresh = E("gpsimd", lambda e: e.dma_start(
                out=wka_sb[:, :, :], in_=wka_in[:, :, :]),
                waits=[tk_last], inc="gd", n=16)

        s3_pend = []
        for q in range(TPB):
            t = b * TPB + q
            xb = t % 2
            nsl = slice(q * NT, (q + 1) * NT)
            tld = E("sync", lambda e, t=t, xb=xb: e.dma_start(
                out=x_sb[xb][:, :, :],
                in_=x_in[t].rearrange("p (c n) -> p c n", c=KD)),
                waits=list(x_cons[xb]), inc="ld", n=16)
            x_cons[xb] = []
            for mc in range(KD):
                bank = [ps_a, ps_b, ps_e1, ps_e2][mc % 4]

                def mmo(e, mc=mc, bank=bank, nsl=nsl):
                    for kc in range(KD):
                        e.matmul(
                            bank[:, :], wt_sb[:, kc, mc * 128:(mc + 1) * 128],
                            mq_sb[:, kc, nsl], start=(kc == 0), stop=False)
                    return e.matmul(bank[:, :], id_bf[:, :], mq_sb[:, mc, nsl],
                                    start=False, stop=True)
                tm = E("tensor", mmo, waits=[t_tfold[0], t_tfold[1],
                                             g_ab[mc % 4]], inc="t")
                tsc = E("scalar", lambda e, mc=mc, bank=bank, b=b: e.activation(
                    out=te_sc[mc % 2][:, :], in_=bank[:, :], func=AF.Identity,
                    scale=modc(MT_GA, mc, b), bias=gabt[:, b, mc:mc + 1]),
                    waits=[tm, g_te[mc % 2]], inc="s")
                g_ab[mc % 4] = tsc
                run_steps(s3_pend, 3)
                tv1 = E("vector", lambda e, mc=mc, xb=xb: e.tensor_add(
                    xm_sb[:, mc, :], te_sc[mc % 2][:, :], x_sb[xb][:, mc, :]),
                    waits=[tsc, tld, g_xm_g, g_xm_cons], inc="v")
                g_te[mc % 2] = tv1
                g_xm_g = None
            t_xm = p.tok("v")
            x_cons[xb] = [t_xm]
            t_st = E("gpsimd", lambda e, t=t: e.dma_start(
                out=xm_hbm[t, :, :, :], in_=xm_sb[:, :, :]),
                waits=[t_xm], inc="gd", n=16)
            g_xm_g = t_st
            run_steps(s3_pend, 99)
            ST3, steps3 = mk_chain(t, "s3")
            ST3["t_src"] = t_xm
            run_steps(steps3, 1)
            g_xm_cons = ST3["t_cp"]
            s3_pend = steps3
        run_steps(s3_pend, 99)
        if b + 1 < B:
            t_wt_g = E("gpsimd", lambda e: e.dma_start(
                out=wt_sb[:, :, :], in_=wt_in[:, :, :]),
                waits=[p.tok("t")], inc="gd", n=16)
            t_wt_fresh = [t_wt_g, t_wt_g]

    t_ph1_v = p.tok("v")

    t_ph1_g = p.tok("g")
    ttm = E("tensor", lambda e: e.transpose(
        ps_f[0:1, 0:128], absg[:, :], id_f[:, :]),
        waits=[t_ph1_v, t_ph1_g, g_psf_v], inc="t")
    tv = E("vector", lambda e: e.tensor_reduce(
        out=rowf[0][:, 0:1], in_=ps_f[0:1, 0:128], axis=AX, op=AL.max),
        waits=[ttm], inc="v")
    tg = E("gpsimd", lambda e: e.dma_start(
        out=cc1_i.rearrange("(a x) -> a x", a=1), in_=rowf[0][:, 0:1]),
        waits=[tv], inc="gd", n=16)
    E("gpsimd", lambda e: e.collective_compute(
        "AllReduce", AL.max, replica_groups=[list(range(8))],
        ins=[cc1_i.opt()], outs=[cc1_o.opt()]), waits=[tg], inc="cc")

    ps1.close()
    ph1.close()
    watt.close()
    p.barrier(owner)

    ph2 = ExitStack()
    wgT = ph2.enter_context(SB("wgT", [128, KD, I], BF16))
    wuT = ph2.enter_context(SB("wuT", [128, KD, I], BF16))
    wst = [ph2.enter_context(SB(f"wst{i}", [128, D], F32)) for i in range(4)]
    wsq = [ph2.enter_context(SB(f"wsq{i}", [128, D], BF16)) for i in range(4)]
    wrow4 = ph2.enter_context(SB("wrow4", [128, 4], F32))
    xm2_sb = [ph2.enter_context(SB(f"xm2_{i}", [128, KD, NT], F32)) for i in range(2)]
    xq_sb = [ph2.enter_context(SB(f"xq{i}", [128, KD, NT], BF16)) for i in range(2)]
    mid_sb = ph2.enter_context(SB("mid_sb", [128, 4, NT], F32))
    r2_sb = [ph2.enter_context(SB(f"r2_{i}", [1, 3 * NT], BF16)) for i in range(2)]
    q_sc = [ph2.enter_context(SB(f"q_sc{i}", [128, NT], F32)) for i in range(2)]
    sil_sc = [ph2.enter_context(SB(f"sil{i}", [128, NT], F32)) for i in range(2)]
    red2 = ph2.enter_context(SB("red2", [128, 1], F32))
    gm1row = ph2.enter_context(SB("gm1row", [1, 4], F32))
    gm1b = ph2.enter_context(SB("gm1b", [1, 4], BF16))

    ps2 = ExitStack()
    p2_a = ps2.enter_context(PS("p2_a", [128, NT], F32))
    p2_b = ps2.enter_context(PS("p2_b", [128, NT], F32))
    p2_c = ps2.enter_context(PS("p2_c", [128, NT], F32))
    p2_d = ps2.enter_context(PS("p2_d", [128, NT], F32))
    p2_e = ps2.enter_context(PS("p2_e", [128, NT], F32))
    p2_f = ps2.enter_context(PS("p2_f", [128, NT], F32))
    ps2pre = ExitStack()
    p2_g = ps2pre.enter_context(PS("p2_g", [128, D], BF16))
    p2_h = ps2pre.enter_context(PS("p2_h", [128, 128], F32))

    WQQ = [("sync", "ld"), ("scalar", "sd"), ("gpsimd", "gd")]
    g_wst = [None, None, None, None]
    g_p2g_v = None

    def quant_weight(src_ap, n_chunks, dstT, scol, q0):
        nonlocal g_p2g_v
        p.strict_v = True
        for ci in range(n_chunks):
            sb_ = ci % 4
            eng, dsem = WQQ[(q0 + ci) % 3]
            tld = E(eng, lambda e, ci=ci, sb_=sb_, src_ap=src_ap: e.dma_start(
                out=wst[sb_][:, :], in_=src_ap[ci, :, :]),
                waits=[g_wst[sb_]], inc=dsem, n=16)
            E("vector", lambda e, sb_=sb_: e.tensor_reduce(
                out=wrow4[:, sb_:sb_ + 1], in_=wst[sb_][:, :], axis=AX,
                op=AL.max, apply_absolute_value=True),
                waits=[tld, g_wst[sb_]], inc="v")
            E("vector", lambda e, ci=ci, sb_=sb_, scol=scol: e.tensor_scalar(
                out=scol[:, ci:ci + 1], in0=wrow4[:, sb_:sb_ + 1],
                scalar1=1.0 / 127.0, scalar2=None,
                op0=AL.mult, op1=AL.bypass), inc="v")
            E("vector", lambda e, ci=ci, sb_=sb_, scol=scol: e.reciprocal(
                wrow4[:, sb_:sb_ + 1], scol[:, ci:ci + 1]), inc="v")
            E("scalar", lambda e, sb_=sb_: e.activation(
                out=wst[sb_][:, :], in_=wst[sb_][:, :], func=AF.Identity,
                scale=wrow4[:, sb_:sb_ + 1], bias=MAGIC),
                waits=[p.tok("v")], inc="s")
            tq = E("scalar", lambda e, sb_=sb_: e.activation(
                out=wsq[sb_][:, :], in_=wst[sb_][:, :], func=AF.Identity,
                bias=-MAGIC), inc="s")
            g_wst[sb_] = tq

            def tpw(e, sb_=sb_):
                last = None
                for kc in range(KD):
                    last = e.transpose(p2_g[:, kc * 128:(kc + 1) * 128],
                                       wsq[sb_][:, kc * 128:(kc + 1) * 128],
                                       id_bf[:, :])
                return last
            ttw = E("tensor", tpw, waits=[tq, g_p2g_v], inc="t")
            g_p2g_v = E("vector", lambda e, ci=ci, dstT=dstT: e.tensor_copy(
                out=dstT[:, :, ci * 128:(ci + 1) * 128], in_=p2_g[:, :]),
                waits=[ttw], inc="v")

    quant_weight(wg_in, KI, wgT, sgc, 0)
    quant_weight(wu_in, KI, wuT, suc, KI)
    p.strict_v = False
    t_wq_done = p.tok("v")

    tg = E("gpsimd", lambda e: e.dma_start(
        out=gm1row[:, 0:1], in_=cc1_o.rearrange("(a x) -> a x", a=1)),
        waits=[p.tok("cc")], inc="gd", n=16)
    p.strict_v = True
    ts_ = E("scalar", lambda e: e.activation(
        out=gm1row[:, 1:2], in_=clipx_sb[:, :], func=AF.Sigmoid),
        waits=[tg], inc="s")
    E("vector", lambda e: e.tensor_mul(
        gm1row[:, 0:1], gm1row[:, 0:1], gm1row[:, 1:2]),
        waits=[tg, ts_], inc="v")
    E("vector", lambda e: e.tensor_scalar(
        out=gm1row[:, 0:1], in0=gm1row[:, 0:1], scalar1=1.0 / 127.0,
        scalar2=None, op0=AL.mult, op1=AL.bypass), inc="v")
    E("vector", lambda e: e.reciprocal(gm1row[:, 1:2], gm1row[:, 0:1]), inc="v")
    E("vector", lambda e: e.tensor_copy(out=gm1b[:, 0:1], in_=gm1row[:, 0:1]),
      inc="v")
    E("vector", lambda e: e.tensor_sub(gm1b[:, 1:2], gm1row[:, 0:1],
                                       gm1b[:, 0:1]), inc="v")
    E("vector", lambda e: e.tensor_copy(out=gm1b[:, 2:3], in_=gm1row[:, 1:2]),
      inc="v")
    tv = E("vector", lambda e: e.tensor_sub(
        gm1b[:, 3:4], gm1row[:, 1:2], gm1b[:, 2:3]), inc="v")

    def bsc(e):
        e.matmul(p2_h[:, 0:1], ones_row[:, :], gm1b[:, 0:1], start=True, stop=False)
        e.matmul(p2_h[:, 0:1], ones_row[:, :], gm1b[:, 1:2], start=False, stop=True)
        e.matmul(p2_h[:, 1:2], ones_row[:, :], gm1b[:, 2:3], start=True, stop=False)
        return e.matmul(p2_h[:, 1:2], ones_row[:, :], gm1b[:, 3:4], start=False,
                        stop=True)
    tb = E("tensor", bsc, waits=[tv], inc="t")
    E("vector", lambda e: e.tensor_copy(out=s1col[:, :], in_=p2_h[:, 0:1]),
      waits=[tb], inc="v")
    tv = E("vector", lambda e: e.tensor_copy(out=is1col[:, :], in_=p2_h[:, 1:2]),
           inc="v")
    E("vector", lambda e: e.tensor_scalar(
        out=sg1[:, :], in0=sgc[:, :], scalar1=s1col[:, :], scalar2=None,
        op0=AL.mult, op1=AL.bypass), waits=[tv, t_wq_done], inc="v")
    E("vector", lambda e: e.tensor_scalar(
        out=su1[:, :], in0=suc[:, :], scalar1=s1col[:, :], scalar2=None,
        op0=AL.mult, op1=AL.bypass), inc="v")
    E("vector", lambda e: e.tensor_scalar(
        out=c2s1[:, :], in0=rng_(MT_CB), scalar1=is1col[:, :], scalar2=None,
        op0=AL.mult, op1=AL.bypass), inc="v")
    t_sc1 = E("vector", lambda e: e.tensor_scalar(
        out=d2s1[:, :], in0=rng_(MT_SB), scalar1=is1col[:, :], scalar2=None,
        op0=AL.mult, op1=AL.bypass), inc="v")
    p.strict_v = False

    ps2pre.close()
    bc_e = [p2_e, ps2.enter_context(PS("p2_e1", [128, NT], F32))]
    bc_f = [p2_f, ps2.enter_context(PS("p2_f1", [128, NT], F32))]

    g_xm2 = [None, None]
    g_xq_t = [None, None]
    g_r2 = [None, None]
    g_bct = [t_sc1, t_sc1]
    g_ab2 = [None, None]
    g_cd2 = [None, None]
    g_sil = [None, None]
    g_mid = [None, None]
    g_qsc = [None, None]
    t_xq_tile = [None] * (T + 1)
    tb_tile = [None] * (T + 1)

    def ph2_loads(t):
        xb = t % 2
        E("sync", lambda e, t=t, xb=xb: e.dma_start(
            out=xm2_sb[xb][:, :, :], in_=xm_hbm[t, :, :, :]),
            waits=[g_xm2[xb]], inc="ld", n=16)
        E("sync", lambda e, t=t, xb=xb: e.dma_start(
            out=r2_sb[xb][:, :],
            in_=rows2_hbm[t].rearrange("a n -> (a n)").rearrange(
                "(a n) -> a n", a=1)),
            waits=[g_r2[xb]], inc="ld", n=16)
        return p.tok("ld")

    def ph2_bcast(t, t_lr):
        xb = t % 2
        pe_, pf_ = bc_e[t % 2], bc_f[t % 2]

        def mmb2(e, xb=xb, pe_=pe_, pf_=pf_):
            e.matmul(pf_[:, :], ones_row[:, :], r2_sb[xb][:, 2 * NT:3 * NT],
                     start=True, stop=True)
            e.matmul(pe_[:, :], ones_row[:, :], r2_sb[xb][:, 0:NT], start=True,
                     stop=False)
            return e.matmul(pe_[:, :], ones_row[:, :], r2_sb[xb][:, NT:2 * NT],
                            start=False, stop=True)
        tb = E("tensor", mmb2, waits=[t_lr, g_bct[t % 2], t_sc1], inc="t")
        g_r2[xb] = tb
        tb_tile[t] = tb

    def ph2_qchunk(t, c):
        b = t // TPB
        xb = t % 2
        qp = c % 2
        pe_, pf_ = bc_e[t % 2], bc_f[t % 2]
        E("vector", lambda e, c=c, xb=xb, qp=qp, pf_=pf_: e.tensor_sub(
            q_sc[qp][:, :], xm2_sb[xb][:, c, :], pf_[:, :]),
            waits=[tb_tile[t], g_qsc[qp]], inc="v")
        E("vector", lambda e, c=c, b=b, qp=qp, pe_=pe_: e.scalar_tensor_tensor(
            out=q_sc[qp][:, :], in0=q_sc[qp][:, :],
            scalar=c2s1[:, c * B + b:c * B + b + 1],
            in1=pe_[:, :], op0=AL.mult, op1=AL.mult), inc="v")
        E("vector", lambda e, c=c, b=b, qp=qp: e.tensor_scalar(
            out=q_sc[qp][:, :], in0=q_sc[qp][:, :],
            scalar1=d2s1[:, c * B + b:c * B + b + 1],
            scalar2=MAGIC, op0=AL.add, op1=AL.add), inc="v")
        E("vector", lambda e, qp=qp: e.tensor_scalar(
            out=q_sc[qp][:, :], in0=q_sc[qp][:, :], scalar1=MAGIC,
            scalar2=-128.0, op0=AL.subtract, op1=AL.max), inc="v")
        tvq = E("vector", lambda e, c=c, xb=xb, qp=qp: e.tensor_scalar(
            out=xq_sb[xb][:, c, :], in0=q_sc[qp][:, :], scalar1=127.0,
            scalar2=None, op0=AL.min, op1=AL.bypass),
            waits=[g_xq_t[xb]], inc="v")
        g_qsc[qp] = tvq
        g_xq_t[xb] = None
        if c == KD - 1:
            t_xq_tile[t] = tvq
            g_xm2[xb] = tvq
            g_bct[t % 2] = tvq

    def ph2_group(t, mc):
        xb = t % 2
        bga = [p2_a, p2_b][mc % 2]
        bup = [p2_c, p2_d][mc % 2]

        def mmgu(e, mc=mc, xb=xb, bga=bga, bup=bup):
            last = None
            for kc in range(KD):
                e.matmul(bga[:, :], wgT[:, kc, mc * 128:(mc + 1) * 128],
                         xq_sb[xb][:, kc, :], start=(kc == 0),
                         stop=(kc == KD - 1))
                last = e.matmul(bup[:, :], wuT[:, kc, mc * 128:(mc + 1) * 128],
                                xq_sb[xb][:, kc, :], start=(kc == 0),
                                stop=(kc == KD - 1))
            return last
        tm = E("tensor", mmgu,
               waits=[t_xq_tile[t], g_ab2[mc % 2], g_cd2[mc % 2]], inc="t")
        tss = E("scalar", lambda e, mc=mc, bga=bga: e.activation(
            out=sil_sc[mc % 2][:, :], in_=bga[:, :], func=AF.Silu,
            scale=sg1[:, mc:mc + 1]), waits=[tm, g_sil[mc % 2]], inc="s")
        g_ab2[mc % 2] = tss
        tvm = E("vector", lambda e, mc=mc, bup=bup: e.scalar_tensor_tensor(
            out=mid_sb[:, mc % 4, :], in0=bup[:, :],
            scalar=su1[:, mc:mc + 1], in1=sil_sc[mc % 2][:, :],
            op0=AL.mult, op1=AL.mult),
            waits=[tss, g_mid[(mc // 2) % 2]], inc="v")
        g_mid[(mc // 2) % 2] = None
        g_cd2[mc % 2] = tvm
        g_sil[mc % 2] = tvm
        E("vector", lambda e, mc=mc: e.tensor_reduce(
            out=red2[:, :], in_=mid_sb[:, mc % 4, :], axis=AX, op=AL.max,
            apply_absolute_value=True), inc="v")
        tvm2 = E("vector", lambda e: e.tensor_max(
            absacc2[:, :], absacc2[:, :], red2[:, :]),
            waits=[p.tok("v")], inc="v")
        if mc % 2 == 1:
            hb = (mc // 2) % 2
            eng, dsem = ("gpsimd", "gd") if hb == 0 else ("scalar", "sd")
            g_mid[hb] = E(eng, lambda e, t=t, mc=mc, hb=hb: e.dma_start(
                out=mid_hbm[t, :, mc - 1:mc + 1, :],
                in_=mid_sb[:, hb * 2:hb * 2 + 2, :]),
                waits=[tvm2], inc=dsem, n=16)

    t_lrn = ph2_loads(0)
    ph2_bcast(0, t_lrn)
    for c in range(KD):
        ph2_qchunk(0, c)
    for t in range(T):
        if t + 1 < T:
            t_lrn = ph2_loads(t + 1)
        for mc in range(KI):
            ph2_group(t, mc)
            if t + 1 < T:
                if mc == 0:
                    ph2_bcast(t + 1, t_lrn)
                elif mc % 2 == 0 and 2 <= mc <= 16:
                    ph2_qchunk(t + 1, mc // 2 - 1)
        g_xq_t[t % 2] = p.tok("t")

    ttm = E("tensor", lambda e: e.transpose(
        bc_e[0][0:1, 0:128], absacc2[:, :], id_f[:, :]),
        waits=[p.tok("v")], inc="t")
    tv = E("vector", lambda e: e.tensor_reduce(
        out=gm1row[:, 0:1], in_=bc_e[0][0:1, 0:128], axis=AX, op=AL.max),
        waits=[ttm], inc="v")
    tg = E("gpsimd", lambda e: e.dma_start(
        out=cc2_i.rearrange("(a x) -> a x", a=1), in_=gm1row[:, 0:1]),
        waits=[tv], inc="gd", n=16)
    E("gpsimd", lambda e: e.collective_compute(
        "AllReduce", AL.max, replica_groups=[list(range(8))],
        ins=[cc2_i.opt()], outs=[cc2_o.opt()]), waits=[tg], inc="cc")

    ps2.close()
    ph2.close()
    p.barrier(owner)

    ph3 = ExitStack()
    wdT = ph3.enter_context(SB("wdT", [128, KI, D], BF16))
    w3st = [ph3.enter_context(SB(f"w3st{i}", [128, I], F32)) for i in range(2)]
    w3sq = [ph3.enter_context(SB(f"w3sq{i}", [128, I], BF16)) for i in range(2)]
    w3row = ph3.enter_context(SB("w3row", [128, 2], F32))
    mid3 = [ph3.enter_context(SB(f"mid3_{i}", [128, 2, NT], F32)) for i in range(4)]
    xq3 = [ph3.enter_context(SB(f"xq3_{i}", [128, KI, NT], BF16)) for i in range(2)]
    xm3 = [ph3.enter_context(SB(f"xm3_{i}", [128, KD, NT], F32)) for i in range(2)]
    q3m = [ph3.enter_context(SB(f"q3m{i}", [128, NT], F32)) for i in range(2)]
    te3 = [ph3.enter_context(SB(f"te3_{i}", [128, NT], F32)) for i in range(2)]
    gm2row = ph3.enter_context(SB("gm2row", [1, 4], F32))
    gm2b = ph3.enter_context(SB("gm2b", [1, 4], BF16))

    ps3 = ExitStack()
    p3_a = ps3.enter_context(PS("p3_a", [128, NT], F32))
    p3_b = ps3.enter_context(PS("p3_b", [128, NT], F32))
    p3_c = ps3.enter_context(PS("p3_c", [128, NT], F32))
    p3_d = ps3.enter_context(PS("p3_d", [128, NT], F32))
    ps3pre = ExitStack()
    p3_g = ps3pre.enter_context(PS("p3_g", [128, 11 * 128], BF16))
    p3_h = ps3pre.enter_context(PS("p3_h", [128, 128], F32))

    g_w3 = [None, None]
    g_p3g_v = None
    p.strict_v = True
    for ci in range(KD):
        sb_ = ci % 2
        eng3, dsem3 = [("sync", "ld"), ("scalar", "sd")][ci % 2]
        tld = E(eng3, lambda e, ci=ci, sb_=sb_: e.dma_start(
            out=w3st[sb_][:, :], in_=wd_in[ci, :, :]), waits=[g_w3[sb_]],
            inc=dsem3, n=16)
        E("vector", lambda e, sb_=sb_: e.tensor_reduce(
            out=w3row[:, sb_:sb_ + 1], in_=w3st[sb_][:, :], axis=AX, op=AL.max,
            apply_absolute_value=True), waits=[tld, g_w3[sb_]], inc="v")
        E("vector", lambda e, ci=ci, sb_=sb_: e.tensor_scalar(
            out=sdc[:, ci:ci + 1], in0=w3row[:, sb_:sb_ + 1],
            scalar1=1.0 / 127.0, scalar2=None,
            op0=AL.mult, op1=AL.bypass), inc="v")
        E("vector", lambda e, ci=ci, sb_=sb_: e.reciprocal(
            w3row[:, sb_:sb_ + 1], sdc[:, ci:ci + 1]), inc="v")
        E("scalar", lambda e, sb_=sb_: e.activation(
            out=w3st[sb_][:, :], in_=w3st[sb_][:, :], func=AF.Identity,
            scale=w3row[:, sb_:sb_ + 1], bias=MAGIC),
            waits=[p.tok("v")], inc="s")
        tq = E("scalar", lambda e, sb_=sb_: e.activation(
            out=w3sq[sb_][:, :], in_=w3st[sb_][:, :], func=AF.Identity,
            bias=-MAGIC), inc="s")
        g_w3[sb_] = tq
        for half in range(2):
            def tpw3(e, sb_=sb_, half=half):
                last = None
                for j in range(11):
                    kc = half * 11 + j
                    last = e.transpose(p3_g[:, j * 128:(j + 1) * 128],
                                       w3sq[sb_][:, kc * 128:(kc + 1) * 128],
                                       id_bf[:, :])
                return last
            ttw = E("tensor", tpw3, waits=[tq, g_p3g_v], inc="t")
            g_p3g_v = E("vector", lambda e, ci=ci, half=half: e.tensor_copy(
                out=wdT[:, half * 11:(half + 1) * 11, ci * 128:(ci + 1) * 128],
                in_=p3_g[:, :]), waits=[ttw], inc="v")
    t_wd_ready = p.tok("v")
    p.strict_v = False

    p.strict_v = True
    tg = E("gpsimd", lambda e: e.dma_start(
        out=gm2row[:, 0:1], in_=cc2_o.rearrange("(a x) -> a x", a=1)),
        waits=[p.tok("cc")], inc="gd", n=16)
    ts_ = E("scalar", lambda e: e.activation(
        out=gm2row[:, 1:2], in_=clipd_sb[:, :], func=AF.Sigmoid),
        waits=[tg], inc="s")
    E("vector", lambda e: e.tensor_mul(
        gm2row[:, 0:1], gm2row[:, 0:1], gm2row[:, 1:2]), waits=[tg, ts_], inc="v")
    E("vector", lambda e: e.tensor_scalar(
        out=gm2row[:, 0:1], in0=gm2row[:, 0:1], scalar1=1.0 / 127.0,
        scalar2=None, op0=AL.mult, op1=AL.bypass), inc="v")
    E("vector", lambda e: e.reciprocal(gm2row[:, 1:2], gm2row[:, 0:1]), inc="v")
    E("vector", lambda e: e.tensor_copy(out=gm2b[:, 0:1], in_=gm2row[:, 0:1]),
      inc="v")
    E("vector", lambda e: e.tensor_sub(gm2b[:, 1:2], gm2row[:, 0:1],
                                       gm2b[:, 0:1]), inc="v")
    E("vector", lambda e: e.tensor_copy(out=gm2b[:, 2:3], in_=gm2row[:, 1:2]),
      inc="v")
    tv = E("vector", lambda e: e.tensor_sub(
        gm2b[:, 3:4], gm2row[:, 1:2], gm2b[:, 2:3]), inc="v")

    def bsc2(e):
        e.matmul(p3_h[:, 0:1], ones_row[:, :], gm2b[:, 0:1], start=True, stop=False)
        e.matmul(p3_h[:, 0:1], ones_row[:, :], gm2b[:, 1:2], start=False, stop=True)
        e.matmul(p3_h[:, 1:2], ones_row[:, :], gm2b[:, 2:3], start=True, stop=False)
        return e.matmul(p3_h[:, 1:2], ones_row[:, :], gm2b[:, 3:4], start=False,
                        stop=True)
    tb = E("tensor", bsc2, waits=[tv], inc="t")
    E("vector", lambda e: e.tensor_copy(out=s2col[:, :], in_=p3_h[:, 0:1]),
      waits=[tb], inc="v")
    tv = E("vector", lambda e: e.tensor_copy(out=is2col[:, :], in_=p3_h[:, 1:2]),
           inc="v")
    p.strict_v = False
    t_is2 = p.tok("v")
    t_s2 = E("vector", lambda e: e.tensor_scalar(
        out=sd2[:, :], in0=sdc[:, :], scalar1=s2col[:, :], scalar2=None,
        op0=AL.mult, op1=AL.bypass), waits=[t_is2, t_wd_ready], inc="v")

    NSL = 11
    M3Q = [("sync", "ld"), ("scalar", "sd"), ("gpsimd", "gd")]
    slot_load = {}
    slot_free = {}
    g_xq3_t = [None, None]
    g_xm3 = [None, None]
    g_ab3 = [None, None, None, None]
    g_te3 = [None, None]
    g_q3m = [None, None]
    t_q3_tile = [None] * T
    ldx_tile = [None] * T

    def emit_mid_load(k):
        t, pr = divmod(k, NSL)
        if t >= T:
            return
        mb = k % 4
        eng3, dsem3 = M3Q[k % 3]
        slot_load[k] = E(eng3, lambda e, t=t, pr=pr, mb=mb: e.dma_start(
            out=mid3[mb][:, :, :], in_=mid_hbm[t, :, pr * 2:pr * 2 + 2, :]),
            waits=[slot_free.get(k - 4)], inc=dsem3, n=16)

    def ph3_xmload(t):
        xb = t % 2
        eng3, dsem3 = [("sync", "ld"), ("scalar", "sd")][t % 2]
        ldx_tile[t] = E(eng3, lambda e, t=t, xb=xb: e.dma_start(
            out=xm3[xb][:, :, :], in_=xm_hbm[t, :, :, :]), waits=[g_xm3[xb]],
            inc=dsem3, n=16)

    def ph3_qchunk(t, pr):
        k = t * NSL + pr
        xb = t % 2
        mb = k % 4
        tld = slot_load[k]
        tsm = None
        for j in range(2):
            kc = pr * 2 + j
            qp = kc % 2
            tsm = E("scalar", lambda e, mb=mb, j=j, qp=qp: e.activation(
                out=q3m[qp][:, :], in_=mid3[mb][:, j, :], func=AF.Identity,
                scale=is2col[:, :], bias=MAGIC),
                waits=[tld, t_is2, g_q3m[qp]], inc="s")
            E("vector", lambda e, qp=qp: e.tensor_scalar(
                out=q3m[qp][:, :], in0=q3m[qp][:, :], scalar1=MAGIC,
                scalar2=-128.0, op0=AL.subtract, op1=AL.max),
                waits=[tsm], inc="v")
            tvq = E("vector", lambda e, kc=kc, xb=xb, qp=qp: e.tensor_scalar(
                out=xq3[xb][:, kc, :], in0=q3m[qp][:, :], scalar1=127.0,
                scalar2=None, op0=AL.min, op1=AL.bypass),
                waits=[g_xq3_t[xb]], inc="v")
            g_q3m[qp] = tvq
            g_xq3_t[xb] = None
        slot_free[k] = tsm
        if pr == NSL - 1:
            t_q3_tile[t] = p.tok("v")
        emit_mid_load(k + 4)

    def ph3_group(t, mc):
        b = t // TPB
        xb = t % 2
        bank = [p3_a, p3_b, p3_c, p3_d][mc % 4]

        def mmd(e, mc=mc, xb=xb, bank=bank):
            last = None
            for kc in range(KI):
                last = e.matmul(bank[:, :], wdT[:, kc, mc * 128:(mc + 1) * 128],
                                xq3[xb][:, kc, :], start=(kc == 0),
                                stop=(kc == KI - 1))
            return last
        tm = E("tensor", mmd,
               waits=[t_q3_tile[t], t_wd_ready, g_ab3[mc % 4]], inc="t")
        tsc = E("scalar", lambda e, mc=mc, bank=bank: e.activation(
            out=te3[mc % 2][:, :], in_=bank[:, :], func=AF.Identity,
            scale=sd2[:, mc:mc + 1]), waits=[tm, t_s2, g_te3[mc % 2]], inc="s")
        g_ab3[mc % 4] = tsc
        tvy = E("vector", lambda e, mc=mc, xb=xb, b=b: e.scalar_tensor_tensor(
            out=xm3[xb][:, mc, :], in0=te3[mc % 2][:, :],
            scalar=modc(MT_GB, mc, b), in1=xm3[xb][:, mc, :],
            op0=AL.mult, op1=AL.add), waits=[tsc, ldx_tile[t]], inc="v")
        g_te3[mc % 2] = tvy

    for k in range(4):
        emit_mid_load(k)
    ph3_xmload(0)
    for pr in range(NSL):
        ph3_qchunk(0, pr)
    for t in range(T):
        if t + 1 < T:
            ph3_xmload(t + 1)
        nxt_pr = 0
        for mc in range(KD):
            ph3_group(t, mc)
            if t + 1 < T:
                tgt = min(NSL, 2 * (mc + 1))
                while nxt_pr < tgt:
                    ph3_qchunk(t + 1, nxt_pr)
                    nxt_pr += 1
        t_y = p.tok("v")
        xb = t % 2
        g_xm3[xb] = E("gpsimd", lambda e, t=t, xb=xb: e.dma_start(
            out=y_out[t, :, :, :], in_=xm3[xb][:, :, :]), waits=[t_y],
            inc="gd", n=16)
        g_xq3_t[xb] = p.tok("t")

    E("gpsimd", None, waits=[p.tok("gd")])
    p.barrier(owner, skip=())

    ps3pre.close()
    ps3.close()
    ph3.close()

    p.finalize()

    def replay(eng, name):
        for ws, fn, inc, n in p.steps[name]:
            for s, v in ws:
                eng.wait_ge(sem[s], v)
            inst = fn(eng) if fn is not None else None
            if inc is not None:
                if inst is None:
                    inst = eng.nop()
                inst.then_inc(sem[inc], n)

    with nc.Block() as block:
        block.sync(lambda e: replay(e, "sync"))
        block.gpsimd(lambda e: replay(e, "gpsimd"))
        block.scalar(lambda e: replay(e, "scalar"))
        block.vector(lambda e: replay(e, "vector"))
        block.tensor(lambda e: replay(e, "tensor"))

    root.close()
    return nc


_NC_CACHE = {}


def _get_nc():
    if "nc" not in _NC_CACHE:
        _NC_CACHE["nc"] = build_nc()
    return _NC_CACHE["nc"]


def kernel(L0_attn_hidden_states, L0_attn_Wq, L0_attn_bq, L0_attn_Wqa,
           L0_attn_bqa, L0_attn_Wk, L0_attn_bk, L0_attn_Wka, L0_attn_bka,
           L0_attn_Wt, L0_attn_bt, L0_ffn_clip_x, L0_ffn_clip_down,
           L0_ffn_w_gate, L0_ffn_w_up, L0_ffn_w_down, L0_c, L0_wmod):
    f32 = np.float32

    def wmat(w):
        M = np.asarray(w).shape[1]
        return np.ascontiguousarray(
            np.asarray(w, f32).reshape(KD, 128, M).transpose(1, 0, 2))

    def bvec(v):
        return np.ascontiguousarray(np.asarray(v, f32).reshape(KD, 128).T)

    x = np.asarray(L0_attn_hidden_states, f32)
    base = dict(
        wq=wmat(L0_attn_Wq), wk=wmat(L0_attn_Wk), wt=wmat(L0_attn_Wt),
        wqa=wmat(L0_attn_Wqa), wka=wmat(L0_attn_Wka),
        bq=bvec(L0_attn_bq), bk=bvec(L0_attn_bk), bt=bvec(L0_attn_bt),
        bqa=np.ascontiguousarray(np.asarray(L0_attn_bqa, f32).reshape(H, 1)),
        bka=np.ascontiguousarray(np.asarray(L0_attn_bka, f32).reshape(H, 1)),
        wg=np.ascontiguousarray(np.asarray(L0_ffn_w_gate, f32).reshape(KI, 128, D)),
        wu=np.ascontiguousarray(np.asarray(L0_ffn_w_up, f32).reshape(KI, 128, D)),
        wd=np.ascontiguousarray(np.asarray(L0_ffn_w_down, f32).reshape(KD, 128, I)),
        wmod=wmat(L0_wmod),
        clipx=np.ascontiguousarray(np.asarray(L0_ffn_clip_x, f32).reshape(1, 1)),
        clipd=np.ascontiguousarray(np.asarray(L0_ffn_clip_down, f32).reshape(1, 1)),
    )
    c_all = np.asarray(L0_c, f32)

    in_maps = []
    for core in range(8):
        xs = x[core * B:(core + 1) * B]
        xh = xs.reshape(B, TPB, NT, KD, 128).transpose(0, 1, 4, 3, 2)
        xh = np.ascontiguousarray(xh).reshape(T, 128, KD * NT)
        ch = c_all[core * B:(core + 1) * B]
        ch = np.ascontiguousarray(ch.reshape(B, KD, 128).transpose(2, 1, 0))
        m = dict(base)
        m["x"] = xh
        m["cmod"] = ch
        in_maps.append(m)

    nc = _get_nc()
    res = run_bass_kernel_spmd(nc, in_maps, core_ids=list(range(8)))

    outs = []
    for core in range(8):
        y = np.asarray(res.results[core]["y"])
        y = y.reshape(B, TPB, 128, KD, NT).transpose(0, 1, 4, 3, 2)
        outs.append(y.reshape(B, 2048, D))
    return np.ascontiguousarray(np.concatenate(outs, axis=0))


# version 16
# speedup vs baseline: 1.0823x; 1.0823x over previous
import numpy as np
from contextlib import ExitStack

import concourse.bass as bass
import concourse.mybir as mybir
from concourse.bass_utils import run_bass_kernel_spmd

F32 = mybir.dt.float32
BF16 = mybir.dt.bfloat16
AL = mybir.AluOpType
AF = mybir.ActivationFunctionType
AX = mybir.AxisListType.X

MAGIC = 12582912.0

D = 1024
KD = 8
H = 16
I = 2816
KI = 22
NT = 512
TPB = 4
B = 4
T = B * TPB
SCL = 0.125
EPS = 1e-5

ENGS = ("sync", "gpsimd", "scalar", "vector", "tensor")
DMA_SEMS = {"ld", "gd", "cc", "sd", "vd"}


class Prog:

    def __init__(self):
        self.steps = {e: [] for e in ENGS}
        self.cnt = {}
        self.strict_v = False

    def tok(self, s):
        return (s, self.cnt.get(s, 0))

    def emit(self, eng, fn=None, waits=(), inc=None, n=1):
        ws = [w for w in waits if w is not None and w[1]]
        if eng == "vector" and self.strict_v and self.cnt.get("v", 0):
            ws.append(("v", self.cnt["v"]))
        post = None
        if inc is not None:
            self.cnt[inc] = self.cnt.get(inc, 0) + n
            post = self.cnt[inc]
        self.steps[eng].append([ws, fn, inc, n])
        return (inc, post) if inc is not None else None

    def barrier(self, owner, skip=("cc",)):
        snap = {s: v for s, v in self.cnt.items() if v > 0 and s not in skip}
        for e in ENGS:
            ws = [(s, v) for s, v in snap.items() if owner.get(s) != e]
            self.emit(e, None, waits=ws)

    def finalize(self):
        used = {}
        for e in ENGS:
            for ws, fn, inc, n in self.steps[e]:
                for s, v in ws:
                    used.setdefault(s, set()).add(v)
        remap = {}
        for s in list(self.cnt):
            if s in DMA_SEMS:
                continue
            kept = sorted(used.get(s, ()))
            remap[s] = {v: i + 1 for i, v in enumerate(kept)}
        for e in ENGS:
            seen = {}
            out = []
            for ws, fn, inc, n in self.steps[e]:
                nws = []
                for s, v in ws:
                    if s in remap:
                        v = remap[s].get(v)
                        if v is None:
                            continue
                    if seen.get(s, 0) >= v:
                        continue
                    seen[s] = v
                    nws.append((s, v))
                if inc is not None and inc in remap:
                    pass
                out.append([nws, fn, inc, n])
            self.steps[e] = out
        run = {}
        for e in ENGS:
            for st in self.steps[e]:
                ws, fn, inc, n = st
                if inc is not None and inc in remap:
                    run[inc] = run.get(inc, 0) + n
                    if run[inc] in remap[inc] and remap[inc][run[inc]]:
                        st[2] = inc
                    else:
                        st[2] = None


def build_nc():
    nc = bass.Bass("TRN2", target_bir_lowering=False,
                   detect_race_conditions=False)

    x_in = nc.dram_tensor("x", [T, 128, KD * NT], F32, kind="ExternalInput").ap()
    wq_in = nc.dram_tensor("wq", [128, KD, D], F32, kind="ExternalInput").ap()
    wk_in = nc.dram_tensor("wk", [128, KD, D], F32, kind="ExternalInput").ap()
    wt_in = nc.dram_tensor("wt", [128, KD, D], F32, kind="ExternalInput").ap()
    wqa_in = nc.dram_tensor("wqa", [128, KD, H], F32, kind="ExternalInput").ap()
    wka_in = nc.dram_tensor("wka", [128, KD, H], F32, kind="ExternalInput").ap()
    bq_in = nc.dram_tensor("bq", [128, KD], F32, kind="ExternalInput").ap()
    bk_in = nc.dram_tensor("bk", [128, KD], F32, kind="ExternalInput").ap()
    bt_in = nc.dram_tensor("bt", [128, KD], F32, kind="ExternalInput").ap()
    bqa_in = nc.dram_tensor("bqa", [H, 1], F32, kind="ExternalInput").ap()
    bka_in = nc.dram_tensor("bka", [H, 1], F32, kind="ExternalInput").ap()
    wg_in = nc.dram_tensor("wg", [KI, 128, D], F32, kind="ExternalInput").ap()
    wu_in = nc.dram_tensor("wu", [KI, 128, D], F32, kind="ExternalInput").ap()
    wd_in = nc.dram_tensor("wd", [KD, 128, I], F32, kind="ExternalInput").ap()
    c_in = nc.dram_tensor("cmod", [128, KD, B], F32, kind="ExternalInput").ap()
    wm_in = nc.dram_tensor("wmod", [128, KD, 6 * D], F32, kind="ExternalInput").ap()
    clipx_in = nc.dram_tensor("clipx", [1, 1], F32, kind="ExternalInput").ap()
    clipd_in = nc.dram_tensor("clipd", [1, 1], F32, kind="ExternalInput").ap()

    y_out = nc.dram_tensor("y", [T, 128, KD, NT], F32, kind="ExternalOutput").ap()

    xm_hbm = nc.dram_tensor("xm_s", [T, 128, KD, NT], F32).ap()
    mid_hbm = nc.dram_tensor("mid_s", [T, 128, KI, NT], F32).ap()
    rows2_hbm = nc.dram_tensor("rows2_s", [T, 3, NT], BF16).ap()
    cc1_i = nc.dram_tensor("cc1_i", [1], F32).ap()
    cc1_o = nc.dram_tensor("cc1_o", [1], F32).ap()
    cc2_i = nc.dram_tensor("cc2_i", [1], F32).ap()
    cc2_o = nc.dram_tensor("cc2_o", [1], F32).ap()

    p = Prog()
    owner = {"ld": "sync", "gd": "gpsimd", "g": "gpsimd", "cc": "gpsimd",
             "sd": "scalar", "vd": "vector",
             "t": "tensor", "v": "vector", "s": "scalar"}
    root = ExitStack()
    sem = {n: root.enter_context(nc.semaphore(n)) for n in owner}
    SB = lambda name, shape, dt: nc.sbuf_tensor(name, shape, dt)
    PS = lambda name, shape, dt: nc.psum_tensor(name, shape, dt)
    E = p.emit

    mods = root.enter_context(SB("mods", [128, 6 * KD * B], F32))
    gabt = root.enter_context(SB("gabt", [128, B, KD], F32))
    bqc = root.enter_context(SB("bqc", [128, KD], F32))
    bkc = root.enter_context(SB("bkc", [128, KD], F32))
    btc = root.enter_context(SB("btc", [128, KD], F32))
    bqa8 = root.enter_context(SB("bqa8", [H, 1], F32))
    bka8 = root.enter_context(SB("bka8", [H, 1], F32))
    id_bf = root.enter_context(SB("id_bf", [128, 128], BF16))
    id_f = root.enter_context(SB("id_f", [128, 128], F32))
    ones_col = root.enter_context(SB("ones_col", [128, 1], BF16))
    ones_row = root.enter_context(SB("ones_row", [1, 128], BF16))
    emaskT = root.enter_context(SB("emaskT", [H, KD * 128], BF16))
    absacc2 = root.enter_context(SB("absacc2", [128, 1], F32))
    clipx_sb = root.enter_context(SB("clipx_sb", [1, 1], F32))
    clipd_sb = root.enter_context(SB("clipd_sb", [1, 1], F32))
    s1col = root.enter_context(SB("s1col", [128, 1], F32))
    is1col = root.enter_context(SB("is1col", [128, 1], F32))
    s2col = root.enter_context(SB("s2col", [128, 1], F32))
    is2col = root.enter_context(SB("is2col", [128, 1], F32))
    sgc = root.enter_context(SB("sgc", [128, KI], F32))
    suc = root.enter_context(SB("suc", [128, KI], F32))
    sdc = root.enter_context(SB("sdc", [128, KD], F32))
    sg1 = root.enter_context(SB("sg1", [128, KI], F32))
    su1 = root.enter_context(SB("su1", [128, KI], F32))
    sd2 = root.enter_context(SB("sd2", [128, KD], F32))
    c2s1 = root.enter_context(SB("c2s1", [128, KD * B], F32))
    d2s1 = root.enter_context(SB("d2s1", [128, KD * B], F32))
    cst_eps = root.enter_context(SB("cst_eps", [128, 1], F32))
    cst_nh = root.enter_context(SB("cst_nh", [128, 1], F32))
    cst_mg = root.enter_context(SB("cst_mg", [128, 1], F32))
    cst_nmg = root.enter_context(SB("cst_nmg", [128, 1], F32))
    cst_scl = root.enter_context(SB("cst_scl", [128, 1], F32))

    MT_SA, MT_CA, MT_GA, MT_SB, MT_CB, MT_GB = range(6)

    def modc(t_, c, b):
        j = t_ * KD * B + c * B + b
        return mods[:, j:j + 1]

    def rng_(t_):
        return mods[:, t_ * KD * B:(t_ + 1) * KD * B]

    watt = ExitStack()
    wq_sb = watt.enter_context(SB("wq_sb", [128, KD, D], BF16))
    wk_sb = watt.enter_context(SB("wk_sb", [128, KD, D], BF16))
    wt_sb = watt.enter_context(SB("wt_sb", [128, KD, D], BF16))
    wqa_sb = watt.enter_context(SB("wqa_sb", [128, KD, H], BF16))
    wka_sb = watt.enter_context(SB("wka_sb", [128, KD, H], BF16))

    su_arena = ExitStack()
    c_f = su_arena.enter_context(SB("c_f", [128, KD * B], F32))
    c_bf = su_arena.enter_context(SB("c_bf", [128, KD, B], BF16))
    wm_sb = [su_arena.enter_context(SB(f"wm{i}", [128, 3 * D], F32)) for i in range(3)]
    wmb_sb = [su_arena.enter_context(SB(f"wmb{i}", [128, 3 * D], BF16)) for i in range(3)]
    wstg = [su_arena.enter_context(SB(f"wstg{i}", [128, D], F32)) for i in range(4)]
    modT = su_arena.enter_context(SB("modT", [B, 3 * D], F32))
    scr_a = su_arena.enter_context(SB("scr_a", [128, KD * B], F32))
    scr_b = su_arena.enter_context(SB("scr_b", [128, KD * B], F32))
    scr_c = su_arena.enter_context(SB("scr_c", [128, KD * B], F32))
    iota_i = su_arena.enter_context(SB("iota_i", [H, 128], mybir.dt.int32))
    jramp = su_arena.enter_context(SB("jramp", [H, 128], F32))
    hidx = su_arena.enter_context(SB("hidx", [H, 1], F32))
    tgt_sc = su_arena.enter_context(SB("tgt_sc", [H, 128], F32))
    mod_ps = su_arena.enter_context(PS("mod_ps", [B, 3 * D], F32))
    tp_ps = su_arena.enter_context(PS("tp_ps", [128, 96], F32))

    for dst, src in ((bqc, bq_in), (bkc, bk_in), (btc, bt_in)):
        E("sync", lambda e, d=dst, s=src: e.dma_start(out=d[:, :], in_=s[:, :]),
          inc="ld", n=16)
    E("sync", lambda e: e.dma_start(out=bqa8[:, :], in_=bqa_in[:, :]), inc="ld", n=16)
    E("sync", lambda e: e.dma_start(out=bka8[:, :], in_=bka_in[:, :]), inc="ld", n=16)
    E("sync", lambda e: e.dma_start(out=clipx_sb[:, :], in_=clipx_in[:, :]), inc="ld", n=16)
    E("sync", lambda e: e.dma_start(out=clipd_sb[:, :], in_=clipd_in[:, :]), inc="ld", n=16)
    E("sync", lambda e: e.dma_start(out=c_f[:, :], in_=c_in[:, :, :]), inc="ld", n=16)
    t_small = p.tok("ld")

    t_w_ld = {}
    for (key, dst, src) in (("wqa", wqa_sb, wqa_in), ("wka", wka_sb, wka_in)):
        t_w_ld[key] = E("gpsimd", lambda e, d=dst, s=src: e.dma_start(
            out=d[:, :, :], in_=s[:, :, :]), inc="gd", n=16)
    g_wstg = [None] * 4
    wi = 0
    for (key, dst, src, ceng, csem) in (
            ("wq", wq_sb, wq_in, "vector", "v"),
            ("wk", wk_sb, wk_in, "gpsimd", "g"),
            ("wt", wt_sb, wt_in, "vector", "v")):
        for c in range(KD):
            sbf = wi % 4
            eng0, sem0 = [("sync", "ld"), ("scalar", "sd")][wi % 2]
            tldw = E(eng0, lambda e, sbf=sbf, src=src, c=c: e.dma_start(
                out=wstg[sbf][:, :], in_=src[:, c, :]),
                waits=[g_wstg[sbf]], inc=sem0, n=16)
            g_wstg[sbf] = E(ceng, lambda e, sbf=sbf, dst=dst, c=c: e.tensor_copy(
                out=dst[:, c, :], in_=wstg[sbf][:, :]),
                waits=[tldw], inc=csem)
            wi += 1
        t_w_ld[key] = g_wstg[(wi - 1) % 4]

    nc.const_aps.register(F32, EPS, cst_eps[:, 0:1])
    nc.const_aps.register(F32, -0.5, cst_nh[:, 0:1])
    nc.const_aps.register(F32, MAGIC, cst_mg[:, 0:1])
    nc.const_aps.register(F32, -MAGIC, cst_nmg[:, 0:1])
    nc.const_aps.register(F32, SCL, cst_scl[:, 0:1])
    p.strict_v = True
    E("vector", lambda e: e.memset(cst_eps[:, :], EPS), inc="v")
    E("vector", lambda e: e.memset(cst_nh[:, :], -0.5), inc="v")
    E("vector", lambda e: e.memset(cst_mg[:, :], MAGIC), inc="v")
    E("vector", lambda e: e.memset(cst_nmg[:, :], -MAGIC), inc="v")
    E("vector", lambda e: e.memset(cst_scl[:, :], SCL), inc="v")
    E("vector", lambda e: e.memset(ones_col[:, :], 1.0 / D), inc="v")
    E("vector", lambda e: e.memset(ones_row[:, :], 1.0), inc="v")
    E("vector", lambda e: e.memset(absacc2[:, :], 0.0), inc="v")
    E("gpsimd", lambda e: e.memset(id_bf[:, :], 0.0))
    E("gpsimd", lambda e: e.affine_select(
        out=id_bf[:, :], in_=id_bf[:, :], compare_op=AL.not_equal, fill=1.0,
        base=0, pattern=[[-1, 128]], channel_multiplier=1), inc="g")
    E("gpsimd", lambda e: e.memset(id_f[:, :], 0.0))
    E("gpsimd", lambda e: e.affine_select(
        out=id_f[:, :], in_=id_f[:, :], compare_op=AL.not_equal, fill=1.0,
        base=0, pattern=[[-1, 128]], channel_multiplier=1), inc="g")
    E("gpsimd", lambda e: e.iota(iota_i[:, :], pattern=[[1, 128]], base=0,
                                 channel_multiplier=0), inc="g")
    t_io1 = p.tok("g")
    E("vector", lambda e: e.tensor_copy(out=jramp[:, :], in_=iota_i[:, :]),
      waits=[t_io1], inc="v")
    E("vector", lambda e: e.tensor_scalar(
        out=jramp[:, :], in0=jramp[:, :], scalar1=64.0, scalar2=None,
        op0=AL.is_ge, op1=AL.bypass), inc="v")
    t_jr = p.tok("v")
    E("gpsimd", lambda e: e.iota(iota_i[:, 0:1], pattern=[[0, 1]], base=0,
                                 channel_multiplier=1), waits=[t_jr], inc="g")
    t_io2 = p.tok("g")
    E("vector", lambda e: e.tensor_copy(out=hidx[:, :], in_=iota_i[:, 0:1]),
      waits=[t_io2], inc="v")
    for c in range(KD):
        E("vector", lambda e, c=c: e.tensor_scalar(
            out=tgt_sc[:, :], in0=jramp[:, :], scalar1=float(2 * c),
            scalar2=None, op0=AL.add, op1=AL.bypass), inc="v")
        E("vector", lambda e, c=c: e.tensor_scalar(
            out=emaskT[:, c * 128:(c + 1) * 128], in0=tgt_sc[:, :],
            scalar1=hidx[:, :], scalar2=None,
            op0=AL.is_equal, op1=AL.bypass), inc="v")
    t_masks = p.tok("v")

    tsil = E("scalar", lambda e: e.activation(
        out=c_f[:, :], in_=c_f[:, :], func=AF.Silu),
        waits=[t_small], inc="s")
    E("vector", lambda e: e.tensor_copy(out=c_bf[:, :, :], in_=c_f[:, :]),
      waits=[tsil], inc="v")
    t_cbf = p.tok("v")

    p.strict_v = False
    g_wm = [None, None, None]
    g_wmb = [None, None, None]
    g_modps_v = None
    WMQ = [("sync", "ld"), ("scalar", "sd"), ("gpsimd", "gd")]
    for half in range(2):
        t_mm_k = [None] * KD
        for kc in range(KD):
            kk = half * KD + kc
            engm, semm = WMQ[kk % 3]
            tld = E(engm, lambda e, kc=kc, half=half, kk=kk: e.dma_start(
                out=wm_sb[kk % 3][:, :],
                in_=wm_in[:, kc, half * 3 * D:(half + 1) * 3 * D]),
                waits=[g_wm[kk % 3]], inc=semm, n=16)
            ceng, csem = ("vector", "v") if kk % 2 == 0 else ("gpsimd", "g")
            tcast = E(ceng, lambda e, kk=kk: e.tensor_copy(
                out=wmb_sb[kk % 3][:, :], in_=wm_sb[kk % 3][:, :]),
                waits=[tld, g_wmb[kk % 3]], inc=csem)
            g_wm[kk % 3] = tcast

            def mm(e, kc=kc, kk=kk):
                last = None
                for j in range(6):
                    last = e.matmul(mod_ps[:, j * NT:(j + 1) * NT],
                                    c_bf[:, kc, :],
                                    wmb_sb[kk % 3][:, j * NT:(j + 1) * NT],
                                    start=(kc == 0), stop=(kc == KD - 1))
                return last
            t_mm_k[kc] = E("tensor", mm,
                           waits=[tcast, t_cbf, g_wmb[kk % 3], g_modps_v],
                           inc="t")
            g_wmb[kk % 3] = t_mm_k[kc]
        tv = E("vector", lambda e: e.tensor_copy(out=modT[:, :], in_=mod_ps[:, :]),
               waits=[t_mm_k[-1]], inc="v")
        g_modps_v = tv

        def tps(e):
            last = None
            for i in range(24):
                last = e.transpose(tp_ps[:, i * B:(i + 1) * B],
                                   modT[:, i * 128:(i + 1) * 128],
                                   id_f[0:B, 0:B])
            return last
        tt_ = E("tensor", tps, waits=[tv, p.tok("g")], inc="t")
        E("vector", lambda e, half=half: e.tensor_copy(
            out=mods[:, half * 96:(half + 1) * 96], in_=tp_ps[:, :]),
            waits=[tt_], inc="v")
        g_wm = [p.tok("t")] * 3
    t_mods_raw = p.tok("v")
    p.strict_v = True

    for tC in (MT_CA, MT_CB):
        E("vector", lambda e, tC=tC: e.tensor_scalar(
            out=rng_(tC), in0=rng_(tC), scalar1=1.0, scalar2=None,
            op0=AL.add, op1=AL.bypass), waits=[t_mods_raw], inc="v")
    for tG in (MT_GA, MT_GB):
        E("vector", lambda e, tG=tG: e.tensor_scalar(
            out=scr_a[:, :], in0=rng_(tG), scalar1=0.0, scalar2=None,
            op0=AL.is_ge, op1=AL.bypass), waits=[t_mods_raw], inc="v")
        E("vector", lambda e, tG=tG: e.tensor_scalar(
            out=scr_b[:, :], in0=rng_(tG), scalar1=0.25, scalar2=None,
            op0=AL.max, op1=AL.bypass), inc="v")
        E("vector", lambda e, tG=tG: e.tensor_scalar(
            out=scr_c[:, :], in0=rng_(tG), scalar1=-0.25, scalar2=None,
            op0=AL.min, op1=AL.bypass), inc="v")
        E("vector", lambda e: e.tensor_sub(scr_b[:, :], scr_b[:, :], scr_c[:, :]),
          inc="v")
        E("vector", lambda e: e.tensor_mul(scr_a[:, :], scr_a[:, :], scr_b[:, :]),
          inc="v")
        E("vector", lambda e, tG=tG: e.tensor_add(rng_(tG), scr_a[:, :], scr_c[:, :]),
          inc="v")
    for b in range(B):
        for mc in range(KD):
            E("vector", lambda e, b=b, mc=mc: e.tensor_mul(
                gabt[:, b, mc:mc + 1], modc(MT_GA, mc, b), btc[:, mc:mc + 1]),
                inc="v")
    E("vector", lambda e: e.tensor_scalar(
        out=bqa8[:, :], in0=bqa8[:, :], scalar1=SCL, scalar2=None,
        op0=AL.mult, op1=AL.bypass), waits=[t_small], inc="v")
    E("vector", lambda e: e.tensor_scalar(
        out=bka8[:, :], in0=bka8[:, :], scalar1=SCL, scalar2=None,
        op0=AL.mult, op1=AL.bypass), inc="v")

    p.strict_v = False
    su_arena.close()
    p.barrier(owner)

    ph1 = ExitStack()
    x_sb = [ph1.enter_context(SB(f"x{i}", [128, KD, NT], F32)) for i in range(2)]
    h_sb = [ph1.enter_context(SB(f"h_sb{i}", [128, KD, NT], BF16)) for i in range(2)]
    mq_sb = ph1.enter_context(SB("mq_sb", [128, KD, TPB * NT], BF16))
    mk_sb = ph1.enter_context(SB("mk_sb", [128, KD, TPB * NT], BF16))
    sc2 = [ph1.enter_context(SB(f"sc2_{i}", [H, NT], BF16)) for i in range(2)]
    pparts = ph1.enter_context(SB("pparts", [128, KD * TPB], F32))
    pqc = ph1.enter_context(SB("pqc", [128, KD], F32))
    pkc = ph1.enter_context(SB("pkc", [128, KD], F32))
    xb_sc = [ph1.enter_context(SB(f"xb{i}", [128, NT], BF16)) for i in range(4)]
    x2_sc = [ph1.enter_context(SB(f"x2{i}", [128, NT], BF16)) for i in range(4)]
    xcp = [ph1.enter_context(SB(f"xcp{i}", [128, NT], F32)) for i in range(2)]
    te_sc = [ph1.enter_context(SB(f"te{i}", [128, NT], F32)) for i in range(2)]
    redg_sc = [ph1.enter_context(SB(f"redg{i}", [128, 1], F32)) for i in range(2)]
    absg = ph1.enter_context(SB("absg", [128, 1], F32))
    xm_sb = ph1.enter_context(SB("xm_sb", [128, KD, NT], F32))
    rowf = [ph1.enter_context(SB(f"rowf{i}", [1, NT], F32)) for i in range(2)]
    rows3 = ph1.enter_context(SB("rows3", [1, 3 * NT], BF16))
    qsum = ph1.enter_context(SB("qsum", [H, 1], F32))
    qmax = ph1.enter_context(SB("qmax", [H, 1], F32))
    qrec = ph1.enter_context(SB("qrec", [H, 1], F32))
    qsums = ph1.enter_context(SB("qsums", [H, TPB], F32))
    negc = ph1.enter_context(SB("negc", [H, 1], F32))
    biasc = ph1.enter_context(SB("biasc", [H, 1], F32))
    qrh = ph1.enter_context(SB("qrh", [H, 1], BF16))
    qrl = ph1.enter_context(SB("qrl", [H, 1], BF16))

    ps1 = ExitStack()
    ps_a = ps1.enter_context(PS("ps_a", [128, NT], F32))
    ps_b = ps1.enter_context(PS("ps_b", [128, NT], F32))
    ps_c = ps1.enter_context(PS("ps_c", [128, NT], F32))
    ps_d = ps1.enter_context(PS("ps_d", [128, NT], F32))
    ps_e1 = ps1.enter_context(PS("ps_e1", [128, NT], F32))
    ps_e2 = ps1.enter_context(PS("ps_e2", [128, NT], F32))
    ps_f = ps1.enter_context(PS("ps_f", [128, NT], F32))
    ps_g2 = ps1.enter_context(PS("ps_g2", [128, NT], F32))

    E("vector", lambda e: e.memset(absg[:, :], 0.0), inc="v")

    g_h_t = [None, None]
    g_ab = [None, None, None, None]
    g_cd_t = None
    g_mq_t = None
    g_mk_t = None
    g_psf_v = None
    g_psg2_v = None
    g_psg2_s = None
    g_xm_g = None
    g_xm_cons = None
    g_te = [None, None]
    g_xb = [None, None, None, None]
    g_x2 = [None, None, None, None]
    g_scq = [None, None]
    g_xcg = [None, None]
    g_redg = [None, None]
    g_rows3 = [None, None]
    x_cons = [[], []]
    pool_st = {}
    mk_ep = [None] * TPB
    t_wka_fresh = t_w_ld["wka"]
    t_wt_fresh = [t_w_ld["wt"], t_w_ld["wt"]]

    def rows_math(wtok):
        E("vector", lambda e: e.tensor_copy(out=rowf[0][:, :], in_=ps_c[0:1, :]),
          waits=[wtok], inc="v")
        E("vector", lambda e: e.tensor_copy(out=rows3[:, 2 * NT:3 * NT],
                                            in_=rowf[0][:, :]),
          waits=[g_rows3[0], g_rows3[1]], inc="v")
        E("vector", lambda e: e.tensor_mul(rowf[0][:, :], rowf[0][:, :],
                                           rowf[0][:, :]), inc="v")
        tv = E("vector", lambda e: e.tensor_sub(rowf[1][:, :], ps_d[0:1, :],
                                                rowf[0][:, :]), inc="v")
        ts_ = E("scalar", lambda e: e.activation(
            out=rowf[1][:, :], in_=rowf[1][:, :], func=AF.Ln, bias=EPS),
            waits=[tv], inc="s")
        ts_ = E("scalar", lambda e: e.activation(
            out=rowf[1][:, :], in_=rowf[1][:, :], func=AF.Exp, scale=-0.5),
            waits=[ts_], inc="s")
        E("vector", lambda e: e.tensor_copy(out=rows3[:, 0:NT], in_=rowf[1][:, :]),
          waits=[ts_], inc="v")
        E("vector", lambda e: e.tensor_sub(rows3[:, NT:2 * NT], rowf[1][:, :],
                                           rows3[:, 0:NT]), inc="v")
        return p.tok("v")

    def bcasts(t_rows):
        def mmb(e):
            e.matmul(ps_d[:, :], ones_row[:, :], rows3[:, 2 * NT:3 * NT],
                     start=True, stop=True)
            e.matmul(ps_c[:, :], ones_row[:, :], rows3[:, 0:NT], start=True,
                     stop=False)
            return e.matmul(ps_c[:, :], ones_row[:, :], rows3[:, NT:2 * NT],
                            start=False, stop=True)
        tb = E("tensor", mmb, waits=[t_rows], inc="t")
        g_rows3[0] = tb
        return tb

    def mk_chain(t, kind):
        b = t // TPB
        xb = t % 2
        ST = {}

        def s_load():
            ST["tld"] = E("sync", lambda e, t=t, xb=xb: e.dma_start(
                out=x_sb[xb][:, :, :],
                in_=x_in[t].rearrange("p (c n) -> p c n", c=KD)),
                waits=list(x_cons[xb]), inc="ld", n=16)
            x_cons[xb] = []

        def s_stats():
            nonlocal g_cd_t
            src = x_sb[xb] if kind == "s1" else xm_sb
            wt0 = ST["tld"] if kind == "s1" else ST["t_src"]
            tm = None
            for c in range(KD):
                cb4 = c % 4
                if c % 2 == 0:
                    tc_ = E("gpsimd", lambda e, c=c, cb4=cb4, src=src:
                            e.tensor_copy(
                                out=xb_sc[cb4][:, :], in_=src[:, c, :]),
                            waits=[wt0, g_xb[cb4]], inc="g")
                else:
                    tc_ = E("scalar", lambda e, c=c, cb4=cb4, src=src:
                            e.activation(
                                out=xb_sc[cb4][:, :], in_=src[:, c, :],
                                func=AF.Identity),
                            waits=[wt0, g_xb[cb4]], inc="s")
                tq_ = E("scalar", lambda e, c=c, cb4=cb4, src=src: e.activation(
                    out=x2_sc[cb4][:, :], in_=src[:, c, :], func=AF.Square),
                    waits=[wt0, g_x2[cb4]], inc="s")

                def mmst(e, c=c, cb4=cb4):
                    e.matmul(ps_c[0:1, :], ones_col[:, :], xb_sc[cb4][:, :],
                             start=(c == 0), stop=(c == KD - 1))
                    return e.matmul(ps_d[0:1, :], ones_col[:, :],
                                    x2_sc[cb4][:, :],
                                    start=(c == 0), stop=(c == KD - 1))
                tm = E("tensor", mmst, waits=[tc_, tq_, g_cd_t], inc="t")
                g_xb[cb4] = tm
                g_x2[cb4] = tm
            ST["t_cp"] = p.tok("g")
            ST["t_sq"] = p.tok("s")
            ST["tm"] = tm

        def s_rows():
            ST["t_rows"] = rows_math(ST["tm"])

        def s_bcast():
            ST["tb"] = bcasts(ST["t_rows"])
            if kind == "s3":
                ST["t_r2st"] = E("gpsimd", lambda e, t=t: e.dma_start(
                    out=rows2_hbm[t].rearrange("a n -> (a n)").rearrange(
                        "(a n) -> a n", a=1), in_=rows3[:, :]),
                    waits=[ST["t_rows"]], inc="gd", n=16)
                g_rows3[1] = ST["t_r2st"]

        def mk_ln(c):
            def s_ln():
                nonlocal g_cd_t
                cp = c % 2
                tC = MT_CA if kind == "s1" else MT_CB
                tS = MT_SA if kind == "s1" else MT_SB
                src = x_sb[xb] if kind == "s1" else xm_sb
                E("vector", lambda e, c=c, cp=cp, src=src: e.tensor_sub(
                    xcp[cp][:, :], src[:, c, :], ps_d[:, :]),
                    waits=[ST["tb"], g_xcg[cp]], inc="v")
                E("vector", lambda e, c=c, cp=cp, b=b, tC=tC:
                    e.scalar_tensor_tensor(
                        out=xcp[cp][:, :], in0=xcp[cp][:, :],
                        scalar=modc(tC, c, b), in1=ps_c[:, :],
                        op0=AL.mult, op1=AL.mult), inc="v")
                if kind == "s1":
                    E("vector", lambda e, c=c, cp=cp, b=b, xb=xb, tS=tS:
                        e.tensor_scalar(
                            out=h_sb[xb][:, c, :], in0=xcp[cp][:, :],
                            scalar1=modc(tS, c, b), scalar2=None,
                            op0=AL.add, op1=AL.bypass),
                        waits=[g_h_t[xb]], inc="v")
                    g_h_t[xb] = None
                else:
                    E("vector", lambda e, c=c, cp=cp, b=b, tS=tS:
                        e.tensor_scalar(
                            out=xcp[cp][:, :], in0=xcp[cp][:, :],
                            scalar1=modc(tS, c, b), scalar2=None,
                            op0=AL.add, op1=AL.bypass), inc="v")
                    E("vector", lambda e, cp=cp: e.tensor_reduce(
                        out=redg_sc[cp][:, :], in_=xcp[cp][:, :], axis=AX,
                        op=AL.max, apply_absolute_value=True), inc="v")
                    E("vector", lambda e, cp=cp: e.tensor_max(
                        absg[:, :], absg[:, :], redg_sc[cp][:, :]), inc="v")
                    g_xcg[cp] = p.tok("v")
                if c == KD - 1:
                    ST["t_ln"] = p.tok("v")
                    if kind == "s1":
                        x_cons[xb] = [p.tok("v"), ST["t_cp"], ST["t_sq"]]
                        ST["t_h"] = p.tok("v")
                    g_cd_t = p.tok("v")
            return s_ln

        steps = []
        if kind == "s1":
            steps.append(s_load)
        steps.append(s_stats)
        steps.append(s_rows)
        steps.append(s_bcast)
        for c in range(KD):
            steps.append(mk_ln(c))
        return ST, steps

    pool_guard = [None, None]

    def pool_scores(q, src_mm, sums, bias8, tq_mm):
        nonlocal g_psg2_s, g_psf_v
        qp = q % 2
        qsl = slice(q * NT, (q + 1) * NT)
        if q == 0:
            E("vector", lambda e, bias8=bias8: e.tensor_scalar(
                out=sc2[0][:, :], in0=ps_g2[0:H, :], scalar1=SCL,
                scalar2=bias8[:, :], op0=AL.mult, op1=AL.add),
                waits=[tq_mm, g_scq[0]], inc="v")
            E("vector", lambda e: e.tensor_reduce(
                out=qmax[:, :], in_=sc2[0][:, :], axis=AX, op=AL.max),
                waits=[p.tok("v")], inc="v")
            E("vector", lambda e: e.tensor_scalar(
                out=negc[:, :], in0=qmax[:, :], scalar1=-1.0, scalar2=None,
                op0=AL.mult, op1=AL.bypass), waits=[p.tok("v")], inc="v")
            tbc = E("vector", lambda e, bias8=bias8: e.tensor_add(
                biasc[:, :], bias8[:, :], negc[:, :]),
                waits=[p.tok("v")], inc="v")
            pool_st["biasc"] = tbc
            tse = E("scalar", lambda e, sums=sums: e.activation(
                out=sc2[0][:, :], in_=sc2[0][:, :], func=AF.Exp,
                bias=negc[:, :], accum_out=sums[:, 0:1]), waits=[tbc], inc="s")
        else:
            tse = E("scalar", lambda e, q=q, qp=qp, sums=sums: e.activation(
                out=sc2[qp][:, :], in_=ps_g2[0:H, :], func=AF.Exp,
                scale=SCL, bias=biasc[:, :], accum_out=sums[:, q:q + 1]),
                waits=[tq_mm, pool_st["biasc"], g_scq[qp]], inc="s")
        g_psg2_s = tse
        tmx = None
        for c in range(KD):
            pb = [ps_f, ps_g2][c % 2]

            def mmx(e, c=c, qp=qp, pb=pb):
                return e.matmul(pb[:, :], emaskT[:, c * 128:(c + 1) * 128],
                                sc2[qp][:, :], start=True, stop=True)
            tmx = E("tensor", mmx, waits=[tse, pool_guard[c % 2]], inc="t")
            pool_guard[c % 2] = E(
                "vector", lambda e, c=c, q=q, qsl=qsl, src_mm=src_mm, pb=pb:
                e.scalar_tensor_tensor(
                    out=xcp[0][:, :], in0=src_mm[:, c, qsl], scalar=1.0,
                    in1=pb[:, :], op0=AL.mult, op1=AL.mult,
                    accum_out=pparts[:, c * TPB + q:c * TPB + q + 1]),
                waits=[tmx, g_xcg[0]], inc="v")
        g_psf_v = pool_guard[0]
        g_scq[qp] = tmx

    def pool_final(dst, sums):
        nonlocal g_psg2_v
        p.strict_v = True
        E("vector", lambda e, sums=sums: e.tensor_reduce(
            out=qsum[:, :], in_=sums[:, :], axis=AX, op=AL.add),
            waits=[p.tok("s"), p.tok("v")], inc="v")
        E("vector", lambda e: e.reciprocal(qrec[:, :], qsum[:, :]), inc="v")
        E("vector", lambda e: e.tensor_copy(out=qrh[:, :], in_=qrec[:, :]),
          inc="v")
        tvl = E("vector", lambda e: e.tensor_sub(qrl[:, :], qrec[:, :],
                                                 qrh[:, :]), inc="v")
        fin_guard = [None, None]
        for c in range(KD):
            pb = [ps_g2, ps_f][c % 2]

            def mmr(e, c=c, pb=pb):
                e.matmul(pb[:, 0:1], emaskT[:, c * 128:(c + 1) * 128],
                         qrh[:, :], start=True, stop=False)
                return e.matmul(pb[:, 0:1], emaskT[:, c * 128:(c + 1) * 128],
                                qrl[:, :], start=False, stop=True)
            tmr = E("tensor", mmr,
                    waits=([tvl, p.tok("s"), p.tok("v")] if c < 2 else
                           [tvl, fin_guard[c % 2]]), inc="t")
            E("vector", lambda e, c=c, dst=dst: e.tensor_reduce(
                out=dst[:, c:c + 1], in_=pparts[:, c * TPB:(c + 1) * TPB],
                axis=AX, op=AL.add), waits=[p.tok("v")], inc="v")
            fin_guard[c % 2] = E(
                "vector", lambda e, c=c, dst=dst, pb=pb: e.tensor_mul(
                    dst[:, c:c + 1], dst[:, c:c + 1], pb[:, 0:1]),
                waits=[tmr, p.tok("v")], inc="v")
        E("vector", None, waits=[p.tok("v")])
        p.strict_v = False
        g_psg2_v = p.tok("v")
        return p.tok("v")

    def run_steps(steps, n):
        for _ in range(n):
            if steps:
                steps.pop(0)()

    ST0, steps0 = mk_chain(0, "s1")
    for s in steps0:
        s()
    cur = ST0

    for b in range(B):
        for q in range(TPB):
            t = b * TPB + q
            xb = t % 2
            nsl = slice(q * NT, (q + 1) * NT)
            t_h = cur["t_h"]
            if t + 1 < T and q < TPB - 1:
                nxt, nsteps = mk_chain(t + 1, "s1")
                run_steps(nsteps, 1)
            else:
                nxt, nsteps = None, []
            gi = 0
            for (wsb, bcol, dst, wkey) in ((wq_sb, bqc, mq_sb, "wq"),
                                           (wk_sb, bkc, mk_sb, "wk")):
                for mc in range(KD):
                    bank = [ps_a, ps_b, ps_e1, ps_e2][gi % 4]

                    def mmp(e, wsb=wsb, mc=mc, bank=bank, xb=xb):
                        last = None
                        for kc in range(KD):
                            last = e.matmul(
                                bank[:, :], wsb[:, kc, mc * 128:(mc + 1) * 128],
                                h_sb[xb][:, kc, :], start=(kc == 0),
                                stop=(kc == KD - 1))
                        return last
                    tm = E("tensor", mmp,
                           waits=[t_h, t_w_ld[wkey], g_ab[gi % 4]], inc="t")
                    wmqk = g_mq_t if dst is mq_sb else g_mk_t
                    if gi % 2 == 0:
                        tv = E("scalar",
                               lambda e, dst=dst, bcol=bcol, mc=mc, bank=bank,
                               nsl=nsl: e.activation(
                                   out=dst[:, mc, nsl], in_=bank[:, :],
                                   func=AF.Identity, bias=bcol[:, mc:mc + 1]),
                               waits=[tm, wmqk], inc="s")
                    else:
                        tv = E("vector",
                               lambda e, dst=dst, bcol=bcol, mc=mc, bank=bank,
                               nsl=nsl: e.tensor_scalar(
                                   out=dst[:, mc, nsl], in0=bank[:, :],
                                   scalar1=bcol[:, mc:mc + 1], scalar2=None,
                                   op0=AL.add, op1=AL.bypass),
                               waits=[tm, wmqk], inc="v")
                    g_ab[gi % 4] = tv
                    gi += 1
                    if gi >= 3:
                        run_steps(nsteps, 1)
            run_steps(nsteps, 99)
            t_mqmk = (p.tok("v"), p.tok("s"))
            mk_ep[q] = t_mqmk

            def mmq(e, nsl=nsl):
                last = None
                for kc in range(KD):
                    last = e.matmul(ps_g2[0:H, :], wqa_sb[:, kc, :],
                                    mq_sb[:, kc, nsl], start=(kc == 0),
                                    stop=(kc == KD - 1))
                return last
            tq = E("tensor", mmq, waits=[t_mqmk[0], t_mqmk[1], t_w_ld["wqa"],
                                         g_psg2_s, g_psg2_v, pool_guard[1]],
                   inc="t")
            pool_scores(q, mq_sb, qsums, bqa8, tq)
            g_h_t[xb] = p.tok("t")
            if nxt is not None:
                cur = nxt
        g_mq_t = None
        g_mk_t = None

        t_pq = pool_final(pqc, qsums)
        g_mq_t = p.tok("t")
        tka = None
        for c in range(KD):
            tka = E("vector", lambda e, c=c: e.tensor_scalar(
                out=wka_sb[:, c, :], in0=wka_sb[:, c, :],
                scalar1=pqc[:, c:c + 1], scalar2=None,
                op0=AL.mult, op1=AL.bypass),
                waits=[t_pq, t_wka_fresh] if c == 0 else [], inc="v")
        t_kafold = tka

        tk_last = None
        for q in range(TPB):
            qsl = slice(q * NT, (q + 1) * NT)

            def mmk(e, qsl=qsl):
                last = None
                for kc in range(KD):
                    last = e.matmul(ps_g2[0:H, :], wka_sb[:, kc, :],
                                    mk_sb[:, kc, qsl], start=(kc == 0),
                                    stop=(kc == KD - 1))
                return last
            tk = E("tensor", mmk, waits=[t_kafold, mk_ep[q][0], mk_ep[q][1],
                                         g_psg2_s, g_psg2_v, pool_guard[1]],
                   inc="t")
            tk_last = tk
            pool_scores(q, mk_sb, qsums, bka8, tk)

        t_pk = pool_final(pkc, qsums)
        g_mk_t = p.tok("t")
        tpk2 = E("vector", lambda e: e.tensor_mul(
            pkc[:, :], pkc[:, :], pqc[:, :]), waits=[t_pk], inc="v")
        tfv = None
        for c in range(KD):
            tfv = E("vector", lambda e, c=c: e.tensor_scalar(
                out=wt_sb[:, c, :], in0=wt_sb[:, c, :],
                scalar1=pkc[:, c:c + 1], scalar2=None,
                op0=AL.mult, op1=AL.bypass),
                waits=[t_wt_fresh[0]] if c == 0 else [], inc="v")
        t_tfold = (tfv, tfv)
        if b + 1 < B:
            t_wka_fresh = E("gpsimd", lambda e: e.dma_start(
                out=wka_sb[:, :, :], in_=wka_in[:, :, :]),
                waits=[tk_last], inc="gd", n=16)

        s3_pend = []
        for q in range(TPB):
            t = b * TPB + q
            xb = t % 2
            nsl = slice(q * NT, (q + 1) * NT)
            tld = E("sync", lambda e, t=t, xb=xb: e.dma_start(
                out=x_sb[xb][:, :, :],
                in_=x_in[t].rearrange("p (c n) -> p c n", c=KD)),
                waits=list(x_cons[xb]), inc="ld", n=16)
            x_cons[xb] = []
            for mc in range(KD):
                bank = [ps_a, ps_b, ps_e1, ps_e2][mc % 4]

                def mmo(e, mc=mc, bank=bank, nsl=nsl):
                    for kc in range(KD):
                        e.matmul(
                            bank[:, :], wt_sb[:, kc, mc * 128:(mc + 1) * 128],
                            mq_sb[:, kc, nsl], start=(kc == 0), stop=False)
                    return e.matmul(bank[:, :], id_bf[:, :], mq_sb[:, mc, nsl],
                                    start=False, stop=True)
                tm = E("tensor", mmo, waits=[t_tfold[0], t_tfold[1],
                                             g_ab[mc % 4]], inc="t")
                tsc = E("scalar", lambda e, mc=mc, bank=bank, b=b: e.activation(
                    out=te_sc[mc % 2][:, :], in_=bank[:, :], func=AF.Identity,
                    scale=modc(MT_GA, mc, b), bias=gabt[:, b, mc:mc + 1]),
                    waits=[tm, g_te[mc % 2]], inc="s")
                g_ab[mc % 4] = tsc
                run_steps(s3_pend, 3)
                tv1 = E("vector", lambda e, mc=mc, xb=xb: e.tensor_add(
                    xm_sb[:, mc, :], te_sc[mc % 2][:, :], x_sb[xb][:, mc, :]),
                    waits=[tsc, tld, g_xm_g, g_xm_cons], inc="v")
                g_te[mc % 2] = tv1
                g_xm_g = None
            t_xm = p.tok("v")
            x_cons[xb] = [t_xm]
            t_st = E("gpsimd", lambda e, t=t: e.dma_start(
                out=xm_hbm[t, :, :, :], in_=xm_sb[:, :, :]),
                waits=[t_xm], inc="gd", n=16)
            g_xm_g = t_st
            run_steps(s3_pend, 99)
            ST3, steps3 = mk_chain(t, "s3")
            ST3["t_src"] = t_xm
            run_steps(steps3, 1)
            g_xm_cons = ST3["t_cp"]
            s3_pend = steps3
        run_steps(s3_pend, 99)
        if b + 1 < B:
            cur, cb_pend = mk_chain((b + 1) * TPB, "s1")
            run_steps(cb_pend, 99)
        if b + 1 < B:
            t_wt_g = E("gpsimd", lambda e: e.dma_start(
                out=wt_sb[:, :, :], in_=wt_in[:, :, :]),
                waits=[p.tok("t")], inc="gd", n=16)
            t_wt_fresh = [t_wt_g, t_wt_g]

    t_ph1_v = p.tok("v")

    t_ph1_g = p.tok("g")
    ttm = E("tensor", lambda e: e.transpose(
        ps_f[0:1, 0:128], absg[:, :], id_f[:, :]),
        waits=[t_ph1_v, t_ph1_g, g_psf_v], inc="t")
    tv = E("vector", lambda e: e.tensor_reduce(
        out=rowf[0][:, 0:1], in_=ps_f[0:1, 0:128], axis=AX, op=AL.max),
        waits=[ttm], inc="v")
    tg = E("gpsimd", lambda e: e.dma_start(
        out=cc1_i.rearrange("(a x) -> a x", a=1), in_=rowf[0][:, 0:1]),
        waits=[tv], inc="gd", n=16)
    E("gpsimd", lambda e: e.collective_compute(
        "AllReduce", AL.max, replica_groups=[list(range(8))],
        ins=[cc1_i.opt()], outs=[cc1_o.opt()]), waits=[tg], inc="cc")

    ps1.close()
    ph1.close()
    watt.close()
    p.barrier(owner)

    ph2 = ExitStack()
    wgT = ph2.enter_context(SB("wgT", [128, KD, I], BF16))
    wuT = ph2.enter_context(SB("wuT", [128, KD, I], BF16))
    wst = [ph2.enter_context(SB(f"wst{i}", [128, D], F32)) for i in range(4)]
    wsq = [ph2.enter_context(SB(f"wsq{i}", [128, D], BF16)) for i in range(4)]
    wrow4 = ph2.enter_context(SB("wrow4", [128, 4], F32))
    xm2_sb = [ph2.enter_context(SB(f"xm2_{i}", [128, KD, NT], F32)) for i in range(2)]
    xq_sb = [ph2.enter_context(SB(f"xq{i}", [128, KD, NT], BF16)) for i in range(2)]
    mid_sb = ph2.enter_context(SB("mid_sb", [128, 4, NT], F32))
    r2_sb = [ph2.enter_context(SB(f"r2_{i}", [1, 3 * NT], BF16)) for i in range(2)]
    q_sc = [ph2.enter_context(SB(f"q_sc{i}", [128, NT], F32)) for i in range(2)]
    sil_sc = [ph2.enter_context(SB(f"sil{i}", [128, NT], F32)) for i in range(2)]
    red2 = ph2.enter_context(SB("red2", [128, 1], F32))
    gm1row = ph2.enter_context(SB("gm1row", [1, 4], F32))
    gm1b = ph2.enter_context(SB("gm1b", [1, 4], BF16))

    ps2 = ExitStack()
    p2_a = ps2.enter_context(PS("p2_a", [128, NT], F32))
    p2_b = ps2.enter_context(PS("p2_b", [128, NT], F32))
    p2_c = ps2.enter_context(PS("p2_c", [128, NT], F32))
    p2_d = ps2.enter_context(PS("p2_d", [128, NT], F32))
    p2_e = ps2.enter_context(PS("p2_e", [128, NT], F32))
    p2_f = ps2.enter_context(PS("p2_f", [128, NT], F32))
    ps2pre = ExitStack()
    p2_g = ps2pre.enter_context(PS("p2_g", [128, D], BF16))
    p2_h = ps2pre.enter_context(PS("p2_h", [128, 128], F32))

    WQQ = [("sync", "ld"), ("scalar", "sd"), ("gpsimd", "gd")]
    g_wst = [None, None, None, None]
    g_p2g_v = None

    def quant_weight(src_ap, n_chunks, dstT, scol, q0):
        nonlocal g_p2g_v
        p.strict_v = True
        for ci in range(n_chunks):
            sb_ = ci % 4
            eng, dsem = WQQ[(q0 + ci) % 3]
            tld = E(eng, lambda e, ci=ci, sb_=sb_, src_ap=src_ap: e.dma_start(
                out=wst[sb_][:, :], in_=src_ap[ci, :, :]),
                waits=[g_wst[sb_]], inc=dsem, n=16)
            E("vector", lambda e, sb_=sb_: e.tensor_reduce(
                out=wrow4[:, sb_:sb_ + 1], in_=wst[sb_][:, :], axis=AX,
                op=AL.max, apply_absolute_value=True),
                waits=[tld, g_wst[sb_]], inc="v")
            E("vector", lambda e, ci=ci, sb_=sb_, scol=scol: e.tensor_scalar(
                out=scol[:, ci:ci + 1], in0=wrow4[:, sb_:sb_ + 1],
                scalar1=1.0 / 127.0, scalar2=None,
                op0=AL.mult, op1=AL.bypass), inc="v")
            E("vector", lambda e, ci=ci, sb_=sb_, scol=scol: e.reciprocal(
                wrow4[:, sb_:sb_ + 1], scol[:, ci:ci + 1]), inc="v")
            E("scalar", lambda e, sb_=sb_: e.activation(
                out=wst[sb_][:, :], in_=wst[sb_][:, :], func=AF.Identity,
                scale=wrow4[:, sb_:sb_ + 1], bias=MAGIC),
                waits=[p.tok("v")], inc="s")
            tq = E("scalar", lambda e, sb_=sb_: e.activation(
                out=wsq[sb_][:, :], in_=wst[sb_][:, :], func=AF.Identity,
                bias=-MAGIC), inc="s")
            g_wst[sb_] = tq

            def tpw(e, sb_=sb_):
                last = None
                for kc in range(KD):
                    last = e.transpose(p2_g[:, kc * 128:(kc + 1) * 128],
                                       wsq[sb_][:, kc * 128:(kc + 1) * 128],
                                       id_bf[:, :])
                return last
            ttw = E("tensor", tpw, waits=[tq, g_p2g_v], inc="t")
            g_p2g_v = E("vector", lambda e, ci=ci, dstT=dstT: e.tensor_copy(
                out=dstT[:, :, ci * 128:(ci + 1) * 128], in_=p2_g[:, :]),
                waits=[ttw], inc="v")

    quant_weight(wg_in, KI, wgT, sgc, 0)
    quant_weight(wu_in, KI, wuT, suc, KI)
    p.strict_v = False
    t_wq_done = p.tok("v")

    tg = E("gpsimd", lambda e: e.dma_start(
        out=gm1row[:, 0:1], in_=cc1_o.rearrange("(a x) -> a x", a=1)),
        waits=[p.tok("cc")], inc="gd", n=16)
    p.strict_v = True
    ts_ = E("scalar", lambda e: e.activation(
        out=gm1row[:, 1:2], in_=clipx_sb[:, :], func=AF.Sigmoid),
        waits=[tg], inc="s")
    E("vector", lambda e: e.tensor_mul(
        gm1row[:, 0:1], gm1row[:, 0:1], gm1row[:, 1:2]),
        waits=[tg, ts_], inc="v")
    E("vector", lambda e: e.tensor_scalar(
        out=gm1row[:, 0:1], in0=gm1row[:, 0:1], scalar1=1.0 / 127.0,
        scalar2=None, op0=AL.mult, op1=AL.bypass), inc="v")
    E("vector", lambda e: e.reciprocal(gm1row[:, 1:2], gm1row[:, 0:1]), inc="v")
    E("vector", lambda e: e.tensor_copy(out=gm1b[:, 0:1], in_=gm1row[:, 0:1]),
      inc="v")
    E("vector", lambda e: e.tensor_sub(gm1b[:, 1:2], gm1row[:, 0:1],
                                       gm1b[:, 0:1]), inc="v")
    E("vector", lambda e: e.tensor_copy(out=gm1b[:, 2:3], in_=gm1row[:, 1:2]),
      inc="v")
    tv = E("vector", lambda e: e.tensor_sub(
        gm1b[:, 3:4], gm1row[:, 1:2], gm1b[:, 2:3]), inc="v")

    def bsc(e):
        e.matmul(p2_h[:, 0:1], ones_row[:, :], gm1b[:, 0:1], start=True, stop=False)
        e.matmul(p2_h[:, 0:1], ones_row[:, :], gm1b[:, 1:2], start=False, stop=True)
        e.matmul(p2_h[:, 1:2], ones_row[:, :], gm1b[:, 2:3], start=True, stop=False)
        return e.matmul(p2_h[:, 1:2], ones_row[:, :], gm1b[:, 3:4], start=False,
                        stop=True)
    tb = E("tensor", bsc, waits=[tv], inc="t")
    E("vector", lambda e: e.tensor_copy(out=s1col[:, :], in_=p2_h[:, 0:1]),
      waits=[tb], inc="v")
    tv = E("vector", lambda e: e.tensor_copy(out=is1col[:, :], in_=p2_h[:, 1:2]),
           inc="v")
    E("vector", lambda e: e.tensor_scalar(
        out=sg1[:, :], in0=sgc[:, :], scalar1=s1col[:, :], scalar2=None,
        op0=AL.mult, op1=AL.bypass), waits=[tv, t_wq_done], inc="v")
    E("vector", lambda e: e.tensor_scalar(
        out=su1[:, :], in0=suc[:, :], scalar1=s1col[:, :], scalar2=None,
        op0=AL.mult, op1=AL.bypass), inc="v")
    E("vector", lambda e: e.tensor_scalar(
        out=c2s1[:, :], in0=rng_(MT_CB), scalar1=is1col[:, :], scalar2=None,
        op0=AL.mult, op1=AL.bypass), inc="v")
    t_sc1 = E("vector", lambda e: e.tensor_scalar(
        out=d2s1[:, :], in0=rng_(MT_SB), scalar1=is1col[:, :], scalar2=None,
        op0=AL.mult, op1=AL.bypass), inc="v")
    p.strict_v = False

    ps2pre.close()
    bc_e = [p2_e, ps2.enter_context(PS("p2_e1", [128, NT], F32))]
    bc_f = [p2_f, ps2.enter_context(PS("p2_f1", [128, NT], F32))]

    g_xm2 = [None, None]
    g_xq_t = [None, None]
    g_r2 = [None, None]
    g_bct = [t_sc1, t_sc1]
    g_ab2 = [None, None]
    g_cd2 = [None, None]
    g_sil = [None, None]
    g_mid = [None, None]
    g_qsc = [None, None]
    t_xq_tile = [None] * (T + 1)
    tb_tile = [None] * (T + 1)

    def ph2_loads(t):
        xb = t % 2
        E("sync", lambda e, t=t, xb=xb: e.dma_start(
            out=xm2_sb[xb][:, :, :], in_=xm_hbm[t, :, :, :]),
            waits=[g_xm2[xb]], inc="ld", n=16)
        E("sync", lambda e, t=t, xb=xb: e.dma_start(
            out=r2_sb[xb][:, :],
            in_=rows2_hbm[t].rearrange("a n -> (a n)").rearrange(
                "(a n) -> a n", a=1)),
            waits=[g_r2[xb]], inc="ld", n=16)
        return p.tok("ld")

    def ph2_bcast(t, t_lr):
        xb = t % 2
        pe_, pf_ = bc_e[t % 2], bc_f[t % 2]

        def mmb2(e, xb=xb, pe_=pe_, pf_=pf_):
            e.matmul(pf_[:, :], ones_row[:, :], r2_sb[xb][:, 2 * NT:3 * NT],
                     start=True, stop=True)
            e.matmul(pe_[:, :], ones_row[:, :], r2_sb[xb][:, 0:NT], start=True,
                     stop=False)
            return e.matmul(pe_[:, :], ones_row[:, :], r2_sb[xb][:, NT:2 * NT],
                            start=False, stop=True)
        tb = E("tensor", mmb2, waits=[t_lr, g_bct[t % 2], t_sc1], inc="t")
        g_r2[xb] = tb
        tb_tile[t] = tb

    def ph2_qchunk(t, c):
        b = t // TPB
        xb = t % 2
        qp = c % 2
        pe_, pf_ = bc_e[t % 2], bc_f[t % 2]
        E("vector", lambda e, c=c, xb=xb, qp=qp, pf_=pf_: e.tensor_sub(
            q_sc[qp][:, :], xm2_sb[xb][:, c, :], pf_[:, :]),
            waits=[tb_tile[t], g_qsc[qp]], inc="v")
        E("vector", lambda e, c=c, b=b, qp=qp, pe_=pe_: e.scalar_tensor_tensor(
            out=q_sc[qp][:, :], in0=q_sc[qp][:, :],
            scalar=c2s1[:, c * B + b:c * B + b + 1],
            in1=pe_[:, :], op0=AL.mult, op1=AL.mult), inc="v")
        E("vector", lambda e, c=c, b=b, qp=qp: e.tensor_scalar(
            out=q_sc[qp][:, :], in0=q_sc[qp][:, :],
            scalar1=d2s1[:, c * B + b:c * B + b + 1],
            scalar2=MAGIC, op0=AL.add, op1=AL.add), inc="v")
        E("vector", lambda e, qp=qp: e.tensor_scalar(
            out=q_sc[qp][:, :], in0=q_sc[qp][:, :], scalar1=MAGIC,
            scalar2=-128.0, op0=AL.subtract, op1=AL.max), inc="v")
        tvq = E("vector", lambda e, c=c, xb=xb, qp=qp: e.tensor_scalar(
            out=xq_sb[xb][:, c, :], in0=q_sc[qp][:, :], scalar1=127.0,
            scalar2=None, op0=AL.min, op1=AL.bypass),
            waits=[g_xq_t[xb]], inc="v")
        g_qsc[qp] = tvq
        g_xq_t[xb] = None
        if c == KD - 1:
            t_xq_tile[t] = tvq
            g_xm2[xb] = tvq
            g_bct[t % 2] = tvq

    def ph2_group(t, mc):
        xb = t % 2
        bga = [p2_a, p2_b][mc % 2]
        bup = [p2_c, p2_d][mc % 2]

        def mmgu(e, mc=mc, xb=xb, bga=bga, bup=bup):
            last = None
            for kc in range(KD):
                e.matmul(bga[:, :], wgT[:, kc, mc * 128:(mc + 1) * 128],
                         xq_sb[xb][:, kc, :], start=(kc == 0),
                         stop=(kc == KD - 1))
                last = e.matmul(bup[:, :], wuT[:, kc, mc * 128:(mc + 1) * 128],
                                xq_sb[xb][:, kc, :], start=(kc == 0),
                                stop=(kc == KD - 1))
            return last
        tm = E("tensor", mmgu,
               waits=[t_xq_tile[t], g_ab2[mc % 2], g_cd2[mc % 2]], inc="t")
        tss = E("scalar", lambda e, mc=mc, bga=bga: e.activation(
            out=sil_sc[mc % 2][:, :], in_=bga[:, :], func=AF.Silu,
            scale=sg1[:, mc:mc + 1]), waits=[tm, g_sil[mc % 2]], inc="s")
        g_ab2[mc % 2] = tss
        tvm = E("vector", lambda e, mc=mc, bup=bup: e.scalar_tensor_tensor(
            out=mid_sb[:, mc % 4, :], in0=bup[:, :],
            scalar=su1[:, mc:mc + 1], in1=sil_sc[mc % 2][:, :],
            op0=AL.mult, op1=AL.mult),
            waits=[tss, g_mid[(mc // 2) % 2]], inc="v")
        g_mid[(mc // 2) % 2] = None
        g_cd2[mc % 2] = tvm
        g_sil[mc % 2] = tvm
        E("vector", lambda e, mc=mc: e.tensor_reduce(
            out=red2[:, :], in_=mid_sb[:, mc % 4, :], axis=AX, op=AL.max,
            apply_absolute_value=True), inc="v")
        tvm2 = E("vector", lambda e: e.tensor_max(
            absacc2[:, :], absacc2[:, :], red2[:, :]),
            waits=[p.tok("v")], inc="v")
        if mc % 2 == 1:
            hb = (mc // 2) % 2
            eng, dsem = ("gpsimd", "gd") if hb == 0 else ("scalar", "sd")
            g_mid[hb] = E(eng, lambda e, t=t, mc=mc, hb=hb: e.dma_start(
                out=mid_hbm[t, :, mc - 1:mc + 1, :],
                in_=mid_sb[:, hb * 2:hb * 2 + 2, :]),
                waits=[tvm2], inc=dsem, n=16)

    t_lrn = ph2_loads(0)
    ph2_bcast(0, t_lrn)
    for c in range(KD):
        ph2_qchunk(0, c)
    for t in range(T):
        if t + 1 < T:
            t_lrn = ph2_loads(t + 1)
        for mc in range(KI):
            ph2_group(t, mc)
            if t + 1 < T:
                if mc == 0:
                    ph2_bcast(t + 1, t_lrn)
                elif mc % 2 == 0 and 2 <= mc <= 16:
                    ph2_qchunk(t + 1, mc // 2 - 1)
        g_xq_t[t % 2] = p.tok("t")

    ttm = E("tensor", lambda e: e.transpose(
        bc_e[0][0:1, 0:128], absacc2[:, :], id_f[:, :]),
        waits=[p.tok("v")], inc="t")
    tv = E("vector", lambda e: e.tensor_reduce(
        out=gm1row[:, 0:1], in_=bc_e[0][0:1, 0:128], axis=AX, op=AL.max),
        waits=[ttm], inc="v")
    tg = E("gpsimd", lambda e: e.dma_start(
        out=cc2_i.rearrange("(a x) -> a x", a=1), in_=gm1row[:, 0:1]),
        waits=[tv], inc="gd", n=16)
    E("gpsimd", lambda e: e.collective_compute(
        "AllReduce", AL.max, replica_groups=[list(range(8))],
        ins=[cc2_i.opt()], outs=[cc2_o.opt()]), waits=[tg], inc="cc")

    ps2.close()
    ph2.close()
    p.barrier(owner)

    ph3 = ExitStack()
    wdT = ph3.enter_context(SB("wdT", [128, KI, D], BF16))
    w3st = [ph3.enter_context(SB(f"w3st{i}", [128, I], F32)) for i in range(2)]
    w3sq = [ph3.enter_context(SB(f"w3sq{i}", [128, I], BF16)) for i in range(2)]
    w3row = ph3.enter_context(SB("w3row", [128, 2], F32))
    mid3 = [ph3.enter_context(SB(f"mid3_{i}", [128, 2, NT], F32)) for i in range(4)]
    xq3 = [ph3.enter_context(SB(f"xq3_{i}", [128, KI, NT], BF16)) for i in range(2)]
    xm3 = [ph3.enter_context(SB(f"xm3_{i}", [128, KD, NT], F32)) for i in range(2)]
    q3m = [ph3.enter_context(SB(f"q3m{i}", [128, NT], F32)) for i in range(2)]
    te3 = [ph3.enter_context(SB(f"te3_{i}", [128, NT], F32)) for i in range(2)]
    gm2row = ph3.enter_context(SB("gm2row", [1, 4], F32))
    gm2b = ph3.enter_context(SB("gm2b", [1, 4], BF16))

    ps3 = ExitStack()
    p3_a = ps3.enter_context(PS("p3_a", [128, NT], F32))
    p3_b = ps3.enter_context(PS("p3_b", [128, NT], F32))
    p3_c = ps3.enter_context(PS("p3_c", [128, NT], F32))
    p3_d = ps3.enter_context(PS("p3_d", [128, NT], F32))
    ps3pre = ExitStack()
    p3_g = ps3pre.enter_context(PS("p3_g", [128, 11 * 128], BF16))
    p3_h = ps3pre.enter_context(PS("p3_h", [128, 128], F32))

    g_w3 = [None, None]
    g_p3g_v = None
    p.strict_v = True
    for ci in range(KD):
        sb_ = ci % 2
        eng3, dsem3 = [("sync", "ld"), ("scalar", "sd")][ci % 2]
        tld = E(eng3, lambda e, ci=ci, sb_=sb_: e.dma_start(
            out=w3st[sb_][:, :], in_=wd_in[ci, :, :]), waits=[g_w3[sb_]],
            inc=dsem3, n=16)
        E("vector", lambda e, sb_=sb_: e.tensor_reduce(
            out=w3row[:, sb_:sb_ + 1], in_=w3st[sb_][:, :], axis=AX, op=AL.max,
            apply_absolute_value=True), waits=[tld, g_w3[sb_]], inc="v")
        E("vector", lambda e, ci=ci, sb_=sb_: e.tensor_scalar(
            out=sdc[:, ci:ci + 1], in0=w3row[:, sb_:sb_ + 1],
            scalar1=1.0 / 127.0, scalar2=None,
            op0=AL.mult, op1=AL.bypass), inc="v")
        E("vector", lambda e, ci=ci, sb_=sb_: e.reciprocal(
            w3row[:, sb_:sb_ + 1], sdc[:, ci:ci + 1]), inc="v")
        E("scalar", lambda e, sb_=sb_: e.activation(
            out=w3st[sb_][:, :], in_=w3st[sb_][:, :], func=AF.Identity,
            scale=w3row[:, sb_:sb_ + 1], bias=MAGIC),
            waits=[p.tok("v")], inc="s")
        tq = E("scalar", lambda e, sb_=sb_: e.activation(
            out=w3sq[sb_][:, :], in_=w3st[sb_][:, :], func=AF.Identity,
            bias=-MAGIC), inc="s")
        g_w3[sb_] = tq
        for half in range(2):
            def tpw3(e, sb_=sb_, half=half):
                last = None
                for j in range(11):
                    kc = half * 11 + j
                    last = e.transpose(p3_g[:, j * 128:(j + 1) * 128],
                                       w3sq[sb_][:, kc * 128:(kc + 1) * 128],
                                       id_bf[:, :])
                return last
            ttw = E("tensor", tpw3, waits=[tq, g_p3g_v], inc="t")
            g_p3g_v = E("vector", lambda e, ci=ci, half=half: e.tensor_copy(
                out=wdT[:, half * 11:(half + 1) * 11, ci * 128:(ci + 1) * 128],
                in_=p3_g[:, :]), waits=[ttw], inc="v")
    t_wd_ready = p.tok("v")
    p.strict_v = False

    p.strict_v = True
    tg = E("gpsimd", lambda e: e.dma_start(
        out=gm2row[:, 0:1], in_=cc2_o.rearrange("(a x) -> a x", a=1)),
        waits=[p.tok("cc")], inc="gd", n=16)
    ts_ = E("scalar", lambda e: e.activation(
        out=gm2row[:, 1:2], in_=clipd_sb[:, :], func=AF.Sigmoid),
        waits=[tg], inc="s")
    E("vector", lambda e: e.tensor_mul(
        gm2row[:, 0:1], gm2row[:, 0:1], gm2row[:, 1:2]), waits=[tg, ts_], inc="v")
    E("vector", lambda e: e.tensor_scalar(
        out=gm2row[:, 0:1], in0=gm2row[:, 0:1], scalar1=1.0 / 127.0,
        scalar2=None, op0=AL.mult, op1=AL.bypass), inc="v")
    E("vector", lambda e: e.reciprocal(gm2row[:, 1:2], gm2row[:, 0:1]), inc="v")
    E("vector", lambda e: e.tensor_copy(out=gm2b[:, 0:1], in_=gm2row[:, 0:1]),
      inc="v")
    E("vector", lambda e: e.tensor_sub(gm2b[:, 1:2], gm2row[:, 0:1],
                                       gm2b[:, 0:1]), inc="v")
    E("vector", lambda e: e.tensor_copy(out=gm2b[:, 2:3], in_=gm2row[:, 1:2]),
      inc="v")
    tv = E("vector", lambda e: e.tensor_sub(
        gm2b[:, 3:4], gm2row[:, 1:2], gm2b[:, 2:3]), inc="v")

    def bsc2(e):
        e.matmul(p3_h[:, 0:1], ones_row[:, :], gm2b[:, 0:1], start=True, stop=False)
        e.matmul(p3_h[:, 0:1], ones_row[:, :], gm2b[:, 1:2], start=False, stop=True)
        e.matmul(p3_h[:, 1:2], ones_row[:, :], gm2b[:, 2:3], start=True, stop=False)
        return e.matmul(p3_h[:, 1:2], ones_row[:, :], gm2b[:, 3:4], start=False,
                        stop=True)
    tb = E("tensor", bsc2, waits=[tv], inc="t")
    E("vector", lambda e: e.tensor_copy(out=s2col[:, :], in_=p3_h[:, 0:1]),
      waits=[tb], inc="v")
    tv = E("vector", lambda e: e.tensor_copy(out=is2col[:, :], in_=p3_h[:, 1:2]),
           inc="v")
    p.strict_v = False
    t_is2 = p.tok("v")
    t_s2 = E("vector", lambda e: e.tensor_scalar(
        out=sd2[:, :], in0=sdc[:, :], scalar1=s2col[:, :], scalar2=None,
        op0=AL.mult, op1=AL.bypass), waits=[t_is2, t_wd_ready], inc="v")

    NSL = 11
    M3Q = [("sync", "ld"), ("scalar", "sd"), ("gpsimd", "gd")]
    slot_load = {}
    slot_free = {}
    g_xq3_t = [None, None]
    g_xm3 = [None, None]
    g_ab3 = [None, None, None, None]
    g_te3 = [None, None]
    g_q3m = [None, None]
    t_q3_tile = [None] * T
    ldx_tile = [None] * T

    def emit_mid_load(k):
        t, pr = divmod(k, NSL)
        if t >= T:
            return
        mb = k % 4
        eng3, dsem3 = M3Q[k % 3]
        slot_load[k] = E(eng3, lambda e, t=t, pr=pr, mb=mb: e.dma_start(
            out=mid3[mb][:, :, :], in_=mid_hbm[t, :, pr * 2:pr * 2 + 2, :]),
            waits=[slot_free.get(k - 4)], inc=dsem3, n=16)

    def ph3_xmload(t):
        xb = t % 2
        eng3, dsem3 = [("sync", "ld"), ("scalar", "sd")][t % 2]
        ldx_tile[t] = E(eng3, lambda e, t=t, xb=xb: e.dma_start(
            out=xm3[xb][:, :, :], in_=xm_hbm[t, :, :, :]), waits=[g_xm3[xb]],
            inc=dsem3, n=16)

    def ph3_qchunk(t, pr):
        k = t * NSL + pr
        xb = t % 2
        mb = k % 4
        tld = slot_load[k]
        tsm = None
        for j in range(2):
            kc = pr * 2 + j
            qp = kc % 2
            tsm = E("scalar", lambda e, mb=mb, j=j, qp=qp: e.activation(
                out=q3m[qp][:, :], in_=mid3[mb][:, j, :], func=AF.Identity,
                scale=is2col[:, :], bias=MAGIC),
                waits=[tld, t_is2, g_q3m[qp]], inc="s")
            E("vector", lambda e, qp=qp: e.tensor_scalar(
                out=q3m[qp][:, :], in0=q3m[qp][:, :], scalar1=MAGIC,
                scalar2=-128.0, op0=AL.subtract, op1=AL.max),
                waits=[tsm], inc="v")
            tvq = E("vector", lambda e, kc=kc, xb=xb, qp=qp: e.tensor_scalar(
                out=xq3[xb][:, kc, :], in0=q3m[qp][:, :], scalar1=127.0,
                scalar2=None, op0=AL.min, op1=AL.bypass),
                waits=[g_xq3_t[xb]], inc="v")
            g_q3m[qp] = tvq
            g_xq3_t[xb] = None
        slot_free[k] = tsm
        if pr == NSL - 1:
            t_q3_tile[t] = p.tok("v")
        emit_mid_load(k + 4)

    def ph3_group(t, mc):
        b = t // TPB
        xb = t % 2
        bank = [p3_a, p3_b, p3_c, p3_d][mc % 4]

        def mmd(e, mc=mc, xb=xb, bank=bank):
            last = None
            for kc in range(KI):
                last = e.matmul(bank[:, :], wdT[:, kc, mc * 128:(mc + 1) * 128],
                                xq3[xb][:, kc, :], start=(kc == 0),
                                stop=(kc == KI - 1))
            return last
        tm = E("tensor", mmd,
               waits=[t_q3_tile[t], t_wd_ready, g_ab3[mc % 4]], inc="t")
        tsc = E("scalar", lambda e, mc=mc, bank=bank: e.activation(
            out=te3[mc % 2][:, :], in_=bank[:, :], func=AF.Identity,
            scale=sd2[:, mc:mc + 1]), waits=[tm, t_s2, g_te3[mc % 2]], inc="s")
        g_ab3[mc % 4] = tsc
        tvy = E("vector", lambda e, mc=mc, xb=xb, b=b: e.scalar_tensor_tensor(
            out=xm3[xb][:, mc, :], in0=te3[mc % 2][:, :],
            scalar=modc(MT_GB, mc, b), in1=xm3[xb][:, mc, :],
            op0=AL.mult, op1=AL.add), waits=[tsc, ldx_tile[t]], inc="v")
        g_te3[mc % 2] = tvy

    for k in range(4):
        emit_mid_load(k)
    ph3_xmload(0)
    for pr in range(NSL):
        ph3_qchunk(0, pr)
    for t in range(T):
        if t + 1 < T:
            ph3_xmload(t + 1)
        nxt_pr = 0
        for mc in range(KD):
            ph3_group(t, mc)
            if t + 1 < T:
                tgt = min(NSL, 2 * (mc + 1))
                while nxt_pr < tgt:
                    ph3_qchunk(t + 1, nxt_pr)
                    nxt_pr += 1
        t_y = p.tok("v")
        xb = t % 2
        g_xm3[xb] = E("gpsimd", lambda e, t=t, xb=xb: e.dma_start(
            out=y_out[t, :, :, :], in_=xm3[xb][:, :, :]), waits=[t_y],
            inc="gd", n=16)
        g_xq3_t[xb] = p.tok("t")

    E("gpsimd", None, waits=[p.tok("gd")])
    p.barrier(owner, skip=())

    ps3pre.close()
    ps3.close()
    ph3.close()

    p.finalize()

    def replay(eng, name):
        for ws, fn, inc, n in p.steps[name]:
            for s, v in ws:
                eng.wait_ge(sem[s], v)
            inst = fn(eng) if fn is not None else None
            if inc is not None:
                if inst is None:
                    inst = eng.nop()
                inst.then_inc(sem[inc], n)

    with nc.Block() as block:
        block.sync(lambda e: replay(e, "sync"))
        block.gpsimd(lambda e: replay(e, "gpsimd"))
        block.scalar(lambda e: replay(e, "scalar"))
        block.vector(lambda e: replay(e, "vector"))
        block.tensor(lambda e: replay(e, "tensor"))

    root.close()
    return nc


_NC_CACHE = {}


def _get_nc():
    if "nc" not in _NC_CACHE:
        _NC_CACHE["nc"] = build_nc()
    return _NC_CACHE["nc"]


def kernel(L0_attn_hidden_states, L0_attn_Wq, L0_attn_bq, L0_attn_Wqa,
           L0_attn_bqa, L0_attn_Wk, L0_attn_bk, L0_attn_Wka, L0_attn_bka,
           L0_attn_Wt, L0_attn_bt, L0_ffn_clip_x, L0_ffn_clip_down,
           L0_ffn_w_gate, L0_ffn_w_up, L0_ffn_w_down, L0_c, L0_wmod):
    f32 = np.float32

    def wmat(w):
        M = np.asarray(w).shape[1]
        return np.ascontiguousarray(
            np.asarray(w, f32).reshape(KD, 128, M).transpose(1, 0, 2))

    def bvec(v):
        return np.ascontiguousarray(np.asarray(v, f32).reshape(KD, 128).T)

    x = np.asarray(L0_attn_hidden_states, f32)
    base = dict(
        wq=wmat(L0_attn_Wq), wk=wmat(L0_attn_Wk), wt=wmat(L0_attn_Wt),
        wqa=wmat(L0_attn_Wqa), wka=wmat(L0_attn_Wka),
        bq=bvec(L0_attn_bq), bk=bvec(L0_attn_bk), bt=bvec(L0_attn_bt),
        bqa=np.ascontiguousarray(np.asarray(L0_attn_bqa, f32).reshape(H, 1)),
        bka=np.ascontiguousarray(np.asarray(L0_attn_bka, f32).reshape(H, 1)),
        wg=np.ascontiguousarray(np.asarray(L0_ffn_w_gate, f32).reshape(KI, 128, D)),
        wu=np.ascontiguousarray(np.asarray(L0_ffn_w_up, f32).reshape(KI, 128, D)),
        wd=np.ascontiguousarray(np.asarray(L0_ffn_w_down, f32).reshape(KD, 128, I)),
        wmod=wmat(L0_wmod),
        clipx=np.ascontiguousarray(np.asarray(L0_ffn_clip_x, f32).reshape(1, 1)),
        clipd=np.ascontiguousarray(np.asarray(L0_ffn_clip_down, f32).reshape(1, 1)),
    )
    c_all = np.asarray(L0_c, f32)

    in_maps = []
    for core in range(8):
        xs = x[core * B:(core + 1) * B]
        xh = xs.reshape(B, TPB, NT, KD, 128).transpose(0, 1, 4, 3, 2)
        xh = np.ascontiguousarray(xh).reshape(T, 128, KD * NT)
        ch = c_all[core * B:(core + 1) * B]
        ch = np.ascontiguousarray(ch.reshape(B, KD, 128).transpose(2, 1, 0))
        m = dict(base)
        m["x"] = xh
        m["cmod"] = ch
        in_maps.append(m)

    nc = _get_nc()
    res = run_bass_kernel_spmd(nc, in_maps, core_ids=list(range(8)))

    outs = []
    for core in range(8):
        y = np.asarray(res.results[core]["y"])
        y = y.reshape(B, TPB, 128, KD, NT).transpose(0, 1, 4, 3, 2)
        outs.append(y.reshape(B, 2048, D))
    return np.ascontiguousarray(np.concatenate(outs, axis=0))


# version 19
# speedup vs baseline: 1.1107x; 1.0262x over previous
import numpy as np
from contextlib import ExitStack

import concourse.bass as bass
import concourse.mybir as mybir
from concourse.bass_utils import run_bass_kernel_spmd

F32 = mybir.dt.float32
BF16 = mybir.dt.bfloat16
AL = mybir.AluOpType
AF = mybir.ActivationFunctionType
AX = mybir.AxisListType.X

MAGIC = 12582912.0

D = 1024
KD = 8
H = 16
I = 2816
KI = 22
NT = 512
TPB = 4
B = 4
T = B * TPB
SCL = 0.125
EPS = 1e-5

ENGS = ("sync", "gpsimd", "scalar", "vector", "tensor")
DMA_SEMS = {"ld", "gd", "cc", "sd", "vd"}


class Prog:

    def __init__(self):
        self.steps = {e: [] for e in ENGS}
        self.cnt = {}
        self.strict_v = False

    def tok(self, s):
        return (s, self.cnt.get(s, 0))

    def emit(self, eng, fn=None, waits=(), inc=None, n=1):
        ws = [w for w in waits if w is not None and w[1]]
        if eng == "vector" and self.strict_v and self.cnt.get("v", 0):
            ws.append(("v", self.cnt["v"]))
        post = None
        if inc is not None:
            self.cnt[inc] = self.cnt.get(inc, 0) + n
            post = self.cnt[inc]
        self.steps[eng].append([ws, fn, inc, n])
        return (inc, post) if inc is not None else None

    def barrier(self, owner, skip=("cc",)):
        snap = {s: v for s, v in self.cnt.items() if v > 0 and s not in skip}
        for e in ENGS:
            ws = [(s, v) for s, v in snap.items() if owner.get(s) != e]
            self.emit(e, None, waits=ws)

    def finalize(self):
        used = {}
        for e in ENGS:
            for ws, fn, inc, n in self.steps[e]:
                for s, v in ws:
                    used.setdefault(s, set()).add(v)
        remap = {}
        for s in list(self.cnt):
            if s in DMA_SEMS:
                continue
            kept = sorted(used.get(s, ()))
            remap[s] = {v: i + 1 for i, v in enumerate(kept)}
        for e in ENGS:
            seen = {}
            out = []
            for ws, fn, inc, n in self.steps[e]:
                nws = []
                for s, v in ws:
                    if s in remap:
                        v = remap[s].get(v)
                        if v is None:
                            continue
                    if seen.get(s, 0) >= v:
                        continue
                    seen[s] = v
                    nws.append((s, v))
                if inc is not None and inc in remap:
                    pass
                out.append([nws, fn, inc, n])
            self.steps[e] = out
        run = {}
        for e in ENGS:
            for st in self.steps[e]:
                ws, fn, inc, n = st
                if inc is not None and inc in remap:
                    run[inc] = run.get(inc, 0) + n
                    if run[inc] in remap[inc] and remap[inc][run[inc]]:
                        st[2] = inc
                    else:
                        st[2] = None


def build_nc():
    nc = bass.Bass("TRN2", target_bir_lowering=False,
                   detect_race_conditions=False)

    x_in = nc.dram_tensor("x", [T, 128, KD * NT], F32, kind="ExternalInput").ap()
    wq_in = nc.dram_tensor("wq", [128, KD, D], F32, kind="ExternalInput").ap()
    wk_in = nc.dram_tensor("wk", [128, KD, D], F32, kind="ExternalInput").ap()
    wt_in = nc.dram_tensor("wt", [128, KD, D], F32, kind="ExternalInput").ap()
    wqa_in = nc.dram_tensor("wqa", [128, KD, H], F32, kind="ExternalInput").ap()
    wka_in = nc.dram_tensor("wka", [128, KD, H], F32, kind="ExternalInput").ap()
    bq_in = nc.dram_tensor("bq", [128, KD], F32, kind="ExternalInput").ap()
    bk_in = nc.dram_tensor("bk", [128, KD], F32, kind="ExternalInput").ap()
    bt_in = nc.dram_tensor("bt", [128, KD], F32, kind="ExternalInput").ap()
    bqa_in = nc.dram_tensor("bqa", [H, 1], F32, kind="ExternalInput").ap()
    bka_in = nc.dram_tensor("bka", [H, 1], F32, kind="ExternalInput").ap()
    wg_in = nc.dram_tensor("wg", [KI, 128, D], F32, kind="ExternalInput").ap()
    wu_in = nc.dram_tensor("wu", [KI, 128, D], F32, kind="ExternalInput").ap()
    wd_in = nc.dram_tensor("wd", [KD, 128, I], F32, kind="ExternalInput").ap()
    c_in = nc.dram_tensor("cmod", [128, KD, B], F32, kind="ExternalInput").ap()
    wm_in = nc.dram_tensor("wmod", [128, KD, 6 * D], F32, kind="ExternalInput").ap()
    clipx_in = nc.dram_tensor("clipx", [1, 1], F32, kind="ExternalInput").ap()
    clipd_in = nc.dram_tensor("clipd", [1, 1], F32, kind="ExternalInput").ap()

    y_out = nc.dram_tensor("y", [T, 128, KD, NT], F32, kind="ExternalOutput").ap()

    xm_hbm = nc.dram_tensor("xm_s", [T, 128, KD, NT], F32).ap()
    mid_hbm = nc.dram_tensor("mid_s", [T, 128, KI, NT], F32).ap()
    rows2_hbm = nc.dram_tensor("rows2_s", [T, 3, NT], BF16).ap()
    cc1_i = nc.dram_tensor("cc1_i", [1], F32).ap()
    cc1_o = nc.dram_tensor("cc1_o", [1], F32).ap()
    cc2_i = nc.dram_tensor("cc2_i", [1], F32).ap()
    cc2_o = nc.dram_tensor("cc2_o", [1], F32).ap()

    p = Prog()
    owner = {"ld": "sync", "gd": "gpsimd", "g": "gpsimd", "cc": "gpsimd",
             "sd": "scalar", "vd": "vector",
             "t": "tensor", "v": "vector", "s": "scalar"}
    root = ExitStack()
    sem = {n: root.enter_context(nc.semaphore(n)) for n in owner}
    SB = lambda name, shape, dt: nc.sbuf_tensor(name, shape, dt)
    PS = lambda name, shape, dt: nc.psum_tensor(name, shape, dt)
    E = p.emit

    mods = root.enter_context(SB("mods", [128, 6 * KD * B], F32))
    gabt = root.enter_context(SB("gabt", [128, B, KD], F32))
    bqc = root.enter_context(SB("bqc", [128, KD], F32))
    bkc = root.enter_context(SB("bkc", [128, KD], F32))
    btc = root.enter_context(SB("btc", [128, KD], F32))
    bqa8 = root.enter_context(SB("bqa8", [H, 1], F32))
    bka8 = root.enter_context(SB("bka8", [H, 1], F32))
    id_bf = root.enter_context(SB("id_bf", [128, 128], BF16))
    id_f = root.enter_context(SB("id_f", [128, 128], F32))
    ones_col = root.enter_context(SB("ones_col", [128, 1], BF16))
    ones_row = root.enter_context(SB("ones_row", [1, 128], BF16))
    emaskT = root.enter_context(SB("emaskT", [H, KD * 128], BF16))
    absacc2 = root.enter_context(SB("absacc2", [128, 1], F32))
    clipx_sb = root.enter_context(SB("clipx_sb", [1, 1], F32))
    clipd_sb = root.enter_context(SB("clipd_sb", [1, 1], F32))
    s1col = root.enter_context(SB("s1col", [128, 1], F32))
    is1col = root.enter_context(SB("is1col", [128, 1], F32))
    s2col = root.enter_context(SB("s2col", [128, 1], F32))
    is2col = root.enter_context(SB("is2col", [128, 1], F32))
    sgc = root.enter_context(SB("sgc", [128, KI], F32))
    suc = root.enter_context(SB("suc", [128, KI], F32))
    sdc = root.enter_context(SB("sdc", [128, KD], F32))
    sg1 = root.enter_context(SB("sg1", [128, KI], F32))
    su1 = root.enter_context(SB("su1", [128, KI], F32))
    sd2 = root.enter_context(SB("sd2", [128, KD], F32))
    c2s1 = root.enter_context(SB("c2s1", [128, KD * B], F32))
    d2s1 = root.enter_context(SB("d2s1", [128, KD * B], F32))
    cst_eps = root.enter_context(SB("cst_eps", [128, 1], F32))
    cst_nh = root.enter_context(SB("cst_nh", [128, 1], F32))
    cst_mg = root.enter_context(SB("cst_mg", [128, 1], F32))
    cst_nmg = root.enter_context(SB("cst_nmg", [128, 1], F32))
    cst_scl = root.enter_context(SB("cst_scl", [128, 1], F32))

    MT_SA, MT_CA, MT_GA, MT_SB, MT_CB, MT_GB = range(6)

    def modc(t_, c, b):
        j = t_ * KD * B + c * B + b
        return mods[:, j:j + 1]

    def rng_(t_):
        return mods[:, t_ * KD * B:(t_ + 1) * KD * B]

    watt = ExitStack()
    wq_sb = watt.enter_context(SB("wq_sb", [128, KD, D], BF16))
    wk_sb = watt.enter_context(SB("wk_sb", [128, KD, D], BF16))
    wt_sb = watt.enter_context(SB("wt_sb", [128, KD, D], BF16))
    wqa_sb = watt.enter_context(SB("wqa_sb", [128, KD, H], BF16))
    wka_sb = watt.enter_context(SB("wka_sb", [128, KD, H], BF16))

    su_arena = ExitStack()
    c_f = su_arena.enter_context(SB("c_f", [128, KD * B], F32))
    c_bf = su_arena.enter_context(SB("c_bf", [128, KD, B], BF16))
    wm_sb = [su_arena.enter_context(SB(f"wm{i}", [128, 3 * D], F32)) for i in range(3)]
    wmb_sb = [su_arena.enter_context(SB(f"wmb{i}", [128, 3 * D], BF16)) for i in range(3)]
    wstg = [su_arena.enter_context(SB(f"wstg{i}", [128, D], F32)) for i in range(4)]
    modT = su_arena.enter_context(SB("modT", [B, 3 * D], F32))
    scr_a = su_arena.enter_context(SB("scr_a", [128, KD * B], F32))
    scr_b = su_arena.enter_context(SB("scr_b", [128, KD * B], F32))
    scr_c = su_arena.enter_context(SB("scr_c", [128, KD * B], F32))
    iota_i = su_arena.enter_context(SB("iota_i", [H, 128], mybir.dt.int32))
    jramp = su_arena.enter_context(SB("jramp", [H, 128], F32))
    hidx = su_arena.enter_context(SB("hidx", [H, 1], F32))
    tgt_sc = su_arena.enter_context(SB("tgt_sc", [H, 128], F32))
    mod_ps = su_arena.enter_context(PS("mod_ps", [B, 3 * D], F32))
    tp_ps = su_arena.enter_context(PS("tp_ps", [128, 96], F32))

    for dst, src in ((bqc, bq_in), (bkc, bk_in), (btc, bt_in)):
        E("sync", lambda e, d=dst, s=src: e.dma_start(out=d[:, :], in_=s[:, :]),
          inc="ld", n=16)
    E("sync", lambda e: e.dma_start(out=bqa8[:, :], in_=bqa_in[:, :]), inc="ld", n=16)
    E("sync", lambda e: e.dma_start(out=bka8[:, :], in_=bka_in[:, :]), inc="ld", n=16)
    E("sync", lambda e: e.dma_start(out=clipx_sb[:, :], in_=clipx_in[:, :]), inc="ld", n=16)
    E("sync", lambda e: e.dma_start(out=clipd_sb[:, :], in_=clipd_in[:, :]), inc="ld", n=16)
    E("sync", lambda e: e.dma_start(out=c_f[:, :], in_=c_in[:, :, :]), inc="ld", n=16)
    t_small = p.tok("ld")

    t_w_ld = {}
    for (key, dst, src) in (("wqa", wqa_sb, wqa_in), ("wka", wka_sb, wka_in)):
        t_w_ld[key] = E("gpsimd", lambda e, d=dst, s=src: e.dma_start(
            out=d[:, :, :], in_=s[:, :, :]), inc="gd", n=16)
    g_wstg = [None] * 4
    wi = 0
    for (key, dst, src, ceng, csem) in (
            ("wq", wq_sb, wq_in, "vector", "v"),
            ("wk", wk_sb, wk_in, "gpsimd", "g"),
            ("wt", wt_sb, wt_in, "vector", "v")):
        for c in range(KD):
            sbf = wi % 4
            eng0, sem0 = [("sync", "ld"), ("scalar", "sd")][wi % 2]
            tldw = E(eng0, lambda e, sbf=sbf, src=src, c=c: e.dma_start(
                out=wstg[sbf][:, :], in_=src[:, c, :]),
                waits=[g_wstg[sbf]], inc=sem0, n=16)
            g_wstg[sbf] = E(ceng, lambda e, sbf=sbf, dst=dst, c=c: e.tensor_copy(
                out=dst[:, c, :], in_=wstg[sbf][:, :]),
                waits=[tldw], inc=csem)
            wi += 1
        t_w_ld[key] = g_wstg[(wi - 1) % 4]

    nc.const_aps.register(F32, EPS, cst_eps[:, 0:1])
    nc.const_aps.register(F32, -0.5, cst_nh[:, 0:1])
    nc.const_aps.register(F32, MAGIC, cst_mg[:, 0:1])
    nc.const_aps.register(F32, -MAGIC, cst_nmg[:, 0:1])
    nc.const_aps.register(F32, SCL, cst_scl[:, 0:1])
    p.strict_v = True
    E("vector", lambda e: e.memset(cst_eps[:, :], EPS), inc="v")
    E("vector", lambda e: e.memset(cst_nh[:, :], -0.5), inc="v")
    E("vector", lambda e: e.memset(cst_mg[:, :], MAGIC), inc="v")
    E("vector", lambda e: e.memset(cst_nmg[:, :], -MAGIC), inc="v")
    E("vector", lambda e: e.memset(cst_scl[:, :], SCL), inc="v")
    E("vector", lambda e: e.memset(ones_col[:, :], 1.0 / D), inc="v")
    E("vector", lambda e: e.memset(ones_row[:, :], 1.0), inc="v")
    E("vector", lambda e: e.memset(absacc2[:, :], 0.0), inc="v")
    E("gpsimd", lambda e: e.memset(id_bf[:, :], 0.0))
    E("gpsimd", lambda e: e.affine_select(
        out=id_bf[:, :], in_=id_bf[:, :], compare_op=AL.not_equal, fill=1.0,
        base=0, pattern=[[-1, 128]], channel_multiplier=1), inc="g")
    E("gpsimd", lambda e: e.memset(id_f[:, :], 0.0))
    E("gpsimd", lambda e: e.affine_select(
        out=id_f[:, :], in_=id_f[:, :], compare_op=AL.not_equal, fill=1.0,
        base=0, pattern=[[-1, 128]], channel_multiplier=1), inc="g")
    E("gpsimd", lambda e: e.iota(iota_i[:, :], pattern=[[1, 128]], base=0,
                                 channel_multiplier=0), inc="g")
    t_io1 = p.tok("g")
    E("vector", lambda e: e.tensor_copy(out=jramp[:, :], in_=iota_i[:, :]),
      waits=[t_io1], inc="v")
    E("vector", lambda e: e.tensor_scalar(
        out=jramp[:, :], in0=jramp[:, :], scalar1=64.0, scalar2=None,
        op0=AL.is_ge, op1=AL.bypass), inc="v")
    t_jr = p.tok("v")
    E("gpsimd", lambda e: e.iota(iota_i[:, 0:1], pattern=[[0, 1]], base=0,
                                 channel_multiplier=1), waits=[t_jr], inc="g")
    t_io2 = p.tok("g")
    E("vector", lambda e: e.tensor_copy(out=hidx[:, :], in_=iota_i[:, 0:1]),
      waits=[t_io2], inc="v")
    for c in range(KD):
        E("vector", lambda e, c=c: e.tensor_scalar(
            out=tgt_sc[:, :], in0=jramp[:, :], scalar1=float(2 * c),
            scalar2=None, op0=AL.add, op1=AL.bypass), inc="v")
        E("vector", lambda e, c=c: e.tensor_scalar(
            out=emaskT[:, c * 128:(c + 1) * 128], in0=tgt_sc[:, :],
            scalar1=hidx[:, :], scalar2=None,
            op0=AL.is_equal, op1=AL.bypass), inc="v")
    t_masks = p.tok("v")

    tsil = E("scalar", lambda e: e.activation(
        out=c_f[:, :], in_=c_f[:, :], func=AF.Silu),
        waits=[t_small], inc="s")
    E("vector", lambda e: e.tensor_copy(out=c_bf[:, :, :], in_=c_f[:, :]),
      waits=[tsil], inc="v")
    t_cbf = p.tok("v")

    p.strict_v = False
    g_wm = [None, None, None]
    g_wmb = [None, None, None]
    g_modps_v = None
    WMQ = [("sync", "ld"), ("scalar", "sd"), ("gpsimd", "gd")]
    for half in range(2):
        t_mm_k = [None] * KD
        for kc in range(KD):
            kk = half * KD + kc
            engm, semm = WMQ[kk % 3]
            tld = E(engm, lambda e, kc=kc, half=half, kk=kk: e.dma_start(
                out=wm_sb[kk % 3][:, :],
                in_=wm_in[:, kc, half * 3 * D:(half + 1) * 3 * D]),
                waits=[g_wm[kk % 3]], inc=semm, n=16)
            ceng, csem = ("vector", "v") if kk % 2 == 0 else ("gpsimd", "g")
            tcast = E(ceng, lambda e, kk=kk: e.tensor_copy(
                out=wmb_sb[kk % 3][:, :], in_=wm_sb[kk % 3][:, :]),
                waits=[tld, g_wmb[kk % 3]], inc=csem)
            g_wm[kk % 3] = tcast

            def mm(e, kc=kc, kk=kk):
                last = None
                for j in range(6):
                    last = e.matmul(mod_ps[:, j * NT:(j + 1) * NT],
                                    c_bf[:, kc, :],
                                    wmb_sb[kk % 3][:, j * NT:(j + 1) * NT],
                                    start=(kc == 0), stop=(kc == KD - 1))
                return last
            t_mm_k[kc] = E("tensor", mm,
                           waits=[tcast, t_cbf, g_wmb[kk % 3], g_modps_v],
                           inc="t")
            g_wmb[kk % 3] = t_mm_k[kc]
        tv = E("vector", lambda e: e.tensor_copy(out=modT[:, :], in_=mod_ps[:, :]),
               waits=[t_mm_k[-1]], inc="v")
        g_modps_v = tv

        def tps(e):
            last = None
            for i in range(24):
                last = e.transpose(tp_ps[:, i * B:(i + 1) * B],
                                   modT[:, i * 128:(i + 1) * 128],
                                   id_f[0:B, 0:B])
            return last
        tt_ = E("tensor", tps, waits=[tv, p.tok("g")], inc="t")
        E("vector", lambda e, half=half: e.tensor_copy(
            out=mods[:, half * 96:(half + 1) * 96], in_=tp_ps[:, :]),
            waits=[tt_], inc="v")
        g_wm = [p.tok("t")] * 3
    t_mods_raw = p.tok("v")
    p.strict_v = True

    for tC in (MT_CA, MT_CB):
        E("vector", lambda e, tC=tC: e.tensor_scalar(
            out=rng_(tC), in0=rng_(tC), scalar1=1.0, scalar2=None,
            op0=AL.add, op1=AL.bypass), waits=[t_mods_raw], inc="v")
    for tG in (MT_GA, MT_GB):
        E("vector", lambda e, tG=tG: e.tensor_scalar(
            out=scr_a[:, :], in0=rng_(tG), scalar1=0.0, scalar2=None,
            op0=AL.is_ge, op1=AL.bypass), waits=[t_mods_raw], inc="v")
        E("vector", lambda e, tG=tG: e.tensor_scalar(
            out=scr_b[:, :], in0=rng_(tG), scalar1=0.25, scalar2=None,
            op0=AL.max, op1=AL.bypass), inc="v")
        E("vector", lambda e, tG=tG: e.tensor_scalar(
            out=scr_c[:, :], in0=rng_(tG), scalar1=-0.25, scalar2=None,
            op0=AL.min, op1=AL.bypass), inc="v")
        E("vector", lambda e: e.tensor_sub(scr_b[:, :], scr_b[:, :], scr_c[:, :]),
          inc="v")
        E("vector", lambda e: e.tensor_mul(scr_a[:, :], scr_a[:, :], scr_b[:, :]),
          inc="v")
        E("vector", lambda e, tG=tG: e.tensor_add(rng_(tG), scr_a[:, :], scr_c[:, :]),
          inc="v")
    for b in range(B):
        for mc in range(KD):
            E("vector", lambda e, b=b, mc=mc: e.tensor_mul(
                gabt[:, b, mc:mc + 1], modc(MT_GA, mc, b), btc[:, mc:mc + 1]),
                inc="v")
    E("vector", lambda e: e.tensor_scalar(
        out=bqa8[:, :], in0=bqa8[:, :], scalar1=SCL, scalar2=None,
        op0=AL.mult, op1=AL.bypass), waits=[t_small], inc="v")
    E("vector", lambda e: e.tensor_scalar(
        out=bka8[:, :], in0=bka8[:, :], scalar1=SCL, scalar2=None,
        op0=AL.mult, op1=AL.bypass), inc="v")

    p.strict_v = False
    su_arena.close()
    p.barrier(owner)

    ph1 = ExitStack()
    x_sb = [ph1.enter_context(SB(f"x{i}", [128, KD, NT], F32)) for i in range(2)]
    h_sb = [ph1.enter_context(SB(f"h_sb{i}", [128, KD, NT], BF16)) for i in range(2)]
    mq_sb = ph1.enter_context(SB("mq_sb", [128, KD, TPB * NT], BF16))
    mk_sb = ph1.enter_context(SB("mk_sb", [128, KD, TPB * NT], BF16))
    sc2 = [ph1.enter_context(SB(f"sc2_{i}", [H, NT], BF16)) for i in range(2)]
    pparts = ph1.enter_context(SB("pparts", [128, KD * TPB], F32))
    pqc = ph1.enter_context(SB("pqc", [128, KD], F32))
    pkc = ph1.enter_context(SB("pkc", [128, KD], F32))
    xb_sc = [ph1.enter_context(SB(f"xb{i}", [128, NT], BF16)) for i in range(4)]
    x2_sc = [ph1.enter_context(SB(f"x2{i}", [128, NT], BF16)) for i in range(4)]
    xcp = [ph1.enter_context(SB(f"xcp{i}", [128, NT], F32)) for i in range(2)]
    te_sc = [ph1.enter_context(SB(f"te{i}", [128, NT], F32)) for i in range(2)]
    redg_sc = [ph1.enter_context(SB(f"redg{i}", [128, 1], F32)) for i in range(2)]
    absg = ph1.enter_context(SB("absg", [128, 1], F32))
    xm_sb = ph1.enter_context(SB("xm_sb", [128, KD, NT], F32))
    rowf = [ph1.enter_context(SB(f"rowf{i}", [1, NT], F32)) for i in range(2)]
    rows3 = ph1.enter_context(SB("rows3", [1, 3 * NT], BF16))
    qsum = ph1.enter_context(SB("qsum", [H, 1], F32))
    qmax = ph1.enter_context(SB("qmax", [H, 1], F32))
    qrec = ph1.enter_context(SB("qrec", [H, 1], F32))
    qsums = ph1.enter_context(SB("qsums", [H, TPB], F32))
    negc = ph1.enter_context(SB("negc", [H, 1], F32))
    biasc = ph1.enter_context(SB("biasc", [H, 1], F32))
    qrh = ph1.enter_context(SB("qrh", [H, 1], BF16))
    qrl = ph1.enter_context(SB("qrl", [H, 1], BF16))

    ps1 = ExitStack()
    ps_a = ps1.enter_context(PS("ps_a", [128, NT], F32))
    ps_b = ps1.enter_context(PS("ps_b", [128, NT], F32))
    ps_c = ps1.enter_context(PS("ps_c", [128, NT], F32))
    ps_d = ps1.enter_context(PS("ps_d", [128, NT], F32))
    ps_e1 = ps1.enter_context(PS("ps_e1", [128, NT], F32))
    ps_e2 = ps1.enter_context(PS("ps_e2", [128, NT], F32))
    ps_f = ps1.enter_context(PS("ps_f", [128, NT], F32))
    ps_g2 = ps1.enter_context(PS("ps_g2", [128, NT], F32))

    E("vector", lambda e: e.memset(absg[:, :], 0.0), inc="v")

    g_h_t = [None, None]
    g_ab = [None, None, None, None]
    g_cd_t = None
    g_mq_t = None
    g_mk_t = None
    g_psf_v = None
    g_psg2_v = None
    g_psg2_s = None
    g_xm_g = None
    g_xm_cons = None
    g_te = [None, None]
    g_xb = [None, None, None, None]
    g_x2 = [None, None, None, None]
    g_scq = [None, None]
    g_xcg = [None, None]
    g_redg = [None, None]
    g_rows3 = [None, None]
    x_cons = [[], []]
    pool_st = {}
    mk_ep = [None] * TPB
    t_wka_fresh = t_w_ld["wka"]
    t_wt_fresh = [t_w_ld["wt"], t_w_ld["wt"]]

    def rows_math(wtok):
        E("vector", lambda e: e.tensor_copy(out=rowf[0][:, :], in_=ps_c[0:1, :]),
          waits=[wtok], inc="v")
        E("vector", lambda e: e.tensor_copy(out=rows3[:, 2 * NT:3 * NT],
                                            in_=rowf[0][:, :]),
          waits=[g_rows3[0], g_rows3[1]], inc="v")
        E("vector", lambda e: e.tensor_mul(rowf[0][:, :], rowf[0][:, :],
                                           rowf[0][:, :]), inc="v")
        tv = E("vector", lambda e: e.tensor_sub(rowf[1][:, :], ps_d[0:1, :],
                                                rowf[0][:, :]), inc="v")
        ts_ = E("scalar", lambda e: e.activation(
            out=rowf[1][:, :], in_=rowf[1][:, :], func=AF.Ln, bias=EPS),
            waits=[tv], inc="s")
        ts_ = E("scalar", lambda e: e.activation(
            out=rowf[1][:, :], in_=rowf[1][:, :], func=AF.Exp, scale=-0.5),
            waits=[ts_], inc="s")
        E("vector", lambda e: e.tensor_copy(out=rows3[:, 0:NT], in_=rowf[1][:, :]),
          waits=[ts_], inc="v")
        E("vector", lambda e: e.tensor_sub(rows3[:, NT:2 * NT], rowf[1][:, :],
                                           rows3[:, 0:NT]), inc="v")
        return p.tok("v")

    def bcasts(t_rows):
        def mmb(e):
            e.matmul(ps_d[:, :], ones_row[:, :], rows3[:, 2 * NT:3 * NT],
                     start=True, stop=True)
            e.matmul(ps_c[:, :], ones_row[:, :], rows3[:, 0:NT], start=True,
                     stop=False)
            return e.matmul(ps_c[:, :], ones_row[:, :], rows3[:, NT:2 * NT],
                            start=False, stop=True)
        tb = E("tensor", mmb, waits=[t_rows], inc="t")
        g_rows3[0] = tb
        return tb

    def mk_chain(t, kind):
        b = t // TPB
        xb = t % 2
        ST = {}

        def s_load():
            ST["tld"] = E("sync", lambda e, t=t, xb=xb: e.dma_start(
                out=x_sb[xb][:, :, :],
                in_=x_in[t].rearrange("p (c n) -> p c n", c=KD)),
                waits=list(x_cons[xb]), inc="ld", n=16)
            x_cons[xb] = []

        def s_stats():
            nonlocal g_cd_t
            src = x_sb[xb] if kind == "s1" else xm_sb
            wt0 = ST["tld"] if kind == "s1" else ST["t_src"]
            tm = None
            for c in range(KD):
                cb4 = c % 4
                if c % 2 == 0:
                    tc_ = E("gpsimd", lambda e, c=c, cb4=cb4, src=src:
                            e.tensor_copy(
                                out=xb_sc[cb4][:, :], in_=src[:, c, :]),
                            waits=[wt0, g_xb[cb4]], inc="g")
                else:
                    tc_ = E("scalar", lambda e, c=c, cb4=cb4, src=src:
                            e.activation(
                                out=xb_sc[cb4][:, :], in_=src[:, c, :],
                                func=AF.Identity),
                            waits=[wt0, g_xb[cb4]], inc="s")
                tq_ = E("scalar", lambda e, c=c, cb4=cb4, src=src: e.activation(
                    out=x2_sc[cb4][:, :], in_=src[:, c, :], func=AF.Square),
                    waits=[wt0, g_x2[cb4]], inc="s")

                def mmst(e, c=c, cb4=cb4):
                    e.matmul(ps_c[0:1, :], ones_col[:, :], xb_sc[cb4][:, :],
                             start=(c == 0), stop=(c == KD - 1))
                    return e.matmul(ps_d[0:1, :], ones_col[:, :],
                                    x2_sc[cb4][:, :],
                                    start=(c == 0), stop=(c == KD - 1))
                tm = E("tensor", mmst, waits=[tc_, tq_, g_cd_t], inc="t")
                g_xb[cb4] = tm
                g_x2[cb4] = tm
            ST["t_cp"] = p.tok("g")
            ST["t_sq"] = p.tok("s")
            ST["tm"] = tm

        def s_rows():
            ST["t_rows"] = rows_math(ST["tm"])

        def s_bcast():
            ST["tb"] = bcasts(ST["t_rows"])
            if kind == "s3":
                ST["t_r2st"] = E("gpsimd", lambda e, t=t: e.dma_start(
                    out=rows2_hbm[t].rearrange("a n -> (a n)").rearrange(
                        "(a n) -> a n", a=1), in_=rows3[:, :]),
                    waits=[ST["t_rows"]], inc="gd", n=16)
                g_rows3[1] = ST["t_r2st"]

        def mk_ln(c):
            def s_ln():
                nonlocal g_cd_t
                cp = c % 2
                tC = MT_CA if kind == "s1" else MT_CB
                tS = MT_SA if kind == "s1" else MT_SB
                src = x_sb[xb] if kind == "s1" else xm_sb
                E("vector", lambda e, c=c, cp=cp, src=src: e.tensor_sub(
                    xcp[cp][:, :], src[:, c, :], ps_d[:, :]),
                    waits=[ST["tb"], g_xcg[cp]], inc="v")
                E("vector", lambda e, c=c, cp=cp, b=b, tC=tC:
                    e.scalar_tensor_tensor(
                        out=xcp[cp][:, :], in0=xcp[cp][:, :],
                        scalar=modc(tC, c, b), in1=ps_c[:, :],
                        op0=AL.mult, op1=AL.mult), inc="v")
                if kind == "s1":
                    E("vector", lambda e, c=c, cp=cp, b=b, xb=xb, tS=tS:
                        e.tensor_scalar(
                            out=h_sb[xb][:, c, :], in0=xcp[cp][:, :],
                            scalar1=modc(tS, c, b), scalar2=None,
                            op0=AL.add, op1=AL.bypass),
                        waits=[g_h_t[xb]], inc="v")
                    g_h_t[xb] = None
                else:
                    E("vector", lambda e, c=c, cp=cp, b=b, tS=tS:
                        e.tensor_scalar(
                            out=xcp[cp][:, :], in0=xcp[cp][:, :],
                            scalar1=modc(tS, c, b), scalar2=None,
                            op0=AL.add, op1=AL.bypass), inc="v")
                    E("vector", lambda e, cp=cp: e.tensor_reduce(
                        out=redg_sc[cp][:, :], in_=xcp[cp][:, :], axis=AX,
                        op=AL.max, apply_absolute_value=True), inc="v")
                    E("vector", lambda e, cp=cp: e.tensor_max(
                        absg[:, :], absg[:, :], redg_sc[cp][:, :]), inc="v")
                    g_xcg[cp] = p.tok("v")
                if c == KD - 1:
                    ST["t_ln"] = p.tok("v")
                    if kind == "s1":
                        x_cons[xb] = [p.tok("v"), ST["t_cp"], ST["t_sq"]]
                        ST["t_h"] = p.tok("v")
                    g_cd_t = p.tok("v")
            return s_ln

        steps = []
        if kind == "s1":
            steps.append(s_load)
        steps.append(s_stats)
        steps.append(s_rows)
        steps.append(s_bcast)
        for c in range(KD):
            steps.append(mk_ln(c))
        return ST, steps

    pool_guard = {}

    def pool_scores(q, src_mm, sums, bias8, tq_mm, banks):
        nonlocal g_psg2_s, g_psf_v
        qp = q % 2
        qsl = slice(q * NT, (q + 1) * NT)
        if q == 0:
            E("vector", lambda e, bias8=bias8: e.tensor_scalar(
                out=sc2[0][:, :], in0=ps_g2[0:H, :], scalar1=SCL,
                scalar2=bias8[:, :], op0=AL.mult, op1=AL.add),
                waits=[tq_mm, g_scq[0]], inc="v")
            E("vector", lambda e: e.tensor_reduce(
                out=qmax[:, :], in_=sc2[0][:, :], axis=AX, op=AL.max),
                waits=[p.tok("v")], inc="v")
            E("vector", lambda e: e.tensor_scalar(
                out=negc[:, :], in0=qmax[:, :], scalar1=-1.0, scalar2=None,
                op0=AL.mult, op1=AL.bypass), waits=[p.tok("v")], inc="v")
            tbc = E("vector", lambda e, bias8=bias8: e.tensor_add(
                biasc[:, :], bias8[:, :], negc[:, :]),
                waits=[p.tok("v")], inc="v")
            pool_st["biasc"] = tbc
            tse = E("scalar", lambda e, sums=sums: e.activation(
                out=sc2[0][:, :], in_=sc2[0][:, :], func=AF.Exp,
                bias=negc[:, :], accum_out=sums[:, 0:1]), waits=[tbc], inc="s")
        else:
            tse = E("scalar", lambda e, q=q, qp=qp, sums=sums: e.activation(
                out=sc2[qp][:, :], in_=ps_g2[0:H, :], func=AF.Exp,
                scale=SCL, bias=biasc[:, :], accum_out=sums[:, q:q + 1]),
                waits=[tq_mm, pool_st["biasc"], g_scq[qp]], inc="s")
        g_psg2_s = tse
        tmx = None
        for c in range(KD):
            pb = banks[c % 2]

            def mmx(e, c=c, qp=qp, pb=pb):
                return e.matmul(pb[:, :], emaskT[:, c * 128:(c + 1) * 128],
                                sc2[qp][:, :], start=True, stop=True)
            tmx = E("tensor", mmx, waits=[tse, pool_guard.get(id(pb))], inc="t")
            pool_guard[id(pb)] = E(
                "vector", lambda e, c=c, q=q, qsl=qsl, src_mm=src_mm, pb=pb:
                e.scalar_tensor_tensor(
                    out=xcp[0][:, :], in0=src_mm[:, c, qsl], scalar=1.0,
                    in1=pb[:, :], op0=AL.mult, op1=AL.mult,
                    accum_out=pparts[:, c * TPB + q:c * TPB + q + 1]),
                waits=[tmx, g_xcg[0]], inc="v")
        g_psf_v = pool_guard[id(ps_f)]
        g_scq[qp] = tmx

    def pool_final(dst, sums):
        nonlocal g_psg2_v
        p.strict_v = True
        E("vector", lambda e, sums=sums: e.tensor_reduce(
            out=qsum[:, :], in_=sums[:, :], axis=AX, op=AL.add),
            waits=[p.tok("s"), p.tok("v")], inc="v")
        E("vector", lambda e: e.reciprocal(qrec[:, :], qsum[:, :]), inc="v")
        E("vector", lambda e: e.tensor_copy(out=qrh[:, :], in_=qrec[:, :]),
          inc="v")
        tvl = E("vector", lambda e: e.tensor_sub(qrl[:, :], qrec[:, :],
                                                 qrh[:, :]), inc="v")
        fin_guard = [None, None]
        for c in range(KD):
            pb = [ps_g2, ps_f][c % 2]

            def mmr(e, c=c, pb=pb):
                e.matmul(pb[:, 0:1], emaskT[:, c * 128:(c + 1) * 128],
                         qrh[:, :], start=True, stop=False)
                return e.matmul(pb[:, 0:1], emaskT[:, c * 128:(c + 1) * 128],
                                qrl[:, :], start=False, stop=True)
            tmr = E("tensor", mmr,
                    waits=([tvl, p.tok("s"), p.tok("v")] if c < 2 else
                           [tvl, fin_guard[c % 2]]), inc="t")
            E("vector", lambda e, c=c, dst=dst: e.tensor_reduce(
                out=dst[:, c:c + 1], in_=pparts[:, c * TPB:(c + 1) * TPB],
                axis=AX, op=AL.add), waits=[p.tok("v")], inc="v")
            fin_guard[c % 2] = E(
                "vector", lambda e, c=c, dst=dst, pb=pb: e.tensor_mul(
                    dst[:, c:c + 1], dst[:, c:c + 1], pb[:, 0:1]),
                waits=[tmr, p.tok("v")], inc="v")
        E("vector", None, waits=[p.tok("v")])
        p.strict_v = False
        g_psg2_v = p.tok("v")
        pool_guard[id(ps_g2)] = p.tok("v")
        pool_guard[id(ps_f)] = p.tok("v")
        return p.tok("v")

    def run_steps(steps, n):
        for _ in range(n):
            if steps:
                steps.pop(0)()

    ST0, steps0 = mk_chain(0, "s1")
    for s in steps0:
        s()
    cur = ST0

    for b in range(B):
        for q in range(TPB):
            t = b * TPB + q
            xb = t % 2
            nsl = slice(q * NT, (q + 1) * NT)
            t_h = cur["t_h"]
            if t + 1 < T and q < TPB - 1:
                nxt, nsteps = mk_chain(t + 1, "s1")
                run_steps(nsteps, 1)
            else:
                nxt, nsteps = None, []
            gi = 0
            for (wsb, bcol, dst, wkey) in ((wq_sb, bqc, mq_sb, "wq"),
                                           (wk_sb, bkc, mk_sb, "wk")):
                for mc in range(KD):
                    bank = [ps_a, ps_b, ps_e1, ps_e2][gi % 4]

                    def mmp(e, wsb=wsb, mc=mc, bank=bank, xb=xb):
                        last = None
                        for kc in range(KD):
                            last = e.matmul(
                                bank[:, :], wsb[:, kc, mc * 128:(mc + 1) * 128],
                                h_sb[xb][:, kc, :], start=(kc == 0),
                                stop=(kc == KD - 1))
                        return last
                    tm = E("tensor", mmp,
                           waits=[t_h, t_w_ld[wkey], g_ab[gi % 4]], inc="t")
                    wmqk = g_mq_t if dst is mq_sb else g_mk_t
                    if gi % 2 == 0:
                        tv = E("scalar",
                               lambda e, dst=dst, bcol=bcol, mc=mc, bank=bank,
                               nsl=nsl: e.activation(
                                   out=dst[:, mc, nsl], in_=bank[:, :],
                                   func=AF.Identity, bias=bcol[:, mc:mc + 1]),
                               waits=[tm, wmqk], inc="s")
                    else:
                        tv = E("vector",
                               lambda e, dst=dst, bcol=bcol, mc=mc, bank=bank,
                               nsl=nsl: e.tensor_scalar(
                                   out=dst[:, mc, nsl], in0=bank[:, :],
                                   scalar1=bcol[:, mc:mc + 1], scalar2=None,
                                   op0=AL.add, op1=AL.bypass),
                               waits=[tm, wmqk], inc="v")
                    g_ab[gi % 4] = tv
                    gi += 1
                    if gi >= 3:
                        run_steps(nsteps, 1)
            run_steps(nsteps, 99)
            t_mqmk = (p.tok("v"), p.tok("s"))
            mk_ep[q] = t_mqmk

            def mmq(e, nsl=nsl):
                last = None
                for kc in range(KD):
                    last = e.matmul(ps_g2[0:H, :], wqa_sb[:, kc, :],
                                    mq_sb[:, kc, nsl], start=(kc == 0),
                                    stop=(kc == KD - 1))
                return last
            tq = E("tensor", mmq, waits=[t_mqmk[0], t_mqmk[1], t_w_ld["wqa"],
                                         g_psg2_s, g_psg2_v,
                                         pool_guard.get(id(ps_g2))],
                   inc="t")
            pool_scores(q, mq_sb, qsums, bqa8, tq, (ps_f, ps_g2))
            g_h_t[xb] = p.tok("t")
            if nxt is not None:
                cur = nxt
        g_mq_t = None
        g_mk_t = None

        t_pq = pool_final(pqc, qsums)
        g_mq_t = p.tok("t")
        tka = None
        for c in range(KD):
            tka = E("scalar", lambda e, c=c: e.activation(
                out=wka_sb[:, c, :], in_=wka_sb[:, c, :],
                func=AF.Identity, scale=pqc[:, c:c + 1]),
                waits=[t_pq, t_wka_fresh] if c == 0 else [], inc="s")
        t_kafold = tka

        pool_guard[id(ps_a)] = g_ab[0]
        tk_last = None
        for q in range(TPB):
            qsl = slice(q * NT, (q + 1) * NT)

            def mmk(e, qsl=qsl):
                last = None
                for kc in range(KD):
                    last = e.matmul(ps_g2[0:H, :], wka_sb[:, kc, :],
                                    mk_sb[:, kc, qsl], start=(kc == 0),
                                    stop=(kc == KD - 1))
                return last
            tk = E("tensor", mmk, waits=[t_kafold, mk_ep[q][0], mk_ep[q][1],
                                         g_psg2_s, g_psg2_v,
                                         pool_guard.get(id(ps_g2))],
                   inc="t")
            tk_last = tk
            pool_scores(q, mk_sb, qsums, bka8, tk, (ps_f, ps_a))

        t_pk = pool_final(pkc, qsums)
        g_mk_t = p.tok("t")
        g_ab[0] = pool_guard[id(ps_a)]
        tpk2 = E("vector", lambda e: e.tensor_mul(
            pkc[:, :], pkc[:, :], pqc[:, :]), waits=[t_pk], inc="v")
        tfv = None
        for c in range(KD):
            tfv = E("scalar", lambda e, c=c: e.activation(
                out=wt_sb[:, c, :], in_=wt_sb[:, c, :],
                func=AF.Identity, scale=pkc[:, c:c + 1]),
                waits=[tpk2, t_wt_fresh[0]] if c == 0 else [], inc="s")
        t_tfold = (tfv, tfv)
        if b + 1 < B:
            t_wka_fresh = E("gpsimd", lambda e: e.dma_start(
                out=wka_sb[:, :, :], in_=wka_in[:, :, :]),
                waits=[tk_last], inc="gd", n=16)

        s3_pend = []
        for q in range(TPB):
            t = b * TPB + q
            xb = t % 2
            nsl = slice(q * NT, (q + 1) * NT)
            tld = E("sync", lambda e, t=t, xb=xb: e.dma_start(
                out=x_sb[xb][:, :, :],
                in_=x_in[t].rearrange("p (c n) -> p c n", c=KD)),
                waits=list(x_cons[xb]), inc="ld", n=16)
            x_cons[xb] = []
            for mc in range(KD):
                bank = [ps_a, ps_b, ps_e1, ps_e2][mc % 4]

                def mmo(e, mc=mc, bank=bank, nsl=nsl):
                    for kc in range(KD):
                        e.matmul(
                            bank[:, :], wt_sb[:, kc, mc * 128:(mc + 1) * 128],
                            mq_sb[:, kc, nsl], start=(kc == 0), stop=False)
                    return e.matmul(bank[:, :], id_bf[:, :], mq_sb[:, mc, nsl],
                                    start=False, stop=True)
                tm = E("tensor", mmo, waits=[t_tfold[0], t_tfold[1],
                                             g_ab[mc % 4]], inc="t")
                tsc = E("scalar", lambda e, mc=mc, bank=bank, b=b: e.activation(
                    out=te_sc[mc % 2][:, :], in_=bank[:, :], func=AF.Identity,
                    scale=modc(MT_GA, mc, b), bias=gabt[:, b, mc:mc + 1]),
                    waits=[tm, g_te[mc % 2]], inc="s")
                g_ab[mc % 4] = tsc
                run_steps(s3_pend, 3)
                tv1 = E("vector", lambda e, mc=mc, xb=xb: e.tensor_add(
                    xm_sb[:, mc, :], te_sc[mc % 2][:, :], x_sb[xb][:, mc, :]),
                    waits=[tsc, tld, g_xm_g, g_xm_cons], inc="v")
                g_te[mc % 2] = tv1
                g_xm_g = None
            t_xm = p.tok("v")
            x_cons[xb] = [t_xm]
            t_st = E("gpsimd", lambda e, t=t: e.dma_start(
                out=xm_hbm[t, :, :, :], in_=xm_sb[:, :, :]),
                waits=[t_xm], inc="gd", n=16)
            g_xm_g = t_st
            run_steps(s3_pend, 99)
            ST3, steps3 = mk_chain(t, "s3")
            ST3["t_src"] = t_xm
            run_steps(steps3, 1)
            g_xm_cons = ST3["t_cp"]
            s3_pend = steps3
        run_steps(s3_pend, 99)
        if b + 1 < B:
            cur, cb_pend = mk_chain((b + 1) * TPB, "s1")
            run_steps(cb_pend, 99)
        if b + 1 < B:
            t_wt_g = E("gpsimd", lambda e: e.dma_start(
                out=wt_sb[:, :, :], in_=wt_in[:, :, :]),
                waits=[p.tok("t")], inc="gd", n=16)
            t_wt_fresh = [t_wt_g, t_wt_g]

    t_ph1_v = p.tok("v")

    t_ph1_g = p.tok("g")
    ttm = E("tensor", lambda e: e.transpose(
        ps_f[0:1, 0:128], absg[:, :], id_f[:, :]),
        waits=[t_ph1_v, t_ph1_g, g_psf_v], inc="t")
    tv = E("vector", lambda e: e.tensor_reduce(
        out=rowf[0][:, 0:1], in_=ps_f[0:1, 0:128], axis=AX, op=AL.max),
        waits=[ttm], inc="v")
    tg = E("gpsimd", lambda e: e.dma_start(
        out=cc1_i.rearrange("(a x) -> a x", a=1), in_=rowf[0][:, 0:1]),
        waits=[tv], inc="gd", n=16)
    E("gpsimd", lambda e: e.collective_compute(
        "AllReduce", AL.max, replica_groups=[list(range(8))],
        ins=[cc1_i.opt()], outs=[cc1_o.opt()]), waits=[tg], inc="cc")

    ps1.close()
    ph1.close()
    watt.close()
    p.barrier(owner)

    ph2 = ExitStack()
    wgT = ph2.enter_context(SB("wgT", [128, KD, I], BF16))
    wuT = ph2.enter_context(SB("wuT", [128, KD, I], BF16))
    wst = [ph2.enter_context(SB(f"wst{i}", [128, D], F32)) for i in range(4)]
    wsq = [ph2.enter_context(SB(f"wsq{i}", [128, D], BF16)) for i in range(4)]
    wrow4 = ph2.enter_context(SB("wrow4", [128, 4], F32))
    xm2_sb = [ph2.enter_context(SB(f"xm2_{i}", [128, KD, NT], F32)) for i in range(2)]
    xq_sb = [ph2.enter_context(SB(f"xq{i}", [128, KD, NT], BF16)) for i in range(2)]
    mid_sb = ph2.enter_context(SB("mid_sb", [128, 4, NT], F32))
    r2_sb = [ph2.enter_context(SB(f"r2_{i}", [1, 3 * NT], BF16)) for i in range(2)]
    q_sc = [ph2.enter_context(SB(f"q_sc{i}", [128, NT], F32)) for i in range(2)]
    sil_sc = [ph2.enter_context(SB(f"sil{i}", [128, NT], F32)) for i in range(2)]
    red2 = ph2.enter_context(SB("red2", [128, 1], F32))
    gm1row = ph2.enter_context(SB("gm1row", [1, 4], F32))
    gm1b = ph2.enter_context(SB("gm1b", [1, 4], BF16))

    ps2 = ExitStack()
    p2_a = ps2.enter_context(PS("p2_a", [128, NT], F32))
    p2_b = ps2.enter_context(PS("p2_b", [128, NT], F32))
    p2_c = ps2.enter_context(PS("p2_c", [128, NT], F32))
    p2_d = ps2.enter_context(PS("p2_d", [128, NT], F32))
    p2_e = ps2.enter_context(PS("p2_e", [128, NT], F32))
    p2_f = ps2.enter_context(PS("p2_f", [128, NT], F32))
    ps2pre = ExitStack()
    p2_g = ps2pre.enter_context(PS("p2_g", [128, D], BF16))
    p2_h = ps2pre.enter_context(PS("p2_h", [128, 128], F32))

    WQQ = [("sync", "ld"), ("scalar", "sd"), ("gpsimd", "gd")]
    g_wst = [None, None, None, None]
    g_p2g_v = None

    def quant_weight(src_ap, n_chunks, dstT, scol, q0):
        nonlocal g_p2g_v
        p.strict_v = True
        for ci in range(n_chunks):
            sb_ = ci % 4
            eng, dsem = WQQ[(q0 + ci) % 3]
            tld = E(eng, lambda e, ci=ci, sb_=sb_, src_ap=src_ap: e.dma_start(
                out=wst[sb_][:, :], in_=src_ap[ci, :, :]),
                waits=[g_wst[sb_]], inc=dsem, n=16)
            E("vector", lambda e, sb_=sb_: e.tensor_reduce(
                out=wrow4[:, sb_:sb_ + 1], in_=wst[sb_][:, :], axis=AX,
                op=AL.max, apply_absolute_value=True),
                waits=[tld, g_wst[sb_]], inc="v")
            E("vector", lambda e, ci=ci, sb_=sb_, scol=scol: e.tensor_scalar(
                out=scol[:, ci:ci + 1], in0=wrow4[:, sb_:sb_ + 1],
                scalar1=1.0 / 127.0, scalar2=None,
                op0=AL.mult, op1=AL.bypass), inc="v")
            E("vector", lambda e, ci=ci, sb_=sb_, scol=scol: e.reciprocal(
                wrow4[:, sb_:sb_ + 1], scol[:, ci:ci + 1]), inc="v")
            E("scalar", lambda e, sb_=sb_: e.activation(
                out=wst[sb_][:, :], in_=wst[sb_][:, :], func=AF.Identity,
                scale=wrow4[:, sb_:sb_ + 1], bias=MAGIC),
                waits=[p.tok("v")], inc="s")
            tq = E("scalar", lambda e, sb_=sb_: e.activation(
                out=wsq[sb_][:, :], in_=wst[sb_][:, :], func=AF.Identity,
                bias=-MAGIC), inc="s")
            g_wst[sb_] = tq

            def tpw(e, sb_=sb_):
                last = None
                for kc in range(KD):
                    last = e.transpose(p2_g[:, kc * 128:(kc + 1) * 128],
                                       wsq[sb_][:, kc * 128:(kc + 1) * 128],
                                       id_bf[:, :])
                return last
            ttw = E("tensor", tpw, waits=[tq, g_p2g_v], inc="t")
            g_p2g_v = E("vector", lambda e, ci=ci, dstT=dstT: e.tensor_copy(
                out=dstT[:, :, ci * 128:(ci + 1) * 128], in_=p2_g[:, :]),
                waits=[ttw], inc="v")

    quant_weight(wg_in, KI, wgT, sgc, 0)
    quant_weight(wu_in, KI, wuT, suc, KI)
    p.strict_v = False
    t_wq_done = p.tok("v")

    tg = E("gpsimd", lambda e: e.dma_start(
        out=gm1row[:, 0:1], in_=cc1_o.rearrange("(a x) -> a x", a=1)),
        waits=[p.tok("cc")], inc="gd", n=16)
    p.strict_v = True
    ts_ = E("scalar", lambda e: e.activation(
        out=gm1row[:, 1:2], in_=clipx_sb[:, :], func=AF.Sigmoid),
        waits=[tg], inc="s")
    E("vector", lambda e: e.tensor_mul(
        gm1row[:, 0:1], gm1row[:, 0:1], gm1row[:, 1:2]),
        waits=[tg, ts_], inc="v")
    E("vector", lambda e: e.tensor_scalar(
        out=gm1row[:, 0:1], in0=gm1row[:, 0:1], scalar1=1.0 / 127.0,
        scalar2=None, op0=AL.mult, op1=AL.bypass), inc="v")
    E("vector", lambda e: e.reciprocal(gm1row[:, 1:2], gm1row[:, 0:1]), inc="v")
    E("vector", lambda e: e.tensor_copy(out=gm1b[:, 0:1], in_=gm1row[:, 0:1]),
      inc="v")
    E("vector", lambda e: e.tensor_sub(gm1b[:, 1:2], gm1row[:, 0:1],
                                       gm1b[:, 0:1]), inc="v")
    E("vector", lambda e: e.tensor_copy(out=gm1b[:, 2:3], in_=gm1row[:, 1:2]),
      inc="v")
    tv = E("vector", lambda e: e.tensor_sub(
        gm1b[:, 3:4], gm1row[:, 1:2], gm1b[:, 2:3]), inc="v")

    def bsc(e):
        e.matmul(p2_h[:, 0:1], ones_row[:, :], gm1b[:, 0:1], start=True, stop=False)
        e.matmul(p2_h[:, 0:1], ones_row[:, :], gm1b[:, 1:2], start=False, stop=True)
        e.matmul(p2_h[:, 1:2], ones_row[:, :], gm1b[:, 2:3], start=True, stop=False)
        return e.matmul(p2_h[:, 1:2], ones_row[:, :], gm1b[:, 3:4], start=False,
                        stop=True)
    tb = E("tensor", bsc, waits=[tv], inc="t")
    E("vector", lambda e: e.tensor_copy(out=s1col[:, :], in_=p2_h[:, 0:1]),
      waits=[tb], inc="v")
    tv = E("vector", lambda e: e.tensor_copy(out=is1col[:, :], in_=p2_h[:, 1:2]),
           inc="v")
    E("vector", lambda e: e.tensor_scalar(
        out=sg1[:, :], in0=sgc[:, :], scalar1=s1col[:, :], scalar2=None,
        op0=AL.mult, op1=AL.bypass), waits=[tv, t_wq_done], inc="v")
    E("vector", lambda e: e.tensor_scalar(
        out=su1[:, :], in0=suc[:, :], scalar1=s1col[:, :], scalar2=None,
        op0=AL.mult, op1=AL.bypass), inc="v")
    E("vector", lambda e: e.tensor_scalar(
        out=c2s1[:, :], in0=rng_(MT_CB), scalar1=is1col[:, :], scalar2=None,
        op0=AL.mult, op1=AL.bypass), inc="v")
    t_sc1 = E("vector", lambda e: e.tensor_scalar(
        out=d2s1[:, :], in0=rng_(MT_SB), scalar1=is1col[:, :], scalar2=None,
        op0=AL.mult, op1=AL.bypass), inc="v")
    p.strict_v = False

    ps2pre.close()
    bc_e = [p2_e, ps2.enter_context(PS("p2_e1", [128, NT], F32))]
    bc_f = [p2_f, ps2.enter_context(PS("p2_f1", [128, NT], F32))]

    g_xm2 = [None, None]
    g_xq_t = [None, None]
    g_r2 = [None, None]
    g_bct = [t_sc1, t_sc1]
    g_ab2 = [None, None]
    g_cd2 = [None, None]
    g_sil = [None, None]
    g_mid = [None, None]
    g_qsc = [None, None]
    t_xq_tile = [None] * (T + 1)
    tb_tile = [None] * (T + 1)

    def ph2_loads(t):
        xb = t % 2
        E("sync", lambda e, t=t, xb=xb: e.dma_start(
            out=xm2_sb[xb][:, :, :], in_=xm_hbm[t, :, :, :]),
            waits=[g_xm2[xb]], inc="ld", n=16)
        E("sync", lambda e, t=t, xb=xb: e.dma_start(
            out=r2_sb[xb][:, :],
            in_=rows2_hbm[t].rearrange("a n -> (a n)").rearrange(
                "(a n) -> a n", a=1)),
            waits=[g_r2[xb]], inc="ld", n=16)
        return p.tok("ld")

    def ph2_bcast(t, t_lr):
        xb = t % 2
        pe_, pf_ = bc_e[t % 2], bc_f[t % 2]

        def mmb2(e, xb=xb, pe_=pe_, pf_=pf_):
            e.matmul(pf_[:, :], ones_row[:, :], r2_sb[xb][:, 2 * NT:3 * NT],
                     start=True, stop=True)
            e.matmul(pe_[:, :], ones_row[:, :], r2_sb[xb][:, 0:NT], start=True,
                     stop=False)
            return e.matmul(pe_[:, :], ones_row[:, :], r2_sb[xb][:, NT:2 * NT],
                            start=False, stop=True)
        tb = E("tensor", mmb2, waits=[t_lr, g_bct[t % 2], t_sc1], inc="t")
        g_r2[xb] = tb
        tb_tile[t] = tb

    def ph2_qchunk(t, c):
        b = t // TPB
        xb = t % 2
        qp = c % 2
        pe_, pf_ = bc_e[t % 2], bc_f[t % 2]
        E("vector", lambda e, c=c, xb=xb, qp=qp, pf_=pf_: e.tensor_sub(
            q_sc[qp][:, :], xm2_sb[xb][:, c, :], pf_[:, :]),
            waits=[tb_tile[t], g_qsc[qp]], inc="v")
        E("vector", lambda e, c=c, b=b, qp=qp, pe_=pe_: e.scalar_tensor_tensor(
            out=q_sc[qp][:, :], in0=q_sc[qp][:, :],
            scalar=c2s1[:, c * B + b:c * B + b + 1],
            in1=pe_[:, :], op0=AL.mult, op1=AL.mult), inc="v")
        E("vector", lambda e, c=c, b=b, qp=qp: e.tensor_scalar(
            out=q_sc[qp][:, :], in0=q_sc[qp][:, :],
            scalar1=d2s1[:, c * B + b:c * B + b + 1],
            scalar2=MAGIC, op0=AL.add, op1=AL.add), inc="v")
        E("vector", lambda e, qp=qp: e.tensor_scalar(
            out=q_sc[qp][:, :], in0=q_sc[qp][:, :], scalar1=MAGIC,
            scalar2=-128.0, op0=AL.subtract, op1=AL.max), inc="v")
        tvq = E("vector", lambda e, c=c, xb=xb, qp=qp: e.tensor_scalar(
            out=xq_sb[xb][:, c, :], in0=q_sc[qp][:, :], scalar1=127.0,
            scalar2=None, op0=AL.min, op1=AL.bypass),
            waits=[g_xq_t[xb]], inc="v")
        g_qsc[qp] = tvq
        g_xq_t[xb] = None
        if c == KD - 1:
            t_xq_tile[t] = tvq
            g_xm2[xb] = tvq
            g_bct[t % 2] = tvq

    def ph2_group(t, mc):
        xb = t % 2
        bga = [p2_a, p2_b][mc % 2]
        bup = [p2_c, p2_d][mc % 2]

        def mmgu(e, mc=mc, xb=xb, bga=bga, bup=bup):
            last = None
            for kc in range(KD):
                e.matmul(bga[:, :], wgT[:, kc, mc * 128:(mc + 1) * 128],
                         xq_sb[xb][:, kc, :], start=(kc == 0),
                         stop=(kc == KD - 1))
                last = e.matmul(bup[:, :], wuT[:, kc, mc * 128:(mc + 1) * 128],
                                xq_sb[xb][:, kc, :], start=(kc == 0),
                                stop=(kc == KD - 1))
            return last
        tm = E("tensor", mmgu,
               waits=[t_xq_tile[t], g_ab2[mc % 2], g_cd2[mc % 2]], inc="t")
        tss = E("scalar", lambda e, mc=mc, bga=bga: e.activation(
            out=sil_sc[mc % 2][:, :], in_=bga[:, :], func=AF.Silu,
            scale=sg1[:, mc:mc + 1]), waits=[tm, g_sil[mc % 2]], inc="s")
        g_ab2[mc % 2] = tss
        tvm = E("vector", lambda e, mc=mc, bup=bup: e.scalar_tensor_tensor(
            out=mid_sb[:, mc % 4, :], in0=bup[:, :],
            scalar=su1[:, mc:mc + 1], in1=sil_sc[mc % 2][:, :],
            op0=AL.mult, op1=AL.mult),
            waits=[tss, g_mid[(mc // 2) % 2]], inc="v")
        g_mid[(mc // 2) % 2] = None
        g_cd2[mc % 2] = tvm
        g_sil[mc % 2] = tvm
        E("vector", lambda e, mc=mc: e.tensor_reduce(
            out=red2[:, :], in_=mid_sb[:, mc % 4, :], axis=AX, op=AL.max,
            apply_absolute_value=True), inc="v")
        tvm2 = E("vector", lambda e: e.tensor_max(
            absacc2[:, :], absacc2[:, :], red2[:, :]),
            waits=[p.tok("v")], inc="v")
        if mc % 2 == 1:
            hb = (mc // 2) % 2
            eng, dsem = ("gpsimd", "gd") if hb == 0 else ("scalar", "sd")
            g_mid[hb] = E(eng, lambda e, t=t, mc=mc, hb=hb: e.dma_start(
                out=mid_hbm[t, :, mc - 1:mc + 1, :],
                in_=mid_sb[:, hb * 2:hb * 2 + 2, :]),
                waits=[tvm2], inc=dsem, n=16)

    t_lrn = ph2_loads(0)
    ph2_bcast(0, t_lrn)
    for c in range(KD):
        ph2_qchunk(0, c)
    for t in range(T):
        if t + 1 < T:
            t_lrn = ph2_loads(t + 1)
        for mc in range(KI):
            ph2_group(t, mc)
            if t + 1 < T:
                if mc == 0:
                    ph2_bcast(t + 1, t_lrn)
                elif mc <= 8:
                    ph2_qchunk(t + 1, mc - 1)
        g_xq_t[t % 2] = p.tok("t")

    ttm = E("tensor", lambda e: e.transpose(
        bc_e[0][0:1, 0:128], absacc2[:, :], id_f[:, :]),
        waits=[p.tok("v")], inc="t")
    tv = E("vector", lambda e: e.tensor_reduce(
        out=gm1row[:, 0:1], in_=bc_e[0][0:1, 0:128], axis=AX, op=AL.max),
        waits=[ttm], inc="v")
    tg = E("gpsimd", lambda e: e.dma_start(
        out=cc2_i.rearrange("(a x) -> a x", a=1), in_=gm1row[:, 0:1]),
        waits=[tv], inc="gd", n=16)
    E("gpsimd", lambda e: e.collective_compute(
        "AllReduce", AL.max, replica_groups=[list(range(8))],
        ins=[cc2_i.opt()], outs=[cc2_o.opt()]), waits=[tg], inc="cc")

    ps2.close()
    ph2.close()
    p.barrier(owner)

    ph3 = ExitStack()
    wdT = ph3.enter_context(SB("wdT", [128, KI, D], BF16))
    w3st = [ph3.enter_context(SB(f"w3st{i}", [128, I], F32)) for i in range(2)]
    w3sq = [ph3.enter_context(SB(f"w3sq{i}", [128, I], BF16)) for i in range(2)]
    w3row = ph3.enter_context(SB("w3row", [128, 2], F32))
    mid3 = [ph3.enter_context(SB(f"mid3_{i}", [128, 2, NT], F32)) for i in range(4)]
    xq3 = [ph3.enter_context(SB(f"xq3_{i}", [128, KI, NT], BF16)) for i in range(2)]
    xm3 = [ph3.enter_context(SB(f"xm3_{i}", [128, KD, NT], F32)) for i in range(2)]
    q3m = [ph3.enter_context(SB(f"q3m{i}", [128, NT], F32)) for i in range(2)]
    te3 = [ph3.enter_context(SB(f"te3_{i}", [128, NT], F32)) for i in range(2)]
    gm2row = ph3.enter_context(SB("gm2row", [1, 4], F32))
    gm2b = ph3.enter_context(SB("gm2b", [1, 4], BF16))

    ps3 = ExitStack()
    p3_a = ps3.enter_context(PS("p3_a", [128, NT], F32))
    p3_b = ps3.enter_context(PS("p3_b", [128, NT], F32))
    p3_c = ps3.enter_context(PS("p3_c", [128, NT], F32))
    p3_d = ps3.enter_context(PS("p3_d", [128, NT], F32))
    ps3pre = ExitStack()
    p3_g = ps3pre.enter_context(PS("p3_g", [128, 11 * 128], BF16))
    p3_h = ps3pre.enter_context(PS("p3_h", [128, 128], F32))

    g_w3 = [None, None]
    g_p3g_v = None
    p.strict_v = True
    for ci in range(KD):
        sb_ = ci % 2
        eng3, dsem3 = [("sync", "ld"), ("scalar", "sd")][ci % 2]
        tld = E(eng3, lambda e, ci=ci, sb_=sb_: e.dma_start(
            out=w3st[sb_][:, :], in_=wd_in[ci, :, :]), waits=[g_w3[sb_]],
            inc=dsem3, n=16)
        E("vector", lambda e, sb_=sb_: e.tensor_reduce(
            out=w3row[:, sb_:sb_ + 1], in_=w3st[sb_][:, :], axis=AX, op=AL.max,
            apply_absolute_value=True), waits=[tld, g_w3[sb_]], inc="v")
        E("vector", lambda e, ci=ci, sb_=sb_: e.tensor_scalar(
            out=sdc[:, ci:ci + 1], in0=w3row[:, sb_:sb_ + 1],
            scalar1=1.0 / 127.0, scalar2=None,
            op0=AL.mult, op1=AL.bypass), inc="v")
        E("vector", lambda e, ci=ci, sb_=sb_: e.reciprocal(
            w3row[:, sb_:sb_ + 1], sdc[:, ci:ci + 1]), inc="v")
        E("scalar", lambda e, sb_=sb_: e.activation(
            out=w3st[sb_][:, :], in_=w3st[sb_][:, :], func=AF.Identity,
            scale=w3row[:, sb_:sb_ + 1], bias=MAGIC),
            waits=[p.tok("v")], inc="s")
        tq = E("scalar", lambda e, sb_=sb_: e.activation(
            out=w3sq[sb_][:, :], in_=w3st[sb_][:, :], func=AF.Identity,
            bias=-MAGIC), inc="s")
        g_w3[sb_] = tq
        for half in range(2):
            def tpw3(e, sb_=sb_, half=half):
                last = None
                for j in range(11):
                    kc = half * 11 + j
                    last = e.transpose(p3_g[:, j * 128:(j + 1) * 128],
                                       w3sq[sb_][:, kc * 128:(kc + 1) * 128],
                                       id_bf[:, :])
                return last
            ttw = E("tensor", tpw3, waits=[tq, g_p3g_v], inc="t")
            g_p3g_v = E("vector", lambda e, ci=ci, half=half: e.tensor_copy(
                out=wdT[:, half * 11:(half + 1) * 11, ci * 128:(ci + 1) * 128],
                in_=p3_g[:, :]), waits=[ttw], inc="v")
    t_wd_ready = p.tok("v")
    p.strict_v = False

    p.strict_v = True
    tg = E("gpsimd", lambda e: e.dma_start(
        out=gm2row[:, 0:1], in_=cc2_o.rearrange("(a x) -> a x", a=1)),
        waits=[p.tok("cc")], inc="gd", n=16)
    ts_ = E("scalar", lambda e: e.activation(
        out=gm2row[:, 1:2], in_=clipd_sb[:, :], func=AF.Sigmoid),
        waits=[tg], inc="s")
    E("vector", lambda e: e.tensor_mul(
        gm2row[:, 0:1], gm2row[:, 0:1], gm2row[:, 1:2]), waits=[tg, ts_], inc="v")
    E("vector", lambda e: e.tensor_scalar(
        out=gm2row[:, 0:1], in0=gm2row[:, 0:1], scalar1=1.0 / 127.0,
        scalar2=None, op0=AL.mult, op1=AL.bypass), inc="v")
    E("vector", lambda e: e.reciprocal(gm2row[:, 1:2], gm2row[:, 0:1]), inc="v")
    E("vector", lambda e: e.tensor_copy(out=gm2b[:, 0:1], in_=gm2row[:, 0:1]),
      inc="v")
    E("vector", lambda e: e.tensor_sub(gm2b[:, 1:2], gm2row[:, 0:1],
                                       gm2b[:, 0:1]), inc="v")
    E("vector", lambda e: e.tensor_copy(out=gm2b[:, 2:3], in_=gm2row[:, 1:2]),
      inc="v")
    tv = E("vector", lambda e: e.tensor_sub(
        gm2b[:, 3:4], gm2row[:, 1:2], gm2b[:, 2:3]), inc="v")

    def bsc2(e):
        e.matmul(p3_h[:, 0:1], ones_row[:, :], gm2b[:, 0:1], start=True, stop=False)
        e.matmul(p3_h[:, 0:1], ones_row[:, :], gm2b[:, 1:2], start=False, stop=True)
        e.matmul(p3_h[:, 1:2], ones_row[:, :], gm2b[:, 2:3], start=True, stop=False)
        return e.matmul(p3_h[:, 1:2], ones_row[:, :], gm2b[:, 3:4], start=False,
                        stop=True)
    tb = E("tensor", bsc2, waits=[tv], inc="t")
    E("vector", lambda e: e.tensor_copy(out=s2col[:, :], in_=p3_h[:, 0:1]),
      waits=[tb], inc="v")
    tv = E("vector", lambda e: e.tensor_copy(out=is2col[:, :], in_=p3_h[:, 1:2]),
           inc="v")
    p.strict_v = False
    t_is2 = p.tok("v")
    t_s2 = E("vector", lambda e: e.tensor_scalar(
        out=sd2[:, :], in0=sdc[:, :], scalar1=s2col[:, :], scalar2=None,
        op0=AL.mult, op1=AL.bypass), waits=[t_is2, t_wd_ready], inc="v")

    NSL = 11
    M3Q = [("sync", "ld"), ("scalar", "sd"), ("gpsimd", "gd")]
    slot_load = {}
    slot_free = {}
    g_xq3_t = [None, None]
    g_xm3 = [None, None]
    g_ab3 = [None, None, None, None]
    g_te3 = [None, None]
    g_q3m = [None, None]
    t_q3_tile = [None] * T
    ldx_tile = [None] * T

    def emit_mid_load(k):
        t, pr = divmod(k, NSL)
        if t >= T:
            return
        mb = k % 4
        eng3, dsem3 = ("gpsimd", "gd") if k < NSL else M3Q[k % 3]
        slot_load[k] = E(eng3, lambda e, t=t, pr=pr, mb=mb: e.dma_start(
            out=mid3[mb][:, :, :], in_=mid_hbm[t, :, pr * 2:pr * 2 + 2, :]),
            waits=[slot_free.get(k - 4)], inc=dsem3, n=16)

    def ph3_xmload(t):
        xb = t % 2
        eng3, dsem3 = [("sync", "ld"), ("scalar", "sd")][t % 2]
        ldx_tile[t] = E(eng3, lambda e, t=t, xb=xb: e.dma_start(
            out=xm3[xb][:, :, :], in_=xm_hbm[t, :, :, :]), waits=[g_xm3[xb]],
            inc=dsem3, n=16)

    def ph3_qchunk(t, pr):
        k = t * NSL + pr
        xb = t % 2
        mb = k % 4
        tld = slot_load[k]
        tsm = None
        for j in range(2):
            kc = pr * 2 + j
            qp = kc % 2
            tsm = E("scalar", lambda e, mb=mb, j=j, qp=qp: e.activation(
                out=q3m[qp][:, :], in_=mid3[mb][:, j, :], func=AF.Identity,
                scale=is2col[:, :], bias=MAGIC),
                waits=[tld, t_is2, g_q3m[qp]], inc="s")
            E("vector", lambda e, qp=qp: e.tensor_scalar(
                out=q3m[qp][:, :], in0=q3m[qp][:, :], scalar1=MAGIC,
                scalar2=-128.0, op0=AL.subtract, op1=AL.max),
                waits=[tsm], inc="v")
            tvq = E("vector", lambda e, kc=kc, xb=xb, qp=qp: e.tensor_scalar(
                out=xq3[xb][:, kc, :], in0=q3m[qp][:, :], scalar1=127.0,
                scalar2=None, op0=AL.min, op1=AL.bypass),
                waits=[g_xq3_t[xb]], inc="v")
            g_q3m[qp] = tvq
            g_xq3_t[xb] = None
        slot_free[k] = tsm
        if pr == NSL - 1:
            t_q3_tile[t] = p.tok("v")
        emit_mid_load(k + 4)

    def ph3_group(t, mc):
        b = t // TPB
        xb = t % 2
        bank = [p3_a, p3_b, p3_c, p3_d][mc % 4]

        def mmd(e, mc=mc, xb=xb, bank=bank):
            last = None
            for kc in range(KI):
                last = e.matmul(bank[:, :], wdT[:, kc, mc * 128:(mc + 1) * 128],
                                xq3[xb][:, kc, :], start=(kc == 0),
                                stop=(kc == KI - 1))
            return last
        tm = E("tensor", mmd,
               waits=[t_q3_tile[t], t_wd_ready, g_ab3[mc % 4]], inc="t")
        tsc = E("scalar", lambda e, mc=mc, bank=bank: e.activation(
            out=te3[mc % 2][:, :], in_=bank[:, :], func=AF.Identity,
            scale=sd2[:, mc:mc + 1]), waits=[tm, t_s2, g_te3[mc % 2]], inc="s")
        g_ab3[mc % 4] = tsc
        tvy = E("vector", lambda e, mc=mc, xb=xb, b=b: e.scalar_tensor_tensor(
            out=xm3[xb][:, mc, :], in0=te3[mc % 2][:, :],
            scalar=modc(MT_GB, mc, b), in1=xm3[xb][:, mc, :],
            op0=AL.mult, op1=AL.add), waits=[tsc, ldx_tile[t]], inc="v")
        g_te3[mc % 2] = tvy

    for k in range(4):
        emit_mid_load(k)
    ph3_xmload(0)
    for pr in range(NSL):
        ph3_qchunk(0, pr)
    for t in range(T):
        if t + 1 < T:
            ph3_xmload(t + 1)
        nxt_pr = 0
        for mc in range(KD):
            ph3_group(t, mc)
            if t + 1 < T:
                tgt = min(NSL, 2 * (mc + 1))
                while nxt_pr < tgt:
                    ph3_qchunk(t + 1, nxt_pr)
                    nxt_pr += 1
        t_y = p.tok("v")
        xb = t % 2
        g_xm3[xb] = E("gpsimd", lambda e, t=t, xb=xb: e.dma_start(
            out=y_out[t, :, :, :], in_=xm3[xb][:, :, :]), waits=[t_y],
            inc="gd", n=16)
        g_xq3_t[xb] = p.tok("t")

    E("gpsimd", None, waits=[p.tok("gd")])
    p.barrier(owner, skip=())

    ps3pre.close()
    ps3.close()
    ph3.close()

    p.finalize()

    def replay(eng, name):
        for ws, fn, inc, n in p.steps[name]:
            for s, v in ws:
                eng.wait_ge(sem[s], v)
            inst = fn(eng) if fn is not None else None
            if inc is not None:
                if inst is None:
                    inst = eng.nop()
                inst.then_inc(sem[inc], n)

    with nc.Block() as block:
        block.sync(lambda e: replay(e, "sync"))
        block.gpsimd(lambda e: replay(e, "gpsimd"))
        block.scalar(lambda e: replay(e, "scalar"))
        block.vector(lambda e: replay(e, "vector"))
        block.tensor(lambda e: replay(e, "tensor"))

    root.close()
    return nc


_NC_CACHE = {}


def _get_nc():
    if "nc" not in _NC_CACHE:
        _NC_CACHE["nc"] = build_nc()
    return _NC_CACHE["nc"]


def kernel(L0_attn_hidden_states, L0_attn_Wq, L0_attn_bq, L0_attn_Wqa,
           L0_attn_bqa, L0_attn_Wk, L0_attn_bk, L0_attn_Wka, L0_attn_bka,
           L0_attn_Wt, L0_attn_bt, L0_ffn_clip_x, L0_ffn_clip_down,
           L0_ffn_w_gate, L0_ffn_w_up, L0_ffn_w_down, L0_c, L0_wmod):
    f32 = np.float32

    def wmat(w):
        M = np.asarray(w).shape[1]
        return np.ascontiguousarray(
            np.asarray(w, f32).reshape(KD, 128, M).transpose(1, 0, 2))

    def bvec(v):
        return np.ascontiguousarray(np.asarray(v, f32).reshape(KD, 128).T)

    x = np.asarray(L0_attn_hidden_states, f32)
    base = dict(
        wq=wmat(L0_attn_Wq), wk=wmat(L0_attn_Wk), wt=wmat(L0_attn_Wt),
        wqa=wmat(L0_attn_Wqa), wka=wmat(L0_attn_Wka),
        bq=bvec(L0_attn_bq), bk=bvec(L0_attn_bk), bt=bvec(L0_attn_bt),
        bqa=np.ascontiguousarray(np.asarray(L0_attn_bqa, f32).reshape(H, 1)),
        bka=np.ascontiguousarray(np.asarray(L0_attn_bka, f32).reshape(H, 1)),
        wg=np.ascontiguousarray(np.asarray(L0_ffn_w_gate, f32).reshape(KI, 128, D)),
        wu=np.ascontiguousarray(np.asarray(L0_ffn_w_up, f32).reshape(KI, 128, D)),
        wd=np.ascontiguousarray(np.asarray(L0_ffn_w_down, f32).reshape(KD, 128, I)),
        wmod=wmat(L0_wmod),
        clipx=np.ascontiguousarray(np.asarray(L0_ffn_clip_x, f32).reshape(1, 1)),
        clipd=np.ascontiguousarray(np.asarray(L0_ffn_clip_down, f32).reshape(1, 1)),
    )
    c_all = np.asarray(L0_c, f32)

    in_maps = []
    for core in range(8):
        xs = x[core * B:(core + 1) * B]
        xh = xs.reshape(B, TPB, NT, KD, 128).transpose(0, 1, 4, 3, 2)
        xh = np.ascontiguousarray(xh).reshape(T, 128, KD * NT)
        ch = c_all[core * B:(core + 1) * B]
        ch = np.ascontiguousarray(ch.reshape(B, KD, 128).transpose(2, 1, 0))
        m = dict(base)
        m["x"] = xh
        m["cmod"] = ch
        in_maps.append(m)

    nc = _get_nc()
    res = run_bass_kernel_spmd(nc, in_maps, core_ids=list(range(8)))

    outs = []
    for core in range(8):
        y = np.asarray(res.results[core]["y"])
        y = y.reshape(B, TPB, 128, KD, NT).transpose(0, 1, 4, 3, 2)
        outs.append(y.reshape(B, 2048, D))
    return np.ascontiguousarray(np.concatenate(outs, axis=0))
